# version 02
# baseline (speedup 1.0000x reference)
import contextlib
import numpy as np
import ml_dtypes
import concourse.bass as bass
import concourse.mybir as mybir
from concourse.expressions import RuntimeValue

F32 = mybir.dt.float32
F16 = mybir.dt.float16
BF16 = mybir.dt.bfloat16
I32 = mybir.dt.int32
AF = mybir.ActivationFunctionType
ALU = mybir.AluOpType
AX = mybir.AxisListType

D = 1024
NH = 16
HD = 64
TOWN = 2048
NC16 = TOWN // 128
DFF0 = 2816
DFF1 = 4096
NROUND = 9
SCHED_COLS = 64
ACC_COLS = 2 * NH * 512
ACC_STRIDE = 128 * ACC_COLS
MAGIC = 6291456.0
IEXP = int(np.uint32(0xFFC00000).astype(np.int32))

ENGS = ("sp", "act", "dve", "pool", "pe")
ENG_ATTR = {"sp": "sync", "act": "scalar", "dve": "vector",
            "pool": "gpsimd", "pe": "tensor"}


class Step:
    __slots__ = ("engine", "fn", "reads", "writes", "milestone", "dma_stream",
                 "idx", "waits", "tag")

    def __init__(self, engine, fn, reads, writes, milestone, dma_stream, tag):
        self.engine = engine
        self.fn = fn
        self.reads = reads
        self.writes = writes
        self.milestone = milestone
        self.dma_stream = dma_stream
        self.tag = tag
        self.idx = None
        self.waits = {}


class Sched:
    def __init__(self, nc):
        self.nc = nc
        self.steps = []
        self.count = {e: 0 for e in ENGS}
        self.dcount = {}
        self.buf = {}
        self._pe_pending = []

    def _wait(self, step, chain, val):
        if val is None or val <= 0:
            return
        if step.waits.get(chain, 0) < val:
            step.waits[chain] = val

    def op(self, engine, fn, reads=(), writes=(), milestone=True, tag=""):
        return self._add(engine, fn, reads, writes, milestone, None, tag)

    def dma(self, engine, fn, reads=(), writes=(), stream=None, tag=""):
        assert engine in ("sp", "act", "pool")
        if stream is None:
            stream = list(writes)[0]
        return self._add(engine, fn, reads, writes, True, stream, tag)

    def collective(self, fn, reads=(), writes=(), tag=""):
        return self._add("pool", fn, reads, writes, True, ("cc", tag), tag)

    def alias(self, new_id, old_ids):
        readers = []
        for o in old_ids:
            st = self.buf.get(o)
            if st:
                if st["writer"] and st["writer"][1] is not None:
                    readers.append(st["writer"])
                readers.extend([x for x in st["readers"] if x[1] is not None])
        cur = self.buf.setdefault(new_id, {"writer": None, "readers": []})
        cur["readers"] = cur["readers"] + readers

    def _add(self, engine, fn, reads, writes, milestone, dma_stream, tag):
        step = Step(engine, fn, list(reads), list(writes), milestone,
                    dma_stream, tag)
        self.steps.append(step)
        is_dma = dma_stream is not None
        chain = ("d", dma_stream) if is_dma else engine

        for b in step.reads:
            st = self.buf.get(b)
            if st and st["writer"]:
                wch, wval = st["writer"]
                if wval is None:
                    if not (engine == "pe" and wch == "pe"):
                        raise RuntimeError(f"read {b} over pending PE ({tag})")
                else:
                    self._wait(step, wch, wval)
        for b in step.writes:
            st = self.buf.get(b)
            if st:
                if st["writer"]:
                    wch, wval = st["writer"]
                    if wval is None:
                        if not (engine == "pe" and wch == "pe"):
                            raise RuntimeError(f"write {b} over pending PE ({tag})")
                    else:
                        self._wait(step, wch, wval)
                for (rch, rval) in st["readers"]:
                    if rval is None:
                        if not (engine == "pe" and rch == "pe"):
                            raise RuntimeError(f"write {b} raced by PE read ({tag})")
                    else:
                        self._wait(step, rch, rval)

        if is_dma:
            val = self.dcount.get(dma_stream, 0) + (1 if dma_stream[0] == "cc" else 16)
            self.dcount[dma_stream] = val
            step.idx = val
            for b in step.writes:
                self.buf[b] = {"writer": (chain, val), "readers": []}
            for b in step.reads:
                st = self.buf.setdefault(b, {"writer": None, "readers": []})
                st["readers"].append((chain, val))
        elif engine == "pe" and not milestone:
            for b in step.writes:
                self.buf[b] = {"writer": ("pe", None), "readers": []}
                self._pe_pending.append(b)
            for b in step.reads:
                st = self.buf.setdefault(b, {"writer": None, "readers": []})
                st["readers"].append(("pe", None))
                self._pe_pending.append(b)
        else:
            self.count[engine] += 1
            val = self.count[engine]
            step.idx = val
            if engine == "pe":
                for b in self._pe_pending:
                    st = self.buf.get(b)
                    if st is None:
                        continue
                    if st["writer"] and st["writer"][0] == "pe" and st["writer"][1] is None:
                        st["writer"] = ("pe", val)
                    st["readers"] = [
                        (c, val if (v is None and c == "pe") else v)
                        for (c, v) in st["readers"]]
                self._pe_pending = []
            for b in step.writes:
                self.buf[b] = {"writer": (chain, val), "readers": []}
            for b in step.reads:
                st = self.buf.setdefault(b, {"writer": None, "readers": []})
                st["readers"].append((chain, val))
        return step

    def barrier_step(self, engine="sp", tag="final"):
        step = Step(engine, lambda eng: eng.nop(), [], [], True, None, tag)
        for e in ENGS:
            if self.count[e] and e != engine:
                step.waits[e] = self.count[e]
        for s, v in self.dcount.items():
            step.waits[("d", s)] = v
        self.steps.append(step)
        self.count[engine] += 1
        step.idx = self.count[engine]
        return step

    def emit(self, stack):
        nc = self.nc
        sems = {}
        for e in ENGS:
            if self.count[e]:
                sems[e] = stack.enter_context(nc.semaphore(f"ch_{e}"))
        for i, s in enumerate(sorted(self.dcount, key=str)):
            sems[("d", s)] = stack.enter_context(nc.semaphore(f"ds{i}"))
        self.sems = sems

        def run_engine(ename, eng):
            waited = {}
            for step in self.steps:
                if step.engine != ename:
                    continue
                for chain in sorted(step.waits, key=str):
                    val = step.waits[chain]
                    if waited.get(chain, 0) >= val:
                        continue
                    eng.wait_ge(sems[chain], val)
                    waited[chain] = val
                inst = step.fn(eng)
                if step.dma_stream is not None:
                    if step.dma_stream[0] == "cc":
                        inst.then_inc(sems[("d", step.dma_stream)])
                    else:
                        inst.then_inc(sems[("d", step.dma_stream)], 16)
                elif step.milestone:
                    inst.then_inc(sems[ename], 1)

        block = stack.enter_context(nc.Block())
        for ename, attr in ENG_ATTR.items():
            if not any(s.engine == ename for s in self.steps):
                continue

            def mk(ename):
                def body(eng):
                    run_engine(ename, eng)
                body.__name__ = f"prog_{ename}"
                return body
            getattr(block, attr)(mk(ename))


class Ctx:
    pass


def sb(ctx, name, shape, dtype):
    t = ctx.stack.enter_context(ctx.nc.sbuf_tensor("sb_" + name, shape, dtype))
    ctx.tiles[name] = t
    return t


def rearr_tok(dram_ap):
    return dram_ap.rearrange("(c p) n -> p c n", p=128)


def rearr_fm(dram_ap):
    return dram_ap.rearrange("(c p) n -> p c n", p=128)


def reg_off(eng, ctx, col, maxv, name):
    if not hasattr(ctx, "regcache"):
        ctx.regcache = {}
    key = eng.engine
    reg = ctx.regcache.get(key)
    if reg is None:
        reg = eng.alloc_register(f"roff_{key}")
        ctx.regcache[key] = reg
    eng.reg_load(reg, ctx.tiles["sched"][0:1, col:col + 1])
    return RuntimeValue(reg, min_val=0, max_val=maxv)


def emit_fp4_quant(ctx, eng, out_ap, x_ap, n, rid, wid, tag):
    S = ctx.S
    nb = n // 16
    T = dict(ctx.tiles)
    y = T["qy"][:, 0:n]
    t2 = T["qt2"][:, 0:n]
    g = T["qg"][:, 0:n]
    bm = T["qbm"][:, 0:nb]
    s = T["qs"][:, 0:nb]
    si = T["qsi"][:, 0:nb]
    ic = T["iconst"]
    x3 = x_ap.rearrange("p (b k) -> p b k", k=16)
    y3 = y.rearrange("p (b k) -> p b k", k=16)
    s_b = s.rearrange("p b -> p b ()").broadcast_to([128, nb, 16])
    si_b = si.rearrange("p b -> p b ()").broadcast_to([128, nb, 16])

    S.op("dve", lambda e: e.tensor_reduce(bm, x3, AX.X, ALU.max,
                                          apply_absolute_value=True),
         reads=[rid], writes=["qbm"], tag=tag + "bm")
    S.op(eng, lambda e: e.tensor_scalar(s, bm, 1.0 / 6.0, 1e-12, ALU.mult, ALU.max),
         reads=["qbm"], writes=["qsc"], tag=tag + "s")
    S.op("dve", lambda e: e.reciprocal(si, s), reads=["qsc"], writes=["qsi"],
         tag=tag + "ri")
    S.op(eng, lambda e: e.tensor_tensor(y3, x3, si_b, ALU.mult),
         reads=[rid, "qsi"], writes=["qy"], tag=tag + "y")
    ql = T["ql"][:, 0:n]
    S.op(eng, lambda e: e.tensor_scalar(ql, y, MAGIC, -MAGIC, ALU.add, ALU.add),
         reads=["qy"], writes=["qlo"], tag=tag + "lo")
    rnd_b = ic[:, 2:3].rearrange("p x -> p x").broadcast_to([128, n]).bitcast(I32)
    exp_b = ic[:, 0:1].rearrange("p x -> p x").broadcast_to([128, n]).bitcast(I32)
    S.op(eng, lambda e: e.tensor_tensor(t2.bitcast(I32), y.bitcast(I32),
                                        rnd_b, ALU.add),
         reads=["qy", "iconst"], writes=["qhi"], tag=tag + "rn")
    S.op(eng, lambda e: e.tensor_tensor(t2.bitcast(I32), t2.bitcast(I32),
                                        exp_b, ALU.bitwise_and),
         reads=["qhi", "iconst"], writes=["qhi"], tag=tag + "rm")
    abs_b = ic[:, 3:4].rearrange("p x -> p x").broadcast_to([128, n]).bitcast(I32)
    S.op(eng, lambda e: e.tensor_tensor(g.bitcast(I32), y.bitcast(I32),
                                        abs_b, ALU.bitwise_and),
         reads=["qy", "iconst"], writes=["qg3"], tag=tag + "ga")
    S.op(eng, lambda e: e.tensor_single_scalar(g, g, 1.0, ALU.is_gt),
         reads=["qg3"], writes=["qg3"], tag=tag + "g")
    if eng == "dve":
        S.op(eng, lambda e: e.copy_predicated(ql.bitcast(I32), g.bitcast(I32),
                                              t2.bitcast(I32)),
             reads=["qg3", "qhi", "qlo"], writes=["qlo"], tag=tag + "sel")
    else:
        S.op(eng, lambda e: e.tensor_tensor(t2, t2, ql, ALU.subtract),
             reads=["qhi", "qlo"], writes=["qhi"], tag=tag + "sd")
        S.op(eng, lambda e: e.tensor_tensor(t2, t2, g, ALU.mult),
             reads=["qhi", "qg3"], writes=["qhi"], tag=tag + "sm")
        S.op(eng, lambda e: e.tensor_tensor(ql, ql, t2, ALU.add),
             reads=["qhi", "qlo"], writes=["qlo"], tag=tag + "sa")
    q3 = ql.rearrange("p (b k) -> p b k", k=16)
    o3 = out_ap.rearrange("p (b k) -> p b k", k=16)
    S.op(eng, lambda e: e.tensor_tensor(o3, q3, s_b, ALU.mult),
         reads=["qlo", "qsc"], writes=[wid], tag=tag + "out")


def transpose_tile(ctx, src_ap, dst_ap, rids, wid, tag):
    S = ctx.S
    i = ctx.tp_rot[0] % 2
    ctx.tp_rot[0] += 1
    pt = ctx.ps_t[i]
    S.op("pe", lambda e: e.transpose(pt, src_ap, ctx.tiles["ident"][:]),
         reads=list(rids) + ["ident"], writes=[("ps2", i)], tag=tag + "t")
    S.op("dve", lambda e: e.tensor_copy(dst_ap, pt),
         reads=[("ps2", i)], writes=[wid], tag=tag + "e")


def p_consts(ctx):
    S = ctx.S
    T = dict(ctx.tiles)
    dr = ctx.dr
    S.dma("sp", lambda e: e.dma_start(out=T["ident"][:], in_=dr["ident"].ap()),
          writes=["ident"], tag="ident")
    S.op("dve", lambda e: e.memset(T["iconst"][:, 0:1], IEXP), writes=["iconst"])
    S.op("dve", lambda e: e.memset(T["iconst"][:, 1:2],
                                   int(np.uint32(0x80000000).astype(np.int32))),
         writes=["iconst"])
    S.op("dve", lambda e: e.memset(T["iconst"][:, 2:3], 0x00200000),
         writes=["iconst"])
    S.op("dve", lambda e: e.memset(T["iconst"][:, 3:4], 0x7FFFFFFF),
         writes=["iconst"])
    S.op("dve", lambda e: e.memset(T["onesr"][0:1, :], 1.0), writes=["onesr"])
    if "masks" in dr:
        S.dma("sp", lambda e: e.dma_start(out=T["masks"][:], in_=dr["masks"].ap()
                                          .rearrange("p (a b) -> p a b", a=4)),
              writes=["masks"], tag="masks")
    if "sched" in dr:
        S.dma("sp", lambda e: e.dma_start(out=T["sched"][:], in_=dr["sched"].ap()),
              writes=["sched"], tag="sched")
    if "wtab" in dr:
        S.dma("sp", lambda e: e.dma_start(out=T["wtab"][:], in_=dr["wtab"].ap()),
              writes=["wtab"], tag="wtab")
    if "ropec" in dr and not getattr(ctx, "defer_rope", False):
        S.dma("sp", lambda e: e.dma_start(out=T["ropec"][:],
                                          in_=rearr_tok(dr["ropec"].ap())),
              writes=["ropec"], tag="ropec")
        S.dma("sp", lambda e: e.dma_start(out=T["ropes"][:],
                                          in_=rearr_tok(dr["ropes"].ap())),
              writes=["ropes"], tag="ropes")


def p_load_x(ctx, src_name, extra_reads=()):
    ctx.S.dma("sp", lambda e: e.dma_start(
        out=ctx.tiles["x"][:], in_=rearr_tok(ctx.dr[src_name].ap())),
        reads=list(extra_reads),
        writes=[("x", c) for c in range(NC16)], stream="xin", tag="xload")


def p_store_x(ctx, dst_name):
    ctx.S.dma("act", lambda e: e.dma_start(
        out=rearr_tok(ctx.dr[dst_name].ap()), in_=ctx.tiles["x"][:]),
        reads=[("x", c) for c in range(NC16)], writes=["xout"], tag="xstore")


def p_ln(ctx, layer):
    S = ctx.S
    T = dict(ctx.tiles)
    st = T["lnst"]
    S.op("dve", lambda e: e.memset(T["lneps"][:], 1e-5), writes=["lneps"],
         tag="lneps")
    for c in range(NC16):
        xc = T["x"][:, c, :]
        xid = ("x", c)
        S.op("dve", lambda e, xc=xc: e.tensor_reduce(st[:, 0:1], xc, AX.X, ALU.add),
             reads=[xid], writes=["ln_s1"], tag=f"ln{c}s1")
        S.op("dve", lambda e, xc=xc: e.tensor_tensor(T["lnsq"][:], xc, xc, ALU.mult),
             reads=[xid], writes=["lnsq"], tag=f"ln{c}sq")
        S.op("dve", lambda e: e.tensor_reduce(st[:, 1:2], T["lnsq"][:], AX.X,
                                              ALU.add),
             reads=["lnsq"], writes=["ln_s2"], tag=f"ln{c}s2")
        S.op("dve", lambda e: e.tensor_single_scalar(st[:, 2:3], st[:, 0:1],
                                                     1.0 / D, ALU.mult),
             reads=["ln_s1"], writes=["ln_m"], tag=f"ln{c}m")
        S.op("dve", lambda e: e.tensor_tensor(st[:, 3:4], st[:, 2:3], st[:, 2:3],
                                              ALU.mult),
             reads=["ln_m"], writes=["ln_m2"], tag=f"ln{c}m2")
        S.op("dve", lambda e: e.scalar_tensor_tensor(st[:, 4:5], st[:, 1:2],
                                                     1.0 / D, st[:, 3:4],
                                                     ALU.mult, ALU.subtract),
             reads=["ln_s2", "ln_m2"], writes=["ln_v"], tag=f"ln{c}v")
        S.op("act", lambda e: e.activation(st[:, 5:6], st[:, 4:5], AF.Sqrt,
                                           bias=T["lneps"][:], scale=1.0),
             reads=["ln_v", "lneps"], writes=["ln_sd"], tag=f"ln{c}sd")
        S.op("dve", lambda e: e.reciprocal(st[:, 6:7], st[:, 5:6]),
             reads=["ln_sd"], writes=["ln_r"], tag=f"ln{c}r")
        S.op("dve", lambda e, xc=xc: e.tensor_scalar(T["xn32"][:], xc, st[:, 2:3],
                                                     st[:, 6:7], ALU.subtract,
                                                     ALU.mult),
             reads=[xid, "ln_m", "ln_r"], writes=["xn32"], tag=f"ln{c}n")
        S.op("dve", lambda e: e.tensor_copy(T["xnbf"][:], T["xn32"][:]),
             reads=["xn32"], writes=["xnbf"], tag=f"ln{c}b")
        for m in range(8):
            transpose_tile(ctx, T["xnbf"][:, 128 * m:128 * (m + 1)],
                           T["xn_fm"][:, m, 128 * c:128 * (c + 1)],
                           ["xnbf"], ("xn_fm", m), f"ln{c}t{m}")
        if layer == 0:
            emit_fp4_quant(ctx, "dve", T["xnqbf"][:], T["xn32"][:], D,
                           "xn32", "xnqbf", f"q{c}")
            for m in range(8):
                transpose_tile(ctx, T["xnqbf"][:, 128 * m:128 * (m + 1)],
                               T["xnq_fm"][:, m, 128 * c:128 * (c + 1)],
                               ["xnqbf"], ("xnq_fm", m), f"lnq{c}t{m}")


def p_l0_qkv(ctx):
    S = ctx.S
    T = dict(ctx.tiles)
    wq = ctx.dr["L0_attn_w_qkv"].ap()
    S.op("dve", lambda e: e.memset(T["vstg"][:], 1.0), writes=["vstg"], tag="vones")
    for m in range(16):
        par = m % 2
        wst = ("wst", par)
        S.dma("pool", lambda e, m=m, par=par: e.dma_start(
            out=T["wstage"][:, par, :], in_=wq[128 * m:128 * (m + 1), :]),
            writes=[wst], stream=("wsts", par), tag=f"qkvw{m}")
        for k in range(8):
            transpose_tile(ctx, T["wstage"][:, par, 128 * k:128 * (k + 1)],
                           T["wT"][:, par, k, :], [wst], ("wT", par, k),
                           f"qkvt{m}_{k}")
        for t in range(4):
            pid = ("psc", t)
            for k in range(8):
                def mfn(e, m=m, k=k, t=t, par=par):
                    return e.matmul(ctx.psc[t][:],
                                    T["wT"][:, par, k, :],
                                    T["xn_fm"][:, k, 512 * t:512 * (t + 1)],
                                    start=(k == 0), stop=(k == 7))
                S.op("pe", mfn, reads=[("wT", par, k), ("xn_fm", k)],
                     writes=[pid], milestone=(k == 7), tag=f"qkvm{m}_{t}_{k}")
            def efn(e, t=t):
                return e.tensor_copy(T["kstg"][:, 512 * t:512 * (t + 1)],
                                     ctx.psc[t][:])
            S.op("dve", efn, reads=[pid], writes=["kstg"], tag=f"qkve{m}_{t}")
        dst = ctx.qdst if m < 8 else ctx.kdst
        mk = m % 8
        S.dma("act", lambda e, mk=mk, dst=dst: e.dma_start(
            out=rearr_fm(ctx.dr[dst].ap())[:, mk, :], in_=T["kstg"][:]),
            reads=["kstg"], writes=[("kout", dst, mk)], stream="kout",
            tag=f"kst{m}")
    for m in range(8):
        par = m % 2
        wst = ("wst", par)
        S.dma("pool", lambda e, m=m, par=par: e.dma_start(
            out=T["wstage"][:, par, :],
            in_=wq[2048 + 128 * m:2048 + 128 * (m + 1), :]),
            writes=[wst], stream=("wsts", par), tag=f"vw{m}")
        for k in range(8):
            transpose_tile(ctx, T["wstage"][:, par, 128 * k:128 * (k + 1)],
                           T["wvT"][:, k, 128 * m:128 * (m + 1)],
                           [wst], ("wvT", k), f"vt{m}_{k}")
    p_v_common(ctx, lambda t, n, k: (T["xn_fm"][:, k, 128 * t:128 * (t + 1)],
                                     [("xn_fm", k), ("wvT", k)],
                                     T["wvT"][:, k, 512 * n:512 * (n + 1)]),
               nbias=False)


def p_v_common(ctx, operands, nbias):
    S = ctx.S
    T = dict(ctx.tiles)
    for t in range(NC16):
        for n in range(2):
            pid = ("psc", n)
            for k in range(8):
                lhs, rids, rhs = operands(t, n, k)
                def mfn(e, lhs=lhs, rhs=rhs, k=k, n=n):
                    return e.matmul(ctx.psc[n][:], lhs, rhs,
                                    start=(k == 0), stop=(k == 7 and not nbias))
                S.op("pe", mfn, reads=rids, writes=[pid],
                     milestone=(k == 7 and not nbias), tag=f"vm{t}_{n}_{k}")
            if nbias:
                def bfn(e, t=t, n=n):
                    return e.matmul(ctx.psc[n][:], T["onesr"][0:1, 0:128],
                                    T["brow"][0:1, 512 * n:512 * (n + 1)],
                                    start=False, stop=True)
                S.op("pe", bfn, reads=["onesr", "brow"], writes=[pid],
                     milestone=True, tag=f"vb{t}_{n}")
            for par2 in range(2):
                def efn(e, t=t, n=n, par2=par2):
                    dst = T["vstg"][:].rearrange("p (h w) -> p h w", w=128)
                    src = ctx.psc[n][:].rearrange("p (h w) -> p h w", w=64)
                    return e.tensor_copy(
                        dst[:, 8 * n + par2:8 * (n + 1):2,
                            64 * par2:64 * par2 + 64],
                        src[:, par2::2, :])
                S.op("dve", efn, reads=[pid, "vstg"], writes=[("vstg2", n)],
                     tag=f"ve{t}_{n}_{par2}")
        vdst_ap = rearr_tok(ctx.dr[ctx.vdst].ap())
        S.dma("act", lambda e, t=t, vdst_ap=vdst_ap: e.dma_start(
            out=vdst_ap[:, t, :], in_=T["vstg"][:]),
            reads=[("vstg2", 0), ("vstg2", 1), "vstg"], writes=[("vout", t)],
            stream="vout", tag=f"vst{t}")
        ctx.S.alias(("vstg2", 0), [("vout", t)])
        ctx.S.alias(("vstg2", 1), [("vout", t)])


def p_prep_l0ffn_weights(ctx):
    S = ctx.S
    T = dict(ctx.tiles)
    for wname, dst in (("L0_ffn_w_gate", "wgqT"), ("L0_ffn_w_up", "wuqT")):
        w = ctx.dr[wname].ap()
        for r in range(22):
            par = r % 2
            S.dma("sp", lambda e, w=w, r=r, par=par: e.dma_start(
                out=T["wqst"][:, par, 0:D], in_=w[128 * r:128 * (r + 1), :]),
                writes=[("wqst", par)], stream=("wqs", par), tag=f"pw{dst}{r}")
            emit_fp4_quant(ctx, "dve", T["wqbf"][:, par, 0:D],
                           T["wqst"][:, par, 0:D], D,
                           ("wqst", par), ("wqbf", par), f"pw{dst}{r}_")
            for c in range(8):
                transpose_tile(ctx, T["wqbf"][:, par, 128 * c:128 * (c + 1)],
                               T["wqT"][:, par, c, 0:128],
                               [("wqbf", par)], ("wqT", par, c), f"pwt{dst}{r}_{c}")
            S.dma("act", lambda e, dst=dst, r=r, par=par: e.dma_start(
                out=rearr_fm(ctx.dr[dst].ap())[:, :, 128 * r:128 * (r + 1)],
                in_=T["wqT"][:, par, 0:8, 0:128]),
                reads=[("wqT", par, c) for c in range(8)],
                writes=[("wgu", dst, r)], stream=("wqTo", par), tag=f"pws{dst}{r}")
    w = ctx.dr["L0_ffn_w_down"].ap()
    wd_splits = [(0, 768), (768, 640), (1408, 768), (2176, 640)]
    for r in range(8):
        for hf, (f0, fwd) in enumerate(wd_splits):
            par = hf % 2
            nch = fwd // 128
            S.dma("sp", lambda e, w=w, r=r, f0=f0, fwd=fwd, par=par: e.dma_start(
                out=T["wqst"][:, par, 0:fwd],
                in_=w[128 * r:128 * (r + 1), f0:f0 + fwd]),
                writes=[("wqst", par)], stream=("wqs", par), tag=f"pwd{r}_{hf}")
            emit_fp4_quant(ctx, "dve", T["wqbf"][:, par, 0:fwd],
                           T["wqst"][:, par, 0:fwd], fwd,
                           ("wqst", par), ("wqbf", par), f"pwd{r}_{hf}_")
            for c in range(nch):
                transpose_tile(ctx, T["wqbf"][:, par, 128 * c:128 * (c + 1)],
                               T["wqT"][:, par, c, 0:128],
                               [("wqbf", par)], ("wqT", par, c), f"pwdt{r}_{hf}_{c}")
            S.dma("act", lambda e, r=r, f0=f0, nch=nch, par=par: e.dma_start(
                out=ctx.dr["wdqT"].ap().rearrange("(c p) n -> p c n", p=128)
                [:, f0 // 128:f0 // 128 + nch, 128 * r:128 * (r + 1)],
                in_=T["wqT"][:, par, 0:nch, 0:128]),
                reads=[("wqT", par, c) for c in range(nch)],
                writes=[("wd_", r, hf)], stream=("wqTo", par), tag=f"pwds{r}_{hf}")


def p_prep_wout(ctx):
    S = ctx.S
    T = dict(ctx.tiles)
    w = ctx.dr["L0_attn_w_out"].ap()
    for r in range(8):
        par = r % 2
        S.dma("sp", lambda e, w=w, r=r, par=par: e.dma_start(
            out=T["wqst"][:, par, 0:D], in_=w[128 * r:128 * (r + 1), :]),
            writes=[("wqst", par)], stream=("wqs", par), tag=f"pwo{r}")
        S.op("dve", lambda e, par=par: e.tensor_copy(
            T["wqbf"][:, par, 0:D], T["wqst"][:, par, 0:D]),
            reads=[("wqst", par)], writes=[("wqbf", par)], tag=f"pwoc{r}")
        for c in range(8):
            transpose_tile(ctx, T["wqbf"][:, par, 128 * c:128 * (c + 1)],
                           T["wqT"][:, par, c, 0:128],
                           [("wqbf", par)], ("wqT", par, c), f"pwot{r}_{c}")
        S.dma("act", lambda e, r=r, par=par: e.dma_start(
            out=rearr_fm(ctx.dr[ctx.woT_dram].ap())[:, :, 128 * r:128 * (r + 1)],
            in_=T["wqT"][:, par, 0:8, 0:128]),
            reads=[("wqT", par, c) for c in range(8)],
            writes=[("wouT", r)], stream=("wqTo", par), tag=f"pwos{r}")


def p_l0_ffn(ctx):
    S = ctx.S
    T = dict(ctx.tiles)
    fgs = [(0, 512), (512, 512), (1024, 512), (1536, 512), (2048, 512),
           (2560, 256)]
    for gi, (f0, fw) in enumerate(fgs):
        par = gi % 2
        nks = fw // 128
        for wname, buf in (("wgqT", "wg"), ("wuqT", "wu")):
            S.dma("sp", lambda e, wname=wname, buf=buf, f0=f0, fw=fw, par=par:
                  e.dma_start(
                      out=T[buf][:, par, :, 0:fw],
                      in_=rearr_fm(ctx.dr[wname].ap())[:, :, f0:f0 + fw]),
                  reads=[("wgu", wname, rr)
                         for rr in range(f0 // 128, (f0 + fw) // 128)],
                  writes=[(buf, par)], stream=(buf, par), tag=f"ffw{gi}{buf}")
        S.dma("sp", lambda e, f0=f0, nks=nks, par=par: e.dma_start(
            out=T["wdn"][:, par, 0:nks, :],
            in_=ctx.dr["wdqT"].ap().rearrange("(c p) n -> p c n", p=128)
            [:, f0 // 128:f0 // 128 + nks, :]),
            reads=[("wd_", r, hf) for r in range(8) for hf in range(4)],
            writes=[("wdn", par)], stream=("wdn", par), tag=f"ffwd{gi}")
        for t in range(NC16):
            for which, buf, pi in (("g", "wg", 2), ("u", "wu", 3)):
                for k in range(8):
                    def mfn(e, k=k, t=t, buf=buf, pi=pi, fw=fw, par=par):
                        return e.matmul(
                            ctx.psc[pi][:, 0:fw],
                            T["xnq_fm"][:, k, 128 * t:128 * (t + 1)],
                            T[buf][:, par, k, 0:fw],
                            start=(k == 0), stop=(k == 7))
                    S.op("pe", mfn, reads=[("xnq_fm", k), (buf, par)],
                         writes=[("psc", pi)], milestone=(k == 7),
                         tag=f"ff{which}{gi}_{t}_{k}")
            S.op("act", lambda e, fw=fw: e.activation(T["gs"][:, 0:fw],
                                                      ctx.psc[2][:, 0:fw],
                                                      AF.Silu),
                 reads=[("psc", 2)], writes=["gs"], tag=f"ffs{gi}_{t}")
            S.op("dve", lambda e, fw=fw: e.tensor_tensor(
                T["h32"][:, 0:fw], T["gs"][:, 0:fw], ctx.psc[3][:, 0:fw],
                ALU.mult),
                reads=["gs", ("psc", 3)], writes=["h32"], tag=f"ffh{gi}_{t}")
            emit_fp4_quant(ctx, "dve", T["hq"][:, 0:fw], T["h32"][:, 0:fw], fw,
                           "h32", "hq", f"fq{gi}_{t}_")
            for ks in range(nks):
                transpose_tile(ctx, T["hq"][:, 128 * ks:128 * (ks + 1)],
                               T["hq_fm"][:, ks, :], ["hq"], ("hq_fm", ks),
                               f"fqt{gi}_{t}_{ks}")
            for n in range(2):
                pi = n
                for ks in range(nks):
                    def mfn(e, ks=ks, n=n, t=t, par=par, pi=pi, nks=nks):
                        return e.matmul(
                            ctx.psc[pi][:],
                            T["hq_fm"][:, ks, :],
                            T["wdn"][:, par, ks, 512 * n:512 * (n + 1)],
                            start=(ks == 0), stop=(ks == nks - 1))
                    S.op("pe", mfn, reads=[("hq_fm", ks), ("wdn", par)],
                         writes=[("psc", pi)], milestone=(ks == nks - 1),
                         tag=f"ffd{gi}_{t}_{n}_{ks}")
                def afn(e, t=t, n=n, pi=pi):
                    xs = T["x"][:, t, 512 * n:512 * (n + 1)]
                    return e.tensor_tensor(xs, xs, ctx.psc[pi][:], ALU.add)
                S.op("dve", afn, reads=[("psc", pi), ("x", t)],
                     writes=[("x", t)], tag=f"ffx{gi}_{t}_{n}")


def p_attention(ctx, layer):
    S = ctx.S
    T = dict(ctx.tiles)
    kfull = ctx.dr[ctx.ksrc].ap()
    vfull = ctx.dr[ctx.vsrc].ap()
    qpark = ctx.dr[ctx.qsrc].ap()
    soff = ctx.sched_off
    accf = None
    sacf = None
    S.op("dve", lambda e: e.memset(T["ctxacc"][:], 0.0),
         writes=[("acc", hp, qt, st) for hp in range(8) for qt in range(2)
                 for st in range(2)], tag=f"am{layer}")
    S.op("dve", lambda e: e.memset(T["sacc"][:], 0.0),
         writes=[("sac", i, hh) for i in range(16) for hh in range(2)],
         tag=f"am2{layer}")

    for r in range(NROUND):
        par = r % 2
        c0, c1, c2, c3, c4 = (soff + 5 * r + i for i in range(5))

        def kfn(e, c0=c0, par=par):
            off = reg_off(e, ctx, c0, 4096 * 2048 - 1024, f"ko{c0}")
            src = bass.AP(kfull.tensor, off,
                          [[2048, 128], [128 * 2048, 8], [1, 1024]])
            return e.dma_start(out=T["kslab"][:, par, :, :], in_=src)
        S.dma("sp", kfn,
              reads=["sched"] + list(getattr(ctx, "k_reads", ())),
              writes=[("kslab", par)],
              stream=("kslab", par), tag=f"a{layer}k{r}")

        def vfn(e, c1=c1):
            off = reg_off(e, ctx, c1, 8192 * 2048 - 2048, f"vo{c1}")
            src = bass.AP(vfull.tensor, off,
                          [[2048, 128], [128 * 2048, 8], [1, 2048]])
            return e.dma_start(out=T["vslab"][:], in_=src)
        S.dma("sp", vfn,
              reads=["sched"] + list(getattr(ctx, "v_reads", ())),
              writes=["vslab"],
              stream="vslab", tag=f"a{layer}v{r}")

        for qt in range(2):
            cq = c2 if qt == 0 else c3
            def qfn(e, cq=cq, qt=qt):
                off = reg_off(e, ctx, cq, 1024 * 2048 - 512, "qo")
                src = bass.AP(qpark.tensor, off,
                              [[2048, 128], [128 * 2048, 8], [1, 512]])
                return e.dma_start(out=T["qsb"][:, qt, :, :], in_=src)
            S.dma("act", qfn,
                  reads=["sched"] + list(getattr(ctx, "q_reads", ())),
                  writes=[("qsb", qt)],
                  stream=("qsb", qt), tag=f"a{layer}q{r}_{qt}")

        diag = r < 2
        for hp in range(8):
            for qt in range(2):
                kl = list(range(4 * (qt + 1) if diag else 8))
                for kb in kl:
                    sp2 = kb % 2
                    pid = ("ps2", sp2)
                    for hh in range(2):
                        h = 2 * hp + hh
                        def sfn(e, h=h, kb=kb, qt=qt, par=par, sp2=sp2, hh=hh):
                            lhs = T["kslab"][64 * (h % 2):64 * (h % 2) + 64,
                                             par, h // 2,
                                             128 * kb:128 * (kb + 1)]
                            rhs = T["qsb"][64 * (h % 2):64 * (h % 2) + 64,
                                           qt, h // 2, :]
                            return e.matmul(ctx.ps2[sp2][:, hh, :], lhs, rhs,
                                            start=True, stop=True)
                        S.op("pe", sfn, reads=[("kslab", par), ("qsb", qt)],
                             writes=[pid], milestone=True,
                             tag=f"a{layer}s{r}_{hp}_{qt}_{kb}_{hh}")
                    def efn(e, sp2=sp2):
                        return e.activation(
                            T["ptile"][:, sp2, :],
                            ctx.ps2[sp2][:].rearrange("p a b -> p (a b)"),
                            AF.Exp, scale=0.125)
                    S.op("act", efn, reads=[pid],
                         writes=[("pt", sp2)], tag=f"a{layer}e{r}_{hp}_{qt}_{kb}")
                    if diag and kb >= 4 * qt:
                        dm = kb - 4 * qt
                        def mfn(e, sp2=sp2, dm=dm):
                            pt = T["ptile"][:, sp2, :].rearrange(
                                "p (a b) -> p a b", a=2)
                            mk = T["masks"][:, dm, :].rearrange(
                                "p b -> p () b").broadcast_to([128, 2, 512])
                            return e.tensor_tensor(pt, pt, mk, ALU.mult)
                        S.op("pool", mfn, reads=[("pt", sp2), "masks"],
                             writes=[("pt", sp2)],
                             tag=f"a{layer}m{r}_{hp}_{qt}_{kb}")
                    for hh in range(2):
                        h = 2 * hp + hh
                        def pfn(e, h=h, kb=kb, hh=hh, sp2=sp2,
                                first=(kb == kl[0]), last=(kb == kl[-1])):
                            lhs = T["vslab"][:, kb, 128 * h:128 * (h + 1)]
                            rhs = T["ptile"][:, sp2, 512 * hh:512 * (hh + 1)]
                            return e.matmul(ctx.psc[hh][:], lhs, rhs,
                                            start=first, stop=last)
                        S.op("pe", pfn, reads=["vslab", ("pt", sp2)],
                             writes=[("psc", hh)], milestone=(kb == kl[-1]),
                             tag=f"a{layer}p{r}_{hp}_{qt}_{kb}_{hh}")
                accf = T["ctxacc"][:].rearrange("p a b c -> p (a b c)")
                sacf = T["sacc"][:].rearrange("p a b -> p (a b)")
                for hh in range(2):
                    h = 2 * hp + hh
                    for st_t in range(2):
                        cc0 = st_t * 8192 + hp * 1024 + qt * 512
                        pb = 64 * hh
                        def cfn(e, hh=hh, st_t=st_t, r=r, cc0=cc0, pb=pb,
                                accf=accf):
                            dst = accf[pb:pb + 64, cc0:cc0 + 512]
                            return e.scalar_tensor_tensor(
                                dst, ctx.psc[hh][pb:pb + 64, :],
                                T["wtab"][pb:pb + 64,
                                          2 * r + st_t:2 * r + st_t + 1],
                                dst, ALU.mult, ALU.add)
                        S.op("dve", cfn,
                             reads=[("psc", hh), "wtab", ("acc", hp, qt, st_t)],
                             writes=[("acc", hp, qt, st_t)],
                             tag=f"a{layer}c{r}_{hp}_{qt}_{hh}_{st_t}")
                        sidx = st_t * 16 + qt * 8 + hp
                        rng = sidx % 16
                        b2 = sidx // 16
                        sb0 = (64 + 32 * b2) if hh == 0 else (32 * b2)
                        cs0 = rng * 512
                        def sfn2(e, hh=hh, st_t=st_t, r=r, cs0=cs0, sb0=sb0,
                                 sacf=sacf):
                            dst = sacf[sb0:sb0 + 32, cs0:cs0 + 512]
                            return e.scalar_tensor_tensor(
                                dst, ctx.psc[hh][sb0:sb0 + 32, :],
                                T["wtab"][sb0:sb0 + 32,
                                          2 * r + st_t:2 * r + st_t + 1],
                                dst, ALU.mult, ALU.add)
                        S.op("dve", sfn2,
                             reads=[("psc", hh), "wtab",
                                    ("sac", rng, hh)],
                             writes=[("sac", rng, hh)],
                             tag=f"a{layer}z{r}_{hp}_{qt}_{hh}_{st_t}")

    if getattr(ctx, "alias_ctx", False):
        for k in range(8):
            S.alias(("ctx_fm", k), [("kslab", 0), ("kslab", 1)])
            S.alias(("woT", k), ["vslab"])
    accf = T["ctxacc"][:].rearrange("p a b c -> p (a b c)")
    sacf = T["sacc"][:].rearrange("p a b -> p (a b)")
    for stripe in range(2):
        for hp in range(8):
            for qt in range(2):
                for hh in range(2):
                    sidx = stripe * 16 + qt * 8 + hp
                    rng = sidx % 16
                    b2 = sidx // 16
                    sb0 = (64 + 32 * b2) if hh == 0 else (32 * b2)
                    c0 = rng * 512
                    def rfn(e, sb0=sb0, c0=c0, sacf=sacf):
                        sl = sacf[sb0:sb0 + 32, c0:c0 + 512]
                        return e.reciprocal(sl, sl)
                    S.op("dve", rfn, reads=[("sac", rng, hh)],
                         writes=[("sac", rng, hh)],
                         tag=f"nr{layer}_{stripe}_{hp}_{qt}_{hh}")
                    for dd in range(2):
                        def dfn2(e, sb0=sb0, c0=c0, hh=hh, dd=dd, sacf=sacf):
                            sl = sacf[sb0:sb0 + 32, c0:c0 + 512]
                            dst = T["nstg2"][64 * hh + 32 * dd:
                                             64 * hh + 32 * (dd + 1), :]
                            return e.tensor_copy(dst, sl)
                        S.op("dve", dfn2, reads=[("sac", rng, hh)],
                             writes=["nstg2"],
                             tag=f"nd{layer}_{stripe}_{hp}_{qt}_{hh}_{dd}")
                def nfn(e, stripe=stripe, hp=hp, qt=qt, accf=accf):
                    c0 = stripe * 8192 + hp * 1024 + qt * 512
                    src = accf[:, c0:c0 + 512]
                    dst = T["ctx_fm"][:, hp,
                                      1024 * stripe + 512 * qt:
                                      1024 * stripe + 512 * (qt + 1)]
                    return e.tensor_tensor(dst, src, T["nstg2"][:], ALU.mult)
                S.op("dve", nfn, reads=[("acc", hp, qt, stripe), "nstg2"],
                     writes=[("ctx_fm", hp)],
                     tag=f"nm{layer}_{stripe}_{hp}_{qt}")


def p_outproj(ctx, layer):
    S = ctx.S
    T = dict(ctx.tiles)
    if layer == 0 and getattr(ctx, "woT_dram", None):
        woTd = ctx.dr[ctx.woT_dram].ap()
        S.dma("pool", lambda e, woTd=woTd: e.dma_start(
            out=T["woT"][:], in_=rearr_fm(woTd)),
            reads=list(getattr(ctx, "woT_reads", ())),
            writes=[("woT", k) for k in range(8)], stream="wo0", tag="owL0d")
    elif layer == 0:
        w = ctx.dr["L0_attn_w_out"].ap()
        for m in range(8):
            par = m % 2
            S.dma("pool", lambda e, w=w, m=m, par=par: e.dma_start(
                out=T["wstage"][:, par, :], in_=w[128 * m:128 * (m + 1), :]),
                writes=[("wst", par)], stream=("wsts", par), tag=f"ow{m}")
            for k in range(8):
                transpose_tile(ctx, T["wstage"][:, par, 128 * k:128 * (k + 1)],
                               T["woT"][:, k, 128 * m:128 * (m + 1)],
                               [("wst", par)], ("woT", k), f"ot{m}_{k}")
    else:
        S.dma("pool", lambda e: e.dma_start(
            out=T["woT"][:], in_=rearr_fm(ctx.dr["L1_attn_Wo"].ap())),
            writes=[("woT", k) for k in range(8)], stream="wo1", tag="owL1")
        S.dma("pool", lambda e: e.dma_start(
            out=T["brow"][0:1, :], in_=ctx.dr["L1_attn_bo"].ap()[None, :]),
            writes=["brow"], stream="brow", tag="obo")
    stream_x = getattr(ctx, "x_stream", False)
    if stream_x:
        xsrc_ap = rearr_tok(ctx.dr[ctx.xsrc].ap())
        xdst_ap = rearr_tok(ctx.dr[ctx.xdst].ap())
    for t in range(NC16):
        if stream_x:
            par = t % 2
            S.dma("sp", lambda e, t=t, par=par, xsrc_ap=xsrc_ap: e.dma_start(
                out=T["xc"][:, par, :], in_=xsrc_ap[:, t, :]),
                reads=list(getattr(ctx, "xsrc_reads", ())),
                writes=[("xc", par)], stream=("xc", par), tag=f"oxl{t}")
        for n in range(2):
            pid = ("psc", n)
            for k in range(8):
                def mfn(e, t=t, n=n, k=k, layer=layer):
                    return e.matmul(
                        ctx.psc[n][:],
                        T["ctx_fm"][:, k, 128 * t:128 * (t + 1)],
                        T["woT"][:, k, 512 * n:512 * (n + 1)],
                        start=(k == 0), stop=(k == 7 and layer == 0))
                S.op("pe", mfn, reads=[("ctx_fm", k), ("woT", k)],
                     writes=[pid], milestone=(k == 7 and layer == 0),
                     tag=f"o{layer}_{t}_{n}_{k}")
            if layer == 1:
                def bfn(e, n=n):
                    return e.matmul(ctx.psc[n][:], T["onesr"][0:1, 0:128],
                                    T["brow"][0:1, 512 * n:512 * (n + 1)],
                                    start=False, stop=True)
                S.op("pe", bfn, reads=["onesr", "brow"], writes=[pid],
                     milestone=True, tag=f"ob{t}_{n}")
            if stream_x:
                def afn(e, t=t, n=n, par=par):
                    xs = T["xc"][:, par, 512 * n:512 * (n + 1)]
                    return e.tensor_tensor(xs, xs, ctx.psc[n][:], ALU.add)
                S.op("dve", afn, reads=[pid, ("xc", par)], writes=[("xc", par)],
                     tag=f"ox{layer}_{t}_{n}")
            else:
                def afn(e, t=t, n=n):
                    xs = T["x"][:, t, 512 * n:512 * (n + 1)]
                    return e.tensor_tensor(xs, xs, ctx.psc[n][:], ALU.add)
                S.op("dve", afn, reads=[pid, ("x", t)], writes=[("x", t)],
                     tag=f"ox{layer}_{t}_{n}")
        if stream_x:
            S.dma("act", lambda e, t=t, par=par, xdst_ap=xdst_ap: e.dma_start(
                out=xdst_ap[:, t, :], in_=T["xc"][:, par, :]),
                reads=[("xc", par)], writes=[("xco", t)], stream="xco",
                tag=f"oxs{t}")


def p_l1_qkv(ctx):
    S = ctx.S
    T = dict(ctx.tiles)
    for proj, wname, bname in (("q", "L1_attn_Wq", "L1_attn_bq"),
                               ("k", "L1_attn_Wk", "L1_attn_bk")):
        S.dma("pool", lambda e, wname=wname: e.dma_start(
            out=T["w1a"][:], in_=rearr_fm(ctx.dr[wname].ap())),
            writes=[("w1a", k) for k in range(8)], stream="w1a", tag=f"l1w{proj}")
        S.dma("pool", lambda e, bname=bname: e.dma_start(
            out=T["brow"][0:1, :], in_=ctx.dr[bname].ap()[None, :]),
            writes=["brow"], stream="brow", tag=f"l1b{proj}")
        for t in range(NC16):
            for n in range(2):
                pid = ("psc", n)
                for k in range(8):
                    def mfn(e, t=t, n=n, k=k):
                        return e.matmul(
                            ctx.psc[n][:],
                            T["xn_fm"][:, k, 128 * t:128 * (t + 1)],
                            T["w1a"][:, k, 512 * n:512 * (n + 1)],
                            start=(k == 0), stop=False)
                    S.op("pe", mfn, reads=[("xn_fm", k), ("w1a", k)],
                         writes=[pid], milestone=False,
                         tag=f"l1{proj}m{t}_{n}_{k}")
                def bfn(e, n=n):
                    return e.matmul(ctx.psc[n][:], T["onesr"][0:1, 0:128],
                                    T["brow"][0:1, 512 * n:512 * (n + 1)],
                                    start=False, stop=True)
                S.op("pe", bfn, reads=["onesr", "brow"], writes=[pid],
                     milestone=True, tag=f"l1{proj}b{t}_{n}")
                ps3 = ctx.psc[n][:].rearrange("p (h d) -> p h d", d=64)
                cosb = T["ropec"][:, t, :].rearrange("p d -> p () d")\
                    .broadcast_to([128, 8, 32])
                sinb = T["ropes"][:, t, :].rearrange("p d -> p () d")\
                    .broadcast_to([128, 8, 32])
                rp3 = T["rp"][:, 0:512].rearrange("p (h d) -> p h d", d=64)
                ra = T["ra"][:].rearrange("p (h d) -> p h d", d=32)
                rb = T["rb"][:].rearrange("p (h d) -> p h d", d=32)
                S.op("dve", lambda e, ps3=ps3, cosb=cosb: e.tensor_tensor(
                    ra, ps3[:, :, 0:32], cosb, ALU.mult),
                    reads=[pid, "ropec"], writes=["ra"], tag=f"l1{proj}r1{t}_{n}")
                S.op("dve", lambda e, ps3=ps3, sinb=sinb: e.tensor_tensor(
                    rb, ps3[:, :, 32:64], sinb, ALU.mult),
                    reads=[pid, "ropes"], writes=["rb"], tag=f"l1{proj}r2{t}_{n}")
                S.op("dve", lambda e, rp3=rp3: e.tensor_tensor(
                    rp3[:, :, 0:32], ra, rb, ALU.subtract),
                    reads=["ra", "rb"], writes=["rp"], tag=f"l1{proj}r3{t}_{n}")
                S.op("dve", lambda e, ps3=ps3, sinb=sinb: e.tensor_tensor(
                    ra, ps3[:, :, 0:32], sinb, ALU.mult),
                    reads=[pid, "ropes"], writes=["ra"], tag=f"l1{proj}r4{t}_{n}")
                S.op("dve", lambda e, ps3=ps3, cosb=cosb: e.tensor_tensor(
                    rb, ps3[:, :, 32:64], cosb, ALU.mult),
                    reads=[pid, "ropec"], writes=["rb"], tag=f"l1{proj}r5{t}_{n}")
                S.op("dve", lambda e, rp3=rp3: e.tensor_tensor(
                    rp3[:, :, 32:64], ra, rb, ALU.add),
                    reads=["ra", "rb"], writes=["rp"], tag=f"l1{proj}r6{t}_{n}")
                for mm4 in range(4):
                    mglob = 4 * n + mm4
                    transpose_tile(ctx, T["rp"][:, 128 * mm4:128 * (mm4 + 1)],
                                   T["qkstg"][:, mglob, :], ["rp"],
                                   ("qkstg", mglob), f"l1{proj}t{t}_{n}_{mm4}")
            dstname = ctx.qdst if proj == "q" else ctx.kdst
            S.dma("act", lambda e, t=t, dstname=dstname: e.dma_start(
                out=rearr_fm(ctx.dr[dstname].ap())[:, :, 128 * t:128 * (t + 1)],
                in_=T["qkstg"][:]),
                reads=[("qkstg", m) for m in range(8)],
                writes=[("qkout", dstname, t)], stream=("qko", dstname),
                tag=f"l1{proj}s{t}")
    S.dma("pool", lambda e: e.dma_start(
        out=T["w1a"][:], in_=rearr_fm(ctx.dr["L1_attn_Wv"].ap())),
        writes=[("w1a", k) for k in range(8)], stream="w1a", tag="l1wv")
    S.dma("pool", lambda e: e.dma_start(
        out=T["brow"][0:1, :], in_=ctx.dr["L1_attn_bv"].ap()[None, :]),
        writes=["brow"], stream="brow", tag="l1bv")
    S.op("dve", lambda e: e.memset(T["vstg"][:], 1.0), writes=["vstg"],
         tag="vones1")
    p_v_common(ctx, lambda t, n, k: (T["xn_fm"][:, k, 128 * t:128 * (t + 1)],
                                     [("xn_fm", k), ("w1a", k)],
                                     T["w1a"][:, k, 512 * n:512 * (n + 1)]),
               nbias=True)


def p_l1_ffn(ctx):
    S = ctx.S
    T = dict(ctx.tiles)

    for wi, wname in enumerate(("L1_ffn_W1", "L1_ffn_W2")):
        w = ctx.dr[wname].ap()
        loads = ([(r, 0, D) for r in range(32)] if wi == 0
                 else [(r, hf, 1024) for r in range(8) for hf in range(4)])
        S.op("dve", lambda e: e.memset(T["gacc"][:, 0:1], 0.0), writes=["gacc"],
             tag=f"g{wi}z")
        for li, (r, hf, ncol) in enumerate(loads):
            par = li % 2
            S.dma("sp", lambda e, w=w, r=r, hf=hf, ncol=ncol, par=par:
                  e.dma_start(out=T["wqst"][:, par, 0:ncol],
                              in_=w[128 * r:128 * (r + 1),
                                    ncol * hf:ncol * (hf + 1)]),
                  writes=[("wqst", par)], stream=("wqs", par), tag=f"g{wi}l{li}")
            S.op("dve", lambda e, par=par, ncol=ncol: e.tensor_reduce(
                T["gacc"][:, 1:2], T["wqst"][:, par, 0:ncol], AX.X, ALU.add,
                apply_absolute_value=True),
                reads=[("wqst", par)], writes=["gred"], tag=f"g{wi}r{li}")
            S.op("dve", lambda e: e.tensor_tensor(
                T["gacc"][:, 0:1], T["gacc"][:, 0:1], T["gacc"][:, 1:2], ALU.add),
                reads=["gred", "gacc"], writes=["gacc"], tag=f"g{wi}a{li}")
        S.op("pe", lambda e: e.matmul(ctx.psc[0][0:1, 0:1], T["onesc"][:, 0:1],
                                      T["gacc"][:, 0:1], start=True, stop=True),
             reads=["gacc", "onesc"], writes=[("psc", 0)], tag=f"g{wi}mm")
        sz = float(DFF1 * D)
        S.op("dve", lambda e, sz=sz: e.tensor_scalar(
            T["gval"][0:1, 0:1], ctx.psc[0][0:1, 0:1], 1.0 / sz, 1e-5,
            ALU.mult, ALU.add),
            reads=[("psc", 0)], writes=["gval"], tag=f"g{wi}g")
        S.op("dve", lambda e: e.reciprocal(T["gval"][0:1, 0:1],
                                           T["gval"][0:1, 0:1]),
             reads=["gval"], writes=["gval"], tag=f"g{wi}i")
        S.op("pe", lambda e: e.matmul(ctx.psc[0][:, 0:1], T["onesrf"][0:1, 0:128],
                                      T["gval"][0:1, 0:1], start=True, stop=True),
             reads=["gval", "onesrf"], writes=[("psc", 0)], tag=f"g{wi}bb")
        S.op("dve", lambda e, wi=wi: e.tensor_copy(T["ginv"][:, wi:wi + 1],
                                                   ctx.psc[0][:, 0:1]),
             reads=[("psc", 0)], writes=[("ginv", wi)], tag=f"g{wi}e")

    def tern_quant(src_ap, dst_ap, wi, rids, wid, tag, ncol):
        y = T["qy"][:, 0:ncol]
        m = T["qg"][:, 0:ncol]
        S.op("dve", lambda e: e.tensor_single_scalar(
            y, src_ap, T["ginv"][:, wi:wi + 1], ALU.mult),
            reads=list(rids) + [("ginv", wi)], writes=["qy"], tag=tag + "y")
        abs_b2 = T["iconst"][:, 3:4].rearrange("p x -> p x")\
            .broadcast_to([128, ncol]).bitcast(I32)
        S.op("dve", lambda e: e.tensor_tensor(m.bitcast(I32), y.bitcast(I32),
                                              abs_b2, ALU.bitwise_and),
             reads=["qy", "iconst"], writes=["qg3"], tag=tag + "ma")
        S.op("dve", lambda e: e.tensor_single_scalar(m, m, 0.5, ALU.is_ge),
             reads=["qg3"], writes=["qg3"], tag=tag + "m")
        sgn_b = T["iconst"][:, 1:2].rearrange("p x -> p x")\
            .broadcast_to([128, ncol]).bitcast(I32)
        S.op("dve", lambda e: e.tensor_tensor(
            y.bitcast(I32), y.bitcast(I32), sgn_b, ALU.bitwise_and),
            reads=["qy", "iconst"], writes=["qy"], tag=tag + "q1")
        S.op("dve", lambda e: e.tensor_tensor(
            y.bitcast(I32), y.bitcast(I32), m.bitcast(I32), ALU.bitwise_or),
            reads=["qy", "qg3"], writes=["qy"], tag=tag + "q2")
        S.op("dve", lambda e: e.tensor_copy(dst_ap, y),
             reads=["qy"], writes=[wid], tag=tag + "o")

    w2 = ctx.dr["L1_ffn_W2"].ap()
    for r in range(8):
        for hf in range(4):
            par = hf % 2
            S.dma("sp", lambda e, r=r, hf=hf, par=par: e.dma_start(
                out=T["wqst"][:, par, 0:1024],
                in_=w2[128 * r:128 * (r + 1), 1024 * hf:1024 * (hf + 1)]),
                writes=[("wqst", par)], stream=("wqs", par), tag=f"w2l{r}_{hf}")
            tern_quant(T["wqst"][:, par, 0:1024], T["wqbf"][:, 0, 0:1024], 1,
                       [("wqst", par)], ("wqbf", 0), f"w2q{r}_{hf}_", 1024)
            for cc in range(8):
                c = 8 * hf + cc
                transpose_tile(ctx, T["wqbf"][:, 0, 128 * cc:128 * (cc + 1)],
                               T["w2stg"][:, par, :], [("wqbf", 0)],
                               ("w2stg", par), f"w2t{r}_{hf}_{cc}")
                S.dma("act", lambda e, r=r, c=c, par=par: e.dma_start(
                    out=ctx.dr["W2qT"].ap().rearrange("(c p) n -> p c n", p=128)
                    [:, c, 128 * r:128 * (r + 1)],
                    in_=T["w2stg"][:, par, :]),
                    reads=[("w2stg", par)], writes=[("w2p", r, c)],
                    stream=("wqTo", par), tag=f"w2s{r}_{c}")

    w1 = ctx.dr["L1_ffn_W1"].ap()
    for fg in range(8):
        for mc in range(4):
            m = 4 * fg + mc
            par = m % 2
            S.dma("sp", lambda e, m=m, par=par: e.dma_start(
                out=T["wqst"][:, par, 0:D], in_=w1[128 * m:128 * (m + 1), :]),
                writes=[("wqst", par)], stream=("wqs", par), tag=f"w1l{m}")
            tern_quant(T["wqst"][:, par, 0:D], T["wqbf"][:, 0, 0:D], 0,
                       [("wqst", par)], ("wqbf", 0), f"w1q{m}_", D)
            for c in range(8):
                transpose_tile(ctx, T["wqbf"][:, 0, 128 * c:128 * (c + 1)],
                               T["wT"][:, par, c, :], [("wqbf", 0)],
                               ("wT", par, c), f"w1t{m}_{c}")
            for t in range(4):
                pid = ("psc", t)
                for k in range(8):
                    def mfn(e, k=k, t=t, par=par):
                        return e.matmul(ctx.psc[t][:],
                                        T["wT"][:, par, k, :],
                                        T["xn_fm"][:, k, 512 * t:512 * (t + 1)],
                                        start=(k == 0), stop=(k == 7))
                    S.op("pe", mfn, reads=[("wT", par, k), ("xn_fm", k)],
                         writes=[pid], milestone=(k == 7), tag=f"w1m{m}_{t}_{k}")
                def efn(e, mc=mc, m=m, t=t):
                    return e.tensor_scalar(
                        T["hfm"][:, mc, 512 * t:512 * (t + 1)], ctx.psc[t][:],
                        T["b1sb"][:, m:m + 1], 0.0, ALU.add, ALU.max)
                S.op("dve", efn, reads=[pid, "b1sb"], writes=[("hfm", mc)],
                     tag=f"w1e{m}_{t}")
        par = fg % 2
        S.dma("sp", lambda e, fg=fg, par=par: e.dma_start(
            out=T["w2g"][:, 0, :, :],
            in_=ctx.dr["W2qT"].ap().rearrange("(c p) n -> p c n", p=128)
            [:, 4 * fg:4 * (fg + 1), :]),
            reads=[("w2p", r, c) for r in range(8)
                   for c in range(4 * fg, 4 * (fg + 1))],
            writes=[("w2g", par)], stream=("w2g", par), tag=f"w2gl{fg}")
        if fg == 7:
            S.dma("pool", lambda e: e.dma_start(
                out=T["brow"][0:1, :], in_=ctx.dr["L1_ffn_b2"].ap()[None, :]),
                writes=["brow"], stream="brow", tag="b2l")
        for t in range(NC16):
            for n in range(2):
                pid = ("psc", n)
                last = (fg == 7)
                for ks in range(4):
                    def mfn(e, ks=ks, t=t, n=n, par=par, last=last):
                        return e.matmul(
                            ctx.psc[n][:],
                            T["hfm"][:, ks, 128 * t:128 * (t + 1)],
                            T["w2g"][:, 0, ks, 512 * n:512 * (n + 1)],
                            start=(ks == 0), stop=(ks == 3 and not last))
                    S.op("pe", mfn, reads=[("hfm", ks), ("w2g", par)],
                         writes=[pid], milestone=(ks == 3 and not last),
                         tag=f"w2m{fg}_{t}_{n}_{ks}")
                if last:
                    def bfn(e, n=n):
                        return e.matmul(ctx.psc[n][:], T["onesr"][0:1, 0:128],
                                        T["brow"][0:1, 512 * n:512 * (n + 1)],
                                        start=False, stop=True)
                    S.op("pe", bfn, reads=["onesr", "brow"], writes=[pid],
                         milestone=True, tag=f"w2b{t}_{n}")
                def afn(e, t=t, n=n):
                    xs = T["x"][:, t, 512 * n:512 * (n + 1)]
                    return e.tensor_tensor(xs, xs, ctx.psc[n][:], ALU.add)
                S.op("dve", afn, reads=[pid, ("x", t)], writes=[("x", t)],
                     tag=f"w2x{fg}_{t}_{n}")


def new_ctx(nc, stack):
    ctx = Ctx()
    ctx.nc = nc
    ctx.stack = stack
    ctx.tiles = {}
    ctx.dr = {}
    ctx.S = Sched(nc)
    return ctx


def din(ctx, name, shape, dtype):
    ctx.dr[name] = ctx.nc.dram_tensor(name, shape, dtype, kind="ExternalInput")


def dout(ctx, name, shape, dtype):
    ctx.dr[name] = ctx.nc.dram_tensor(name, shape, dtype, kind="ExternalOutput")


def dint(ctx, name, shape, dtype):
    ctx.dr[name] = ctx.nc.dram_tensor(name, shape, dtype, kind="Internal")


def view_flat(t, n0, n1):
    ap = t[:]
    nd = len(ap.shape)
    names = " ".join(f"d{i}" for i in range(nd - 1))
    ap = ap.rearrange(f"p {names} -> p ({names})")
    return ap[:, n0:n1]


def alloc_tiny(ctx):
    sb(ctx, "ident", [128, 128], BF16)
    sb(ctx, "iconst", [128, 4], I32)
    sb(ctx, "onesr", [1, 128], BF16)


def alloc_psum(ctx):
    nc = ctx.nc
    ctx.psc = [ctx.stack.enter_context(nc.psum_tensor(f"psc{i}", [128, 512], F32))
               for i in range(4)]
    ctx.ps2 = [ctx.stack.enter_context(
        nc.psum_tensor(f"ps2{i}", [128, 2, 512], F32)) for i in range(2)]
    ctx.ps_t = [ctx.ps2[i][:].bitcast(BF16).rearrange("p a b -> p (a b)")[:, 0:128]
                for i in range(2)]
    ctx.tp_rot = [0]


def alloc_quant(ctx, qs, tern_only=False):
    names = ("qy", "qg") if tern_only else ("qy", "qt2", "qg", "ql")
    for nm in names:
        sb(ctx, nm, [128, qs], F32)
    if not tern_only:
        sb(ctx, "qbm", [128, qs // 16], F32)
        sb(ctx, "qs", [128, qs // 16], F32)
        sb(ctx, "qsi", [128, qs // 16], F32)


def alloc_ln(ctx):
    sb(ctx, "lnst", [128, 8], F32)
    sb(ctx, "lneps", [128, 1], F32)
    sb(ctx, "lnsq", [128, D], F32)
    sb(ctx, "xn32", [128, D], F32)
    sb(ctx, "xnbf", [128, D], BF16)


def build_launch1a():
    nc = bass.Bass("TRN2", target_bir_lowering=False)
    stack = contextlib.ExitStack()
    ctx = new_ctx(nc, stack)
    din(ctx, "L0_ffn_w_gate", [DFF0, D], F32)
    din(ctx, "L0_ffn_w_up", [DFF0, D], F32)
    din(ctx, "L0_ffn_w_down", [D, DFF0], F32)
    din(ctx, "ident", [128, 128], BF16)
    dout(ctx, "wgqT", [D, DFF0], BF16)
    dout(ctx, "wuqT", [D, DFF0], BF16)
    dout(ctx, "wdqT", [DFF0, D], BF16)
    with stack:
        alloc_tiny(ctx)
        alloc_psum(ctx)
        alloc_quant(ctx, 1408)
        sb(ctx, "wqst", [128, 2, 1408], F32)
        sb(ctx, "wqbf", [128, 2, 1408], BF16)
        sb(ctx, "wqT", [128, 2, 11, 128], BF16)
        p_consts(ctx)
        p_prep_l0ffn_weights(ctx)
        ctx.S.barrier_step()
        ctx.S.emit(stack)
    return nc


def build_launch1b():
    nc = bass.Bass("TRN2", target_bir_lowering=False)
    stack = contextlib.ExitStack()
    ctx = new_ctx(nc, stack)
    din(ctx, "x0", [TOWN, D], F32)
    din(ctx, "L0_attn_w_qkv", [3 * D, D], F32)
    din(ctx, "wgqT", [D, DFF0], BF16)
    din(ctx, "wuqT", [D, DFF0], BF16)
    din(ctx, "wdqT", [DFF0, D], BF16)
    din(ctx, "ident", [128, 128], BF16)
    dout(ctx, "kout", [D, TOWN], BF16)
    dout(ctx, "vout", [TOWN, 2048], BF16)
    dout(ctx, "qout", [D, TOWN], BF16)
    dout(ctx, "xf", [TOWN, D], F32)
    ctx.kdst, ctx.vdst, ctx.qdst = "kout", "vout", "qout"
    with stack:
        alloc_tiny(ctx)
        alloc_psum(ctx)
        alloc_quant(ctx, 1024)
        alloc_ln(ctx)
        sb(ctx, "x", [128, NC16, D], F32)
        sb(ctx, "xn_fm", [128, 8, TOWN], BF16)
        sb(ctx, "xnq_fm", [128, 8, TOWN], BF16)
        sb(ctx, "xnqbf", [128, D], BF16)
        sb(ctx, "wstage", [128, 2, D], BF16)
        sb(ctx, "wT", [128, 2, 8, 128], BF16)
        sb(ctx, "wvT", [128, 8, D], BF16)
        sb(ctx, "kstg", [128, TOWN], BF16)
        sb(ctx, "vstg", [128, 2048], BF16)
        sb(ctx, "gs", [128, 512], F32)
        sb(ctx, "h32", [128, 512], F32)
        sb(ctx, "hq", [128, 512], BF16)
        sb(ctx, "hq_fm", [128, 4, 128], BF16)
        xfm = ctx.tiles["xn_fm"]
        ctx.tiles["wg"] = view_flat(xfm, 0, 8192).rearrange(
            "p (a k c) -> p a k c", a=2, k=8)
        ctx.tiles["wu"] = view_flat(xfm, 8192, 16384).rearrange(
            "p (a k c) -> p a k c", a=2, k=8)
        ctx.tiles["wdn"] = ctx.tiles["wvT"][:].rearrange(
            "p k n -> p (k n)").rearrange("p (a k c) -> p a k c", a=2, k=4)
        p_consts(ctx)
        p_load_x(ctx, "x0")
        p_ln(ctx, 0)
        p_l0_qkv(ctx)
        for k in range(8):
            ctx.S.alias(("wg", 0), [("xn_fm", k)])
            ctx.S.alias(("wg", 1), [("xn_fm", k)])
            ctx.S.alias(("wu", 0), [("xn_fm", k)])
            ctx.S.alias(("wu", 1), [("xn_fm", k)])
            ctx.S.alias(("wdn", 0), [("wvT", k)])
            ctx.S.alias(("wdn", 1), [("wvT", k)])
        p_l0_ffn(ctx)
        p_store_x(ctx, "xf")
        ctx.S.barrier_step()
        ctx.S.emit(stack)
    return nc


def build_attn_launch(layer):
    nc = bass.Bass("TRN2", target_bir_lowering=False)
    stack = contextlib.ExitStack()
    ctx = new_ctx(nc, stack)
    din(ctx, "xf", [TOWN, D], F32)
    din(ctx, "qin", [D, TOWN], BF16)
    din(ctx, "kfull", [4 * D, TOWN], BF16)
    din(ctx, "vfull", [8192, 2048], BF16)
    if layer == 0:
        din(ctx, "L0_attn_w_out", [D, D], F32)
    else:
        din(ctx, "L1_attn_Wo", [D, D], F16)
        din(ctx, "L1_attn_bo", [D], F16)
    din(ctx, "ident", [128, 128], BF16)
    din(ctx, "masks", [128, 2048], BF16)
    din(ctx, "sched", [1, SCHED_COLS], I32)
    din(ctx, "wtab", [128, 2 * NROUND], F32)
    dout(ctx, "xout", [TOWN, D], F32)
    with stack:
        alloc_tiny(ctx)
        alloc_psum(ctx)
        sb(ctx, "sched", [1, SCHED_COLS], I32)
        sb(ctx, "masks", [128, 4, 512], BF16)
        sb(ctx, "wtab", [128, 2 * NROUND], F32)
        sb(ctx, "ctxacc", [128, 2, 8, 1024], F32)
        sb(ctx, "sacc", [128, 16, 512], F32)
        sb(ctx, "kslab", [128, 2, 8, 1024], BF16)
        sb(ctx, "vslab", [128, 8, 2048], BF16)
        sb(ctx, "qsb", [128, 2, 8, 512], BF16)
        sb(ctx, "ptile", [128, 2, 1024], BF16)
        sb(ctx, "nstg2", [128, 512], F32)
        sb(ctx, "xc", [128, 2, D], F32)
        sb(ctx, "wstage", [128, 2, D], BF16)
        sb(ctx, "brow", [1, D], BF16)
        ctx.tiles["ctx_fm"] = ctx.tiles["kslab"][:].rearrange(
            "p a k c -> p (a k c)").rearrange("p (k t) -> p k t", k=8)
        ctx.tiles["woT"] = view_flat(ctx.tiles["vslab"], 0, 8192).rearrange(
            "p (k n) -> p k n", k=8)
        p_consts(ctx)
        ctx.ksrc, ctx.vsrc, ctx.qsrc = "kfull", "vfull", "qin"
        ctx.xsrc, ctx.xdst = "xf", "xout"
        ctx.x_stream = True
        ctx.sched_off = 0
        ctx.alias_ctx = True
        p_attention(ctx, layer)
        p_outproj(ctx, layer)
        ctx.S.barrier_step()
        ctx.S.emit(stack)
    return nc


def build_launch3():
    nc = bass.Bass("TRN2", target_bir_lowering=False)
    stack = contextlib.ExitStack()
    ctx = new_ctx(nc, stack)
    din(ctx, "xf", [TOWN, D], F32)
    din(ctx, "L1_attn_Wq", [D, D], F16)
    din(ctx, "L1_attn_bq", [D], F16)
    din(ctx, "L1_attn_Wk", [D, D], F16)
    din(ctx, "L1_attn_bk", [D], F16)
    din(ctx, "L1_attn_Wv", [D, D], F16)
    din(ctx, "L1_attn_bv", [D], F16)
    din(ctx, "L1_ffn_W1", [DFF1, D], F32)
    din(ctx, "b1sb_in", [128, 32], F32)
    din(ctx, "L1_ffn_W2", [D, DFF1], F32)
    din(ctx, "L1_ffn_b2", [D], F32)
    din(ctx, "ident", [128, 128], BF16)
    din(ctx, "ropec", [TOWN, 32], F32)
    din(ctx, "ropes", [TOWN, 32], F32)
    dout(ctx, "k1out", [D, TOWN], BF16)
    dout(ctx, "v1out", [TOWN, 2048], BF16)
    dout(ctx, "q1out", [D, TOWN], BF16)
    dout(ctx, "xf1", [TOWN, D], F32)
    dint(ctx, "W2qT", [DFF1, D], BF16)
    ctx.kdst, ctx.vdst, ctx.qdst = "k1out", "v1out", "q1out"
    with stack:
        alloc_tiny(ctx)
        alloc_psum(ctx)
        alloc_quant(ctx, 2048, tern_only=True)
        alloc_ln(ctx)
        sb(ctx, "x", [128, NC16, D], F32)
        sb(ctx, "xn_fm", [128, 8, TOWN], BF16)
        sb(ctx, "ropec", [128, NC16, 32], F32)
        sb(ctx, "ropes", [128, NC16, 32], F32)
        sb(ctx, "w1a", [128, 8, D], BF16)
        sb(ctx, "rp", [128, 512], BF16)
        sb(ctx, "ra", [128, 256], F32)
        sb(ctx, "rb", [128, 256], F32)
        sb(ctx, "qkstg", [128, 8, 128], BF16)
        sb(ctx, "vstg", [128, 2048], BF16)
        sb(ctx, "gacc", [128, 2], F32)
        sb(ctx, "gval", [1, 1], F32)
        sb(ctx, "gbf", [128, 1], BF16)
        sb(ctx, "ginv", [128, 2], F32)
        sb(ctx, "onesc", [128, 1], F32)
        sb(ctx, "onesrf", [1, 128], F32)
        sb(ctx, "b1sb", [128, 32], F32)
        sb(ctx, "hfm", [128, 4, TOWN], BF16)
        sb(ctx, "w2g", [128, 1, 4, D], BF16)
        sb(ctx, "wqst", [128, 2, 2048], F32)
        sb(ctx, "wqbf", [128, 1, 2048], BF16)
        sb(ctx, "wT", [128, 2, 8, 128], BF16)
        sb(ctx, "w2stg", [128, 2, 128], BF16)
        sb(ctx, "brow", [1, D], BF16)
        S = ctx.S
        p_consts(ctx)
        S.dma("sp", lambda e: e.dma_start(out=ctx.tiles["b1sb"][:],
                                          in_=ctx.dr["b1sb_in"].ap()),
              writes=["b1sb"], tag="b1l")
        S.op("dve", lambda e: e.memset(ctx.tiles["onesc"][:], 1.0),
             writes=["onesc"], tag="onesc")
        S.op("dve", lambda e: e.memset(ctx.tiles["onesrf"][:], 1.0),
             writes=["onesrf"], tag="onesrf")
        p_load_x(ctx, "xf")
        p_ln(ctx, 1)
        p_l1_qkv(ctx)
        p_l1_ffn(ctx)
        p_store_x(ctx, "xf1")
        ctx.S.barrier_step()
        ctx.S.emit(stack)
    return nc


def host_sched_table(j):
    rows = []

    def slab_offs(sp):
        rk = sp if sp < 4 else 7 - sp
        hf = 0 if sp < 4 else 1
        return rk * 1024 * 2048 + hf * 1024, (2048 * rk + 1024 * hf) * 2048

    def add(sp, stripe):
        ko, vo = slab_offs(sp)
        rows.append([ko, vo, stripe * 1024 + 0, stripe * 1024 + 512, stripe])

    add(j, 0)
    add(7 - j, 1)
    for sp in range(j):
        add(sp, 0)
    for sp in range(7 - j):
        add(sp, 1)
    assert len(rows) == NROUND
    tab = np.zeros((1, SCHED_COLS), dtype=np.int32)
    wtab = np.zeros((128, 2 * NROUND), dtype=np.float32)
    for r, row in enumerate(rows):
        wtab[:, 2 * r + row[4]] = 1.0
    for r, row in enumerate(rows):
        tab[0, 5 * r:5 * r + 4] = row[:4]
    return tab, wtab


def host_masks():
    m = np.zeros((128, 4, 512), dtype=np.float32)
    qi = np.arange(512)
    for dd in range(4):
        for p in range(128):
            m[p, dd, :] = (p <= qi - 128 * dd).astype(np.float32)
    return np.ascontiguousarray(m.reshape(128, 2048).astype(ml_dtypes.bfloat16))


def host_rope(positions):
    inv = np.exp(np.arange(0, HD, 2, dtype=np.float32) *
                 (-np.log(10000.0) / HD))
    ang = positions[:, None].astype(np.float32) * inv[None, :]
    sin = np.sin(ang).astype(np.float16).astype(np.float32)
    cos = np.cos(ang).astype(np.float16).astype(np.float32)
    return np.ascontiguousarray(cos), np.ascontiguousarray(sin)


def own_rows(j):
    a = np.arange(1024 * j, 1024 * (j + 1))
    b = np.arange(1024 * (7 - j), 1024 * (8 - j))
    return np.concatenate([a, b])


def kernel(**inputs):
    from concourse.bass_utils import run_bass_kernel_spmd
    ident = np.ascontiguousarray(np.eye(128, dtype=np.float32)
                                 .astype(ml_dtypes.bfloat16))
    masks = host_masks()
    x0 = np.asarray(inputs["L0_attn_x"])
    B, T, _ = x0.shape
    cores = list(range(8))
    rows = [own_rows(c % 4) for c in cores]
    x0s = [np.ascontiguousarray(x0[c // 4][rows[c]]) for c in cores]
    sw = [host_sched_table(c % 4) for c in cores]
    scheds = [x[0] for x in sw]
    wtabs = [np.ascontiguousarray(x[1]) for x in sw]
    ropecs, ropess = [], []
    for c in cores:
        cos, sin = host_rope(rows[c])
        ropecs.append(cos)
        ropess.append(sin)
    b1sb = np.ascontiguousarray(
        np.asarray(inputs["L1_ffn_b1"], np.float32).reshape(32, 128).T)
    f32 = np.float32

    nc1a = build_launch1a()
    in1a = [dict(
        L0_ffn_w_gate=np.asarray(inputs["L0_ffn_w_gate"], f32),
        L0_ffn_w_up=np.asarray(inputs["L0_ffn_w_up"], f32),
        L0_ffn_w_down=np.asarray(inputs["L0_ffn_w_down"], f32),
        ident=ident) for c in cores]
    r1a = run_bass_kernel_spmd(nc1a, in1a, core_ids=cores).results
    print("L1a done", flush=True)

    nc1b = build_launch1b()
    in1b = [dict(
        x0=x0s[c],
        L0_attn_w_qkv=np.asarray(inputs["L0_attn_w_qkv"], f32),
        wgqT=r1a[c]["wgqT"], wuqT=r1a[c]["wuqT"], wdqT=r1a[c]["wdqT"],
        ident=ident) for c in cores]
    r1 = run_bass_kernel_spmd(nc1b, in1b, core_ids=cores).results
    print("L1b done", flush=True)

    kf = [np.concatenate([r1[4 * g + r]["kout"] for r in range(4)], axis=0)
          for g in range(2)]
    vf = [np.concatenate([r1[4 * g + r]["vout"] for r in range(4)], axis=0)
          for g in range(2)]

    nc2 = build_attn_launch(0)
    in2 = [dict(
        xf=r1[c]["xf"], qin=r1[c]["qout"], kfull=kf[c // 4], vfull=vf[c // 4],
        L0_attn_w_out=np.asarray(inputs["L0_attn_w_out"], f32),
        ident=ident, masks=masks, sched=scheds[c], wtab=wtabs[c]) for c in cores]
    r2 = run_bass_kernel_spmd(nc2, in2, core_ids=cores).results
    print("L2 done", flush=True)

    nc3 = build_launch3()
    in3 = [dict(
        xf=r2[c]["xout"],
        L1_attn_Wq=np.asarray(inputs["L1_attn_Wq"]),
        L1_attn_bq=np.asarray(inputs["L1_attn_bq"]),
        L1_attn_Wk=np.asarray(inputs["L1_attn_Wk"]),
        L1_attn_bk=np.asarray(inputs["L1_attn_bk"]),
        L1_attn_Wv=np.asarray(inputs["L1_attn_Wv"]),
        L1_attn_bv=np.asarray(inputs["L1_attn_bv"]),
        L1_ffn_W1=np.asarray(inputs["L1_ffn_W1"], f32),
        b1sb_in=b1sb,
        L1_ffn_W2=np.asarray(inputs["L1_ffn_W2"], f32),
        L1_ffn_b2=np.asarray(inputs["L1_ffn_b2"], f32),
        ident=ident, ropec=ropecs[c], ropes=ropess[c]) for c in cores]
    r3 = run_bass_kernel_spmd(nc3, in3, core_ids=cores).results
    print("L3 done", flush=True)

    kf1 = [np.concatenate([r3[4 * g + r]["k1out"] for r in range(4)], axis=0)
           for g in range(2)]
    vf1 = [np.concatenate([r3[4 * g + r]["v1out"] for r in range(4)], axis=0)
           for g in range(2)]

    nc4 = build_attn_launch(1)
    in4 = [dict(
        xf=r3[c]["xf1"], qin=r3[c]["q1out"], kfull=kf1[c // 4],
        vfull=vf1[c // 4],
        L1_attn_Wo=np.asarray(inputs["L1_attn_Wo"]),
        L1_attn_bo=np.asarray(inputs["L1_attn_bo"]),
        ident=ident, masks=masks, sched=scheds[c], wtab=wtabs[c]) for c in cores]
    r4 = run_bass_kernel_spmd(nc4, in4, core_ids=cores).results
    print("L4 done", flush=True)

    out = np.zeros((B, T, D), dtype=np.float32)
    for c in cores:
        out[c // 4][rows[c]] = r4[c]["xout"]
    return out


def vw(t, n0_bytes, shape, dtype):
    ap = t[:]
    nd = len(ap.shape)
    names = " ".join(f"d{i}" for i in range(nd - 1))
    flat = ap.rearrange(f"p {names} -> p ({names})")
    esz = {F32: 4, BF16: 2, I32: 4}[dtype]
    if ap.dtype != dtype:
        flat = flat.bitcast(dtype)
    n0 = n0_bytes // esz
    n1 = n0 + int(np.prod(shape))
    v = flat[:, n0:n1]
    if len(shape) > 1:
        dims = " ".join(f"e{i}" for i in range(len(shape)))
        kw = {f"e{i}": shape[i] for i in range(len(shape))}
        v = v.rearrange(f"p ({dims}) -> p {dims}", **kw)
    return v


def build_fused():
    nc = bass.Bass("TRN2", target_bir_lowering=False)
    stack = contextlib.ExitStack()
    ctx = new_ctx(nc, stack)
    din(ctx, "x0", [TOWN, D], F32)
    din(ctx, "L0_attn_w_qkv", [3 * D, D], F32)
    din(ctx, "L0_attn_w_out", [D, D], F32)
    din(ctx, "L0_ffn_w_gate", [DFF0, D], F32)
    din(ctx, "L0_ffn_w_up", [DFF0, D], F32)
    din(ctx, "L0_ffn_w_down", [D, DFF0], F32)
    din(ctx, "L1_attn_Wq", [D, D], F16)
    din(ctx, "L1_attn_bq", [D], F16)
    din(ctx, "L1_attn_Wk", [D, D], F16)
    din(ctx, "L1_attn_bk", [D], F16)
    din(ctx, "L1_attn_Wv", [D, D], F16)
    din(ctx, "L1_attn_bv", [D], F16)
    din(ctx, "L1_attn_Wo", [D, D], F16)
    din(ctx, "L1_attn_bo", [D], F16)
    din(ctx, "L1_ffn_W1", [DFF1, D], F32)
    din(ctx, "b1sb_in", [128, 32], F32)
    din(ctx, "L1_ffn_W2", [D, DFF1], F32)
    din(ctx, "L1_ffn_b2", [D], F32)
    din(ctx, "ident", [128, 128], BF16)
    din(ctx, "masks", [128, 2048], BF16)
    din(ctx, "sched", [1, SCHED_COLS], I32)
    din(ctx, "wtab", [128, 2 * NROUND], F32)
    din(ctx, "ropec", [TOWN, 32], F32)
    din(ctx, "ropes", [TOWN, 32], F32)
    dout(ctx, "xout", [TOWN, D], F32)
    for nm, shp in (("wgqT", [D, DFF0]), ("wuqT", [D, DFF0]),
                    ("wdqT", [DFF0, D]), ("W2qT", [DFF1, D]),
                    ("woTd", [D, D]),
                    ("qpark0", [D, TOWN]), ("qpark1", [D, TOWN]),
                    ("kag_in0", [D, TOWN]), ("kag_in1", [D, TOWN]),
                    ("vag_in0", [TOWN, 2048]), ("vag_in1", [TOWN, 2048])):
        dint(ctx, nm, shp, BF16)
    for nm in ("kag_out0", "kag_out1"):
        dint(ctx, nm, [4 * D, TOWN], BF16)
    for nm in ("vag_out0", "vag_out1"):
        dint(ctx, nm, [8192, 2048], BF16)
    dbg = getattr(build_fused, "_dbg", False)
    for nm in ("xfd", "x1d", "xf1d"):
        (dout if dbg else dint)(ctx, nm, [TOWN, D], F32)

    ctx.defer_rope = True
    with stack:
        alloc_tiny(ctx)
        alloc_psum(ctx)
        alloc_ln(ctx)
        T = ctx.tiles
        sb(ctx, "x", [128, NC16, D], F32)
        sb(ctx, "xn_fm", [128, 8, TOWN], BF16)
        sb(ctx, "kslab", [128, 2, 8, 1024], BF16)
        sb(ctx, "vslab", [128, 8, 2048], BF16)
        sb(ctx, "qsb", [128, 2, 8, 512], BF16)
        sb(ctx, "ptile", [128, 2, 1024], BF16)
        sb(ctx, "nstg2", [128, 512], F32)
        sb(ctx, "xc", [128, 2, D], F32)
        sb(ctx, "masks", [128, 4, 512], BF16)
        sb(ctx, "sched", [1, SCHED_COLS], I32)
        sb(ctx, "wtab", [128, 2 * NROUND], F32)
        sb(ctx, "brow", [1, D], BF16)
        sb(ctx, "gacc", [128, 2], F32)
        sb(ctx, "gval", [1, 1], F32)
        sb(ctx, "gbf", [128, 1], BF16)
        sb(ctx, "ginv", [128, 2], F32)
        sb(ctx, "onesc", [128, 1], F32)
        sb(ctx, "onesrf", [1, 128], F32)
        sb(ctx, "b1sb", [128, 32], F32)
        vs = T["vslab"]
        T["wqst"] = vw(vs, 0, [2, 1024], F32)
        T["wqbf"] = vw(vs, 8192, [2, 1024], BF16)
        T["wqT"] = vw(vs, 12288, [2, 11, 128], BF16)
        T["wstage"] = vw(vs, 0, [2, 1024], BF16)
        T["wT"] = vw(vs, 4096, [2, 8, 128], BF16)
        T["wvT"] = vw(vs, 8192, [8, 1024], BF16)
        T["wdn"] = vw(vs, 8192, [2, 4, 1024], BF16)
        T["kstg"] = vw(vs, 24576, [TOWN], BF16)
        T["vstg"] = vw(vs, 28672, [2048], BF16)
        T["gs"] = vw(vs, 0, [512], F32)
        T["h32"] = vw(vs, 2048, [512], F32)
        T["hq"] = vw(vs, 4096, [512], BF16)
        T["hq_fm"] = vw(vs, 5120, [4, 128], BF16)
        T["woT"] = vw(vs, 0, [8, 1024], BF16)
        T["w1a"] = vw(T["qsb"], 0, [8, 1024], BF16)
        T["rp"] = vw(T["ptile"], 0, [512], BF16)
        T["ra"] = vw(T["ptile"], 1024, [256], F32)
        T["rb"] = vw(T["ptile"], 2048, [256], F32)
        T["qkstg"] = vw(T["nstg2"], 0, [8, 128], BF16)
        T["wqstL1"] = vw(vs, 0, [2, 1024], F32)
        T["wqbfL1"] = vw(vs, 8192, [1, 1024], BF16)
        T["wTl1"] = vw(vs, 10240, [2, 8, 128], BF16)
        T["vstgL1"] = vw(vs, 16384, [2048], BF16)
        T["w2stg"] = vw(vs, 14336, [2, 128], BF16)
        T["hfm"] = vw(T["kslab"], 0, [4, TOWN], BF16)
        T["ropec"] = vw(T["kslab"], 24576, [NC16, 32], F32)
        T["ropes"] = vw(T["kslab"], 26624, [NC16, 32], F32)
        T["w2g"] = vw(T["kslab"], 16384, [1, 4, D], BF16)
        T["qy"] = vw(T["qsb"], 0, [1024], F32)
        T["qt2"] = vw(T["qsb"], 4096, [1024], F32)
        T["qg"] = vw(T["qsb"], 8192, [1024], F32)
        T["qbm"] = vw(T["qsb"], 12288, [64], F32)
        T["qs"] = vw(T["qsb"], 12544, [64], F32)
        T["qsi"] = vw(T["qsb"], 12800, [64], F32)
        T["ql"] = vw(T["ptile"], 0, [1024], F32)
        T["wg"] = vw(T["xn_fm"], 0, [2, 8, 512], BF16)
        T["wu"] = vw(T["xn_fm"], 16384, [2, 8, 512], BF16)
        T["ctxacc"] = vw(T["x"], 0, [2, 8, 1024], F32)
        T["xnqbf"] = vw(T["xc"], 4096, [1024], BF16)
        T["sacc"] = vw(T["xn_fm"], 0, [16, 512], F32)
        T["xnq_fm"] = vw(T["kslab"], 0, [8, TOWN], BF16)
        T["ctx_fm"] = vw(T["kslab"], 0, [8, TOWN], BF16)

        S = ctx.S

        def phase_alias(new_ids, old_ids):
            for nid in new_ids:
                S.alias(nid, old_ids)

        p_consts(ctx)
        S.dma("sp", lambda e: e.dma_start(out=T["b1sb"][:],
                                          in_=ctx.dr["b1sb_in"].ap()),
              writes=["b1sb"], tag="b1l")
        S.op("dve", lambda e: e.memset(T["onesc"][:], 1.0), writes=["onesc"])
        S.op("dve", lambda e: e.memset(T["onesrf"][:], 1.0), writes=["onesrf"])
        p_load_x(ctx, "x0")
        p_prep_l0ffn_weights(ctx)
        ctx.woT_dram = "woTd"
        ctx.woT_reads = [("wouT", r) for r in range(8)]
        p_prep_wout(ctx)
        phase_alias([("wst", 0), ("wst", 1)] + [("wT", p, k) for p in range(2)
                                                for k in range(8)]
                    + [("wvT", k) for k in range(8)] + ["kstg", "vstg"],
                    [("wqst", 0), ("wqst", 1), ("wqbf", 0), ("wqbf", 1)]
                    + [("wqT", p, c) for p in range(2) for c in range(11)])
        p_ln(ctx, 0)
        ctx.kdst, ctx.vdst, ctx.qdst = "kag_in0", "vag_in0", "qpark0"
        p_l0_qkv(ctx)
        rg = [[0, 1, 2, 3], [4, 5, 6, 7]]
        S.collective(lambda e: e.collective_compute(
            "AllGather", ALU.bypass, replica_groups=rg,
            ins=[ctx.dr["kag_in0"].ap().opt()],
            outs=[ctx.dr["kag_out0"].ap().opt()]),
            reads=[("kout", "kag_in0", mk_) for mk_ in range(8)],
            writes=["agk0"], tag="agk0")
        S.collective(lambda e: e.collective_compute(
            "AllGather", ALU.bypass, replica_groups=rg,
            ins=[ctx.dr["vag_in0"].ap().opt()],
            outs=[ctx.dr["vag_out0"].ap().opt()]),
            reads=[("vout", t) for t in range(NC16)],
            writes=["agv0"], tag="agv0")
        phase_alias([("wdn", 0), ("wdn", 1), "gs", "h32", "hq"]
                    + [("hq_fm", k) for k in range(4)],
                    [("wst", 0), ("wst", 1), "kstg", "vstg"]
                    + [("wT", p, k) for p in range(2) for k in range(8)]
                    + [("wvT", k) for k in range(8)])
        phase_alias([("wg", 0), ("wg", 1), ("wu", 0), ("wu", 1)],
                    [("xn_fm", m) for m in range(8)])
        p_l0_ffn(ctx)
        S.dma("act", lambda e: e.dma_start(
            out=rearr_tok(ctx.dr["xfd"].ap()), in_=T["x"][:]),
            reads=[("x", c) for c in range(NC16)], writes=["xfd"], tag="xspill0")
        phase_alias([("acc", hp, qt, st) for hp in range(8) for qt in range(2)
                     for st in range(2)],
                    [("x", c) for c in range(NC16)] + ["xfd"])
        phase_alias([("sac", i, hh) for i in range(16) for hh in range(2)],
                    [("xn_fm", m) for m in range(8)]
                    + [("wg", 0), ("wg", 1), ("wu", 0), ("wu", 1)])
        phase_alias([("kslab", 0), ("kslab", 1)],
                    [("xnq_fm", k) for k in range(8)])
        phase_alias(["vslab"],
                    [("wdn", 0), ("wdn", 1), "gs", "h32", "hq"]
                    + [("hq_fm", k) for k in range(4)])
        phase_alias([("qsb", 0), ("qsb", 1)],
                    ["qy", "qhi", "qg3", "qbm", "qsc", "qsi"])
        phase_alias([("pt", 0), ("pt", 1)], ["qlo"])
        phase_alias([("xc", 0), ("xc", 1)], ["xnqbf"])
        ctx.ksrc, ctx.vsrc, ctx.qsrc = "kag_out0", "vag_out0", "qpark0"
        ctx.k_reads = ["agk0"]
        ctx.v_reads = ["agv0"]
        ctx.q_reads = [("kout", "qpark0", mk_) for mk_ in range(8)]
        ctx.xsrc_reads = ["xfd"]
        ctx.xsrc, ctx.xdst = "xfd", "x1d"
        ctx.x_stream = True
        ctx.sched_off = 0
        ctx.alias_ctx = True
        p_attention(ctx, 0)
        p_outproj(ctx, 0)
        phase_alias([("x", c) for c in range(NC16)],
                    [("acc", hp, qt, st) for hp in range(8) for qt in range(2)
                     for st in range(2)])
        ctx.x_stream = False
        p_load_x(ctx, "x1d", extra_reads=[("xco", t) for t in range(NC16)])
        phase_alias([("xn_fm", m) for m in range(8)],
                    [("sac", i, hh) for i in range(16) for hh in range(2)])
        p_ln(ctx, 1)
        phase_alias([("w1a", k) for k in range(8)],
                    [("qsb", 0), ("qsb", 1)])
        phase_alias(["rp", "ra", "rb"], [("pt", 0), ("pt", 1)])
        phase_alias([("qkstg", m) for m in range(8)], ["nstg2"])
        phase_alias(["vstg"], ["vslab"])
        phase_alias(["ropec", "ropes"], [("ctx_fm", k) for k in range(8)])
        S.dma("sp", lambda e: e.dma_start(out=T["ropec"],
                                          in_=rearr_tok(ctx.dr["ropec"].ap())),
              writes=["ropec"], tag="ropecL")
        S.dma("sp", lambda e: e.dma_start(out=T["ropes"],
                                          in_=rearr_tok(ctx.dr["ropes"].ap())),
              writes=["ropes"], tag="ropesL")
        ctx.kdst, ctx.vdst, ctx.qdst = "kag_in1", "vag_in1", "qpark1"
        p_l1_qkv(ctx)
        S.collective(lambda e: e.collective_compute(
            "AllGather", ALU.bypass, replica_groups=rg,
            ins=[ctx.dr["kag_in1"].ap().opt()],
            outs=[ctx.dr["kag_out1"].ap().opt()]),
            reads=[("qkout", "kag_in1", t) for t in range(NC16)],
            writes=["agk1"], tag="agk1")
        S.collective(lambda e: e.collective_compute(
            "AllGather", ALU.bypass, replica_groups=rg,
            ins=[ctx.dr["vag_in1"].ap().opt()],
            outs=[ctx.dr["vag_out1"].ap().opt()]),
            reads=[("vout", t) for t in range(NC16)],
            writes=["agv1"], tag="agv1")
        phase_alias([("wqst", 0), ("wqst", 1), ("wqbf", 0), "vstgL1"]
                    + [("wT", p, k) for p in range(2) for k in range(8)],
                    ["vstg"] + [("w1a", k) for k in range(8)]
                    + [("woT", k) for k in range(8)])
        phase_alias([("hfm", k) for k in range(4)] + [("w2g", 0)],
                    [("xnq_fm", k) for k in range(8)]
                    + [("ctx_fm", k) for k in range(8)])
        T["wqst"] = T["wqstL1"]
        T["wqbf"] = T["wqbfL1"]
        T["wT"] = T["wTl1"]
        T["vstg"] = T["vstgL1"]
        p_l1_ffn(ctx)
        S.dma("act", lambda e: e.dma_start(
            out=rearr_tok(ctx.dr["xf1d"].ap()), in_=T["x"][:]),
            reads=[("x", c) for c in range(NC16)], writes=["xf1d"], tag="xspill1")
        phase_alias([("acc", hp, qt, st) for hp in range(8) for qt in range(2)
                     for st in range(2)],
                    [("x", c) for c in range(NC16)] + ["xf1d"])
        phase_alias([("sac", i, hh) for i in range(16) for hh in range(2)],
                    [("xn_fm", m) for m in range(8)])
        phase_alias([("kslab", 0), ("kslab", 1)],
                    [("hfm", k) for k in range(4)] + [("w2g", 0), ("w2g", 1),
                                                      "ropec", "ropes"])
        phase_alias(["vslab"],
                    [("wqst", 0), ("wqst", 1), ("wqbf", 0), "vstgL1",
                     ("w2stg", 0), ("w2stg", 1)]
                    + [("wT", p, k) for p in range(2) for k in range(8)])
        phase_alias([("qsb", 0), ("qsb", 1)],
                    [("w1a", k) for k in range(8)] + ["qy", "qg3"])
        phase_alias([("pt", 0), ("pt", 1)], ["rp", "ra", "rb", "qlo"])
        phase_alias(["nstg2"], [("qkstg", m) for m in range(8)])
        ctx.ksrc, ctx.vsrc, ctx.qsrc = "kag_out1", "vag_out1", "qpark1"
        ctx.k_reads = ["agk1"]
        ctx.v_reads = ["agv1"]
        ctx.q_reads = [("qkout", "qpark1", t) for t in range(NC16)]
        ctx.xsrc_reads = ["xf1d"]
        ctx.xsrc, ctx.xdst = "xf1d", "xout"
        ctx.x_stream = True
        p_attention(ctx, 1)
        p_outproj(ctx, 1)
        S.barrier_step()
        S.emit(stack)
    return nc


def kernel_fused(**inputs):
    from concourse.bass_utils import run_bass_kernel_spmd
    ident = np.ascontiguousarray(np.eye(128, dtype=np.float32)
                                 .astype(ml_dtypes.bfloat16))
    masks = host_masks()
    x0 = np.asarray(inputs["L0_attn_x"])
    B, Tt, _ = x0.shape
    cores = list(range(8))
    rows = [own_rows(c % 4) for c in cores]
    sw = [host_sched_table(c % 4) for c in cores]
    scheds = [x[0] for x in sw]
    wtabs = [np.ascontiguousarray(x[1]) for x in sw]
    b1sb = np.ascontiguousarray(
        np.asarray(inputs["L1_ffn_b1"], np.float32).reshape(32, 128).T)
    f32 = np.float32
    nc = build_fused()
    in_maps = []
    for c in cores:
        cos, sin = host_rope(rows[c])
        in_maps.append(dict(
            x0=np.ascontiguousarray(x0[c // 4][rows[c]]),
            L0_attn_w_qkv=np.asarray(inputs["L0_attn_w_qkv"], f32),
            L0_attn_w_out=np.asarray(inputs["L0_attn_w_out"], f32),
            L0_ffn_w_gate=np.asarray(inputs["L0_ffn_w_gate"], f32),
            L0_ffn_w_up=np.asarray(inputs["L0_ffn_w_up"], f32),
            L0_ffn_w_down=np.asarray(inputs["L0_ffn_w_down"], f32),
            L1_attn_Wq=np.asarray(inputs["L1_attn_Wq"]),
            L1_attn_bq=np.asarray(inputs["L1_attn_bq"]),
            L1_attn_Wk=np.asarray(inputs["L1_attn_Wk"]),
            L1_attn_bk=np.asarray(inputs["L1_attn_bk"]),
            L1_attn_Wv=np.asarray(inputs["L1_attn_Wv"]),
            L1_attn_bv=np.asarray(inputs["L1_attn_bv"]),
            L1_attn_Wo=np.asarray(inputs["L1_attn_Wo"]),
            L1_attn_bo=np.asarray(inputs["L1_attn_bo"]),
            L1_ffn_W1=np.asarray(inputs["L1_ffn_W1"], f32),
            b1sb_in=b1sb,
            L1_ffn_W2=np.asarray(inputs["L1_ffn_W2"], f32),
            L1_ffn_b2=np.asarray(inputs["L1_ffn_b2"], f32),
            ident=ident, masks=masks, sched=scheds[c], wtab=wtabs[c],
            ropec=cos, ropes=sin))
    res = run_bass_kernel_spmd(nc, in_maps, core_ids=cores)
    out = np.zeros((B, Tt, D), dtype=np.float32)
    for c in cores:
        out[c // 4][rows[c]] = res.results[c]["xout"]
    return out


kernel = kernel_fused


# version 03 (final)
# speedup vs baseline: 1.3213x; 1.3213x over previous
import contextlib
import numpy as np
import ml_dtypes
import concourse.bass as bass
import concourse.mybir as mybir
from concourse.expressions import RuntimeValue

F32 = mybir.dt.float32
F16 = mybir.dt.float16
BF16 = mybir.dt.bfloat16
I32 = mybir.dt.int32
AF = mybir.ActivationFunctionType
ALU = mybir.AluOpType
AX = mybir.AxisListType

D = 1024
NH = 16
HD = 64
TOWN = 2048
NC16 = TOWN // 128
DFF0 = 2816
DFF1 = 4096
NROUND = 9
SCHED_COLS = 64
ACC_COLS = 2 * NH * 512
ACC_STRIDE = 128 * ACC_COLS
MAGIC = 6291456.0
IEXP = int(np.uint32(0xFFC00000).astype(np.int32))

ENGS = ("sp", "act", "dve", "pool", "pe")
ENG_ATTR = {"sp": "sync", "act": "scalar", "dve": "vector",
            "pool": "gpsimd", "pe": "tensor"}


class Step:
    __slots__ = ("engine", "fn", "reads", "writes", "milestone", "dma_stream",
                 "idx", "waits", "tag")

    def __init__(self, engine, fn, reads, writes, milestone, dma_stream, tag):
        self.engine = engine
        self.fn = fn
        self.reads = reads
        self.writes = writes
        self.milestone = milestone
        self.dma_stream = dma_stream
        self.tag = tag
        self.idx = None
        self.waits = {}


class Sched:
    def __init__(self, nc):
        self.nc = nc
        self.steps = []
        self.count = {e: 0 for e in ENGS}
        self.dcount = {}
        self.buf = {}
        self._pe_pending = []

    def _wait(self, step, chain, val):
        if val is None or val <= 0:
            return
        if step.waits.get(chain, 0) < val:
            step.waits[chain] = val

    def op(self, engine, fn, reads=(), writes=(), milestone=True, tag=""):
        return self._add(engine, fn, reads, writes, milestone, None, tag)

    def dma(self, engine, fn, reads=(), writes=(), stream=None, tag=""):
        assert engine in ("sp", "act", "pool")
        if stream is None:
            stream = list(writes)[0]
        return self._add(engine, fn, reads, writes, True, stream, tag)

    def collective(self, fn, reads=(), writes=(), tag=""):
        return self._add("pool", fn, reads, writes, True, ("cc", tag), tag)

    def alias(self, new_id, old_ids):
        readers = []
        for o in old_ids:
            st = self.buf.get(o)
            if st:
                if st["writer"] and st["writer"][1] is not None:
                    readers.append(st["writer"])
                readers.extend([x for x in st["readers"] if x[1] is not None])
        cur = self.buf.setdefault(new_id, {"writer": None, "readers": []})
        cur["readers"] = cur["readers"] + readers

    def _add(self, engine, fn, reads, writes, milestone, dma_stream, tag):
        step = Step(engine, fn, list(reads), list(writes), milestone,
                    dma_stream, tag)
        self.steps.append(step)
        is_dma = dma_stream is not None
        chain = ("d", dma_stream) if is_dma else engine

        for b in step.reads:
            st = self.buf.get(b)
            if st and st["writer"]:
                wch, wval = st["writer"]
                if wval is None:
                    if not (engine == "pe" and wch == "pe"):
                        raise RuntimeError(f"read {b} over pending PE ({tag})")
                else:
                    self._wait(step, wch, wval)
        for b in step.writes:
            st = self.buf.get(b)
            if st:
                if st["writer"]:
                    wch, wval = st["writer"]
                    if wval is None:
                        if not (engine == "pe" and wch == "pe"):
                            raise RuntimeError(f"write {b} over pending PE ({tag})")
                    else:
                        self._wait(step, wch, wval)
                for (rch, rval) in st["readers"]:
                    if rval is None:
                        if not (engine == "pe" and rch == "pe"):
                            raise RuntimeError(f"write {b} raced by PE read ({tag})")
                    else:
                        self._wait(step, rch, rval)

        if is_dma:
            val = self.dcount.get(dma_stream, 0) + (1 if dma_stream[0] == "cc" else 16)
            self.dcount[dma_stream] = val
            step.idx = val
            for b in step.writes:
                self.buf[b] = {"writer": (chain, val), "readers": []}
            for b in step.reads:
                st = self.buf.setdefault(b, {"writer": None, "readers": []})
                st["readers"].append((chain, val))
        elif engine == "pe" and not milestone:
            for b in step.writes:
                self.buf[b] = {"writer": ("pe", None), "readers": []}
                self._pe_pending.append(b)
            for b in step.reads:
                st = self.buf.setdefault(b, {"writer": None, "readers": []})
                st["readers"].append(("pe", None))
                self._pe_pending.append(b)
        else:
            self.count[engine] += 1
            val = self.count[engine]
            step.idx = val
            if engine == "pe":
                for b in self._pe_pending:
                    st = self.buf.get(b)
                    if st is None:
                        continue
                    if st["writer"] and st["writer"][0] == "pe" and st["writer"][1] is None:
                        st["writer"] = ("pe", val)
                    st["readers"] = [
                        (c, val if (v is None and c == "pe") else v)
                        for (c, v) in st["readers"]]
                self._pe_pending = []
            for b in step.writes:
                self.buf[b] = {"writer": (chain, val), "readers": []}
            for b in step.reads:
                st = self.buf.setdefault(b, {"writer": None, "readers": []})
                st["readers"].append((chain, val))
        return step

    def barrier_step(self, engine="sp", tag="final"):
        step = Step(engine, lambda eng: eng.nop(), [], [], True, None, tag)
        for e in ENGS:
            if self.count[e] and e != engine:
                step.waits[e] = self.count[e]
        for s, v in self.dcount.items():
            step.waits[("d", s)] = v
        self.steps.append(step)
        self.count[engine] += 1
        step.idx = self.count[engine]
        return step

    def emit(self, stack):
        nc = self.nc
        sems = {}
        for e in ENGS:
            if self.count[e]:
                sems[e] = stack.enter_context(nc.semaphore(f"ch_{e}"))
        for i, s in enumerate(sorted(self.dcount, key=str)):
            sems[("d", s)] = stack.enter_context(nc.semaphore(f"ds{i}"))
        self.sems = sems

        def run_engine(ename, eng):
            waited = {}
            for step in self.steps:
                if step.engine != ename:
                    continue
                for chain in sorted(step.waits, key=str):
                    val = step.waits[chain]
                    if waited.get(chain, 0) >= val:
                        continue
                    eng.wait_ge(sems[chain], val)
                    waited[chain] = val
                inst = step.fn(eng)
                if step.dma_stream is not None:
                    if step.dma_stream[0] == "cc":
                        inst.then_inc(sems[("d", step.dma_stream)])
                    else:
                        inst.then_inc(sems[("d", step.dma_stream)], 16)
                elif step.milestone:
                    inst.then_inc(sems[ename], 1)

        block = stack.enter_context(nc.Block())
        for ename, attr in ENG_ATTR.items():
            if not any(s.engine == ename for s in self.steps):
                continue

            def mk(ename):
                def body(eng):
                    run_engine(ename, eng)
                body.__name__ = f"prog_{ename}"
                return body
            getattr(block, attr)(mk(ename))


class Ctx:
    pass


def sb(ctx, name, shape, dtype):
    t = ctx.stack.enter_context(ctx.nc.sbuf_tensor("sb_" + name, shape, dtype))
    ctx.tiles[name] = t
    return t


def rearr_tok(dram_ap):
    return dram_ap.rearrange("(c p) n -> p c n", p=128)


def rearr_fm(dram_ap):
    return dram_ap.rearrange("(c p) n -> p c n", p=128)


def reg_off(eng, ctx, col, maxv, name):
    if not hasattr(ctx, "regcache"):
        ctx.regcache = {}
    key = eng.engine
    reg = ctx.regcache.get(key)
    if reg is None:
        reg = eng.alloc_register(f"roff_{key}")
        ctx.regcache[key] = reg
    eng.reg_load(reg, ctx.tiles["sched"][0:1, col:col + 1])
    return RuntimeValue(reg, min_val=0, max_val=maxv)


def emit_fp4_quant(ctx, eng, out_ap, x_ap, n, rid, wid, tag):
    S = ctx.S
    nb = n // 16
    T = dict(ctx.tiles)
    y = T["qy"][:, 0:n]
    t2 = T["qt2"][:, 0:n]
    g = T["qg"][:, 0:n]
    bm = T["qbm"][:, 0:nb]
    s = T["qs"][:, 0:nb]
    si = T["qsi"][:, 0:nb]
    ic = T["iconst"]
    x3 = x_ap.rearrange("p (b k) -> p b k", k=16)
    y3 = y.rearrange("p (b k) -> p b k", k=16)
    s_b = s.rearrange("p b -> p b ()").broadcast_to([128, nb, 16])
    si_b = si.rearrange("p b -> p b ()").broadcast_to([128, nb, 16])

    S.op("dve", lambda e: e.tensor_reduce(bm, x3, AX.X, ALU.max,
                                          apply_absolute_value=True),
         reads=[rid], writes=["qbm"], tag=tag + "bm")
    S.op(eng, lambda e: e.tensor_scalar(s, bm, 1.0 / 6.0, 1e-12, ALU.mult, ALU.max),
         reads=["qbm"], writes=["qsc"], tag=tag + "s")
    S.op("dve", lambda e: e.reciprocal(si, s), reads=["qsc"], writes=["qsi"],
         tag=tag + "ri")
    S.op(eng, lambda e: e.tensor_tensor(y3, x3, si_b, ALU.mult),
         reads=[rid, "qsi"], writes=["qy"], tag=tag + "y")
    ql = T["ql"][:, 0:n]
    S.op(eng, lambda e: e.tensor_scalar(ql, y, MAGIC, -MAGIC, ALU.add, ALU.add),
         reads=["qy"], writes=["qlo"], tag=tag + "lo")
    rnd_b = ic[:, 2:3].rearrange("p x -> p x").broadcast_to([128, n]).bitcast(I32)
    exp_b = ic[:, 0:1].rearrange("p x -> p x").broadcast_to([128, n]).bitcast(I32)
    S.op(eng, lambda e: e.tensor_tensor(t2.bitcast(I32), y.bitcast(I32),
                                        rnd_b, ALU.add),
         reads=["qy", "iconst"], writes=["qhi"], tag=tag + "rn")
    S.op(eng, lambda e: e.tensor_tensor(t2.bitcast(I32), t2.bitcast(I32),
                                        exp_b, ALU.bitwise_and),
         reads=["qhi", "iconst"], writes=["qhi"], tag=tag + "rm")
    abs_b = ic[:, 3:4].rearrange("p x -> p x").broadcast_to([128, n]).bitcast(I32)
    S.op(eng, lambda e: e.tensor_tensor(g.bitcast(I32), y.bitcast(I32),
                                        abs_b, ALU.bitwise_and),
         reads=["qy", "iconst"], writes=["qg3"], tag=tag + "ga")
    S.op(eng, lambda e: e.tensor_single_scalar(g, g, 1.0, ALU.is_gt),
         reads=["qg3"], writes=["qg3"], tag=tag + "g")
    if eng == "dve":
        S.op(eng, lambda e: e.copy_predicated(ql.bitcast(I32), g.bitcast(I32),
                                              t2.bitcast(I32)),
             reads=["qg3", "qhi", "qlo"], writes=["qlo"], tag=tag + "sel")
    else:
        S.op(eng, lambda e: e.tensor_tensor(t2, t2, ql, ALU.subtract),
             reads=["qhi", "qlo"], writes=["qhi"], tag=tag + "sd")
        S.op(eng, lambda e: e.tensor_tensor(t2, t2, g, ALU.mult),
             reads=["qhi", "qg3"], writes=["qhi"], tag=tag + "sm")
        S.op(eng, lambda e: e.tensor_tensor(ql, ql, t2, ALU.add),
             reads=["qhi", "qlo"], writes=["qlo"], tag=tag + "sa")
    q3 = ql.rearrange("p (b k) -> p b k", k=16)
    o3 = out_ap.rearrange("p (b k) -> p b k", k=16)
    S.op(eng, lambda e: e.tensor_tensor(o3, q3, s_b, ALU.mult),
         reads=["qlo", "qsc"], writes=[wid], tag=tag + "out")


def transpose_tile(ctx, src_ap, dst_ap, rids, wid, tag):
    S = ctx.S
    i = ctx.tp_rot[0] % 2
    ctx.tp_rot[0] += 1
    pt = ctx.ps_t[i]
    S.op("pe", lambda e: e.transpose(pt, src_ap, ctx.tiles["ident"][:]),
         reads=list(rids) + ["ident"], writes=[("ps2", i)], tag=tag + "t")
    S.op("dve", lambda e: e.tensor_copy(dst_ap, pt),
         reads=[("ps2", i)], writes=[wid], tag=tag + "e")


def p_consts(ctx):
    S = ctx.S
    T = dict(ctx.tiles)
    dr = ctx.dr
    S.dma("sp", lambda e: e.dma_start(out=T["ident"][:], in_=dr["ident"].ap()),
          writes=["ident"], tag="ident")
    S.op("dve", lambda e: e.memset(T["iconst"][:, 0:1], IEXP), writes=["iconst"])
    S.op("dve", lambda e: e.memset(T["iconst"][:, 1:2],
                                   int(np.uint32(0x80000000).astype(np.int32))),
         writes=["iconst"])
    S.op("dve", lambda e: e.memset(T["iconst"][:, 2:3], 0x00200000),
         writes=["iconst"])
    S.op("dve", lambda e: e.memset(T["iconst"][:, 3:4], 0x7FFFFFFF),
         writes=["iconst"])
    S.op("dve", lambda e: e.memset(T["onesr"][0:1, :], 1.0), writes=["onesr"])
    if "masks" in dr:
        S.dma("sp", lambda e: e.dma_start(out=T["masks"][:], in_=dr["masks"].ap()
                                          .rearrange("p (a b) -> p a b", a=4)),
              writes=["masks"], tag="masks")
    if "sched" in dr:
        S.dma("sp", lambda e: e.dma_start(out=T["sched"][:], in_=dr["sched"].ap()),
              writes=["sched"], tag="sched")
    if "wtab" in dr:
        S.dma("sp", lambda e: e.dma_start(out=T["wtab"][:], in_=dr["wtab"].ap()),
              writes=["wtab"], tag="wtab")
    if "ropec" in dr and not getattr(ctx, "defer_rope", False):
        S.dma("sp", lambda e: e.dma_start(out=T["ropec"][:],
                                          in_=rearr_tok(dr["ropec"].ap())),
              writes=["ropec"], tag="ropec")
        S.dma("sp", lambda e: e.dma_start(out=T["ropes"][:],
                                          in_=rearr_tok(dr["ropes"].ap())),
              writes=["ropes"], tag="ropes")


def p_load_x(ctx, src_name, extra_reads=()):
    ctx.S.dma("sp", lambda e: e.dma_start(
        out=ctx.tiles["x"][:], in_=rearr_tok(ctx.dr[src_name].ap())),
        reads=list(extra_reads),
        writes=[("x", c) for c in range(NC16)], stream="xin", tag="xload")


def p_store_x(ctx, dst_name):
    ctx.S.dma("act", lambda e: e.dma_start(
        out=rearr_tok(ctx.dr[dst_name].ap()), in_=ctx.tiles["x"][:]),
        reads=[("x", c) for c in range(NC16)], writes=["xout"], tag="xstore")


def p_ln(ctx, layer):
    S = ctx.S
    T = dict(ctx.tiles)
    st = T["lnst"]
    S.op("dve", lambda e: e.memset(T["lneps"][:], 1e-5), writes=["lneps"],
         tag="lneps")
    for c in range(NC16):
        xc = T["x"][:, c, :]
        xid = ("x", c)
        S.op("dve", lambda e, xc=xc: e.tensor_reduce(st[:, 0:1], xc, AX.X, ALU.add),
             reads=[xid], writes=["ln_s1"], tag=f"ln{c}s1")
        S.op("dve", lambda e, xc=xc: e.tensor_tensor(T["lnsq"][:], xc, xc, ALU.mult),
             reads=[xid], writes=["lnsq"], tag=f"ln{c}sq")
        S.op("dve", lambda e: e.tensor_reduce(st[:, 1:2], T["lnsq"][:], AX.X,
                                              ALU.add),
             reads=["lnsq"], writes=["ln_s2"], tag=f"ln{c}s2")
        S.op("dve", lambda e: e.tensor_single_scalar(st[:, 2:3], st[:, 0:1],
                                                     1.0 / D, ALU.mult),
             reads=["ln_s1"], writes=["ln_m"], tag=f"ln{c}m")
        S.op("dve", lambda e: e.tensor_tensor(st[:, 3:4], st[:, 2:3], st[:, 2:3],
                                              ALU.mult),
             reads=["ln_m"], writes=["ln_m2"], tag=f"ln{c}m2")
        S.op("dve", lambda e: e.scalar_tensor_tensor(st[:, 4:5], st[:, 1:2],
                                                     1.0 / D, st[:, 3:4],
                                                     ALU.mult, ALU.subtract),
             reads=["ln_s2", "ln_m2"], writes=["ln_v"], tag=f"ln{c}v")
        S.op("act", lambda e: e.activation(st[:, 5:6], st[:, 4:5], AF.Sqrt,
                                           bias=T["lneps"][:], scale=1.0),
             reads=["ln_v", "lneps"], writes=["ln_sd"], tag=f"ln{c}sd")
        S.op("dve", lambda e: e.reciprocal(st[:, 6:7], st[:, 5:6]),
             reads=["ln_sd"], writes=["ln_r"], tag=f"ln{c}r")
        S.op("dve", lambda e, xc=xc: e.tensor_scalar(T["xn32"][:], xc, st[:, 2:3],
                                                     st[:, 6:7], ALU.subtract,
                                                     ALU.mult),
             reads=[xid, "ln_m", "ln_r"], writes=["xn32"], tag=f"ln{c}n")
        S.op("dve", lambda e: e.tensor_copy(T["xnbf"][:], T["xn32"][:]),
             reads=["xn32"], writes=["xnbf"], tag=f"ln{c}b")
        for m in range(8):
            transpose_tile(ctx, T["xnbf"][:, 128 * m:128 * (m + 1)],
                           T["xn_fm"][:, m, 128 * c:128 * (c + 1)],
                           ["xnbf"], ("xn_fm", m), f"ln{c}t{m}")
        if layer == 0:
            emit_fp4_quant(ctx, "dve", T["xnqbf"][:], T["xn32"][:], D,
                           "xn32", "xnqbf", f"q{c}")
            for m in range(8):
                transpose_tile(ctx, T["xnqbf"][:, 128 * m:128 * (m + 1)],
                               T["xnq_fm"][:, m, 128 * c:128 * (c + 1)],
                               ["xnqbf"], ("xnq_fm", m), f"lnq{c}t{m}")


def p_l0_qkv(ctx):
    S = ctx.S
    T = dict(ctx.tiles)
    wq = ctx.dr["L0_attn_w_qkv"].ap()
    S.op("dve", lambda e: e.memset(T["vstg"][:], 1.0), writes=["vstg"], tag="vones")
    for m in range(16):
        par = m % 2
        wst = ("wst", par)
        S.dma("pool", lambda e, m=m, par=par: e.dma_start(
            out=T["wstage"][:, par, :], in_=wq[128 * m:128 * (m + 1), :]),
            writes=[wst], stream=("wsts", par), tag=f"qkvw{m}")
        for k in range(8):
            transpose_tile(ctx, T["wstage"][:, par, 128 * k:128 * (k + 1)],
                           T["wT"][:, par, k, :], [wst], ("wT", par, k),
                           f"qkvt{m}_{k}")
        for t in range(4):
            pid = ("psc", t)
            for k in range(8):
                def mfn(e, m=m, k=k, t=t, par=par):
                    return e.matmul(ctx.psc[t][:],
                                    T["wT"][:, par, k, :],
                                    T["xn_fm"][:, k, 512 * t:512 * (t + 1)],
                                    start=(k == 0), stop=(k == 7))
                S.op("pe", mfn, reads=[("wT", par, k), ("xn_fm", k)],
                     writes=[pid], milestone=(k == 7), tag=f"qkvm{m}_{t}_{k}")
            def efn(e, t=t):
                return e.tensor_copy(T["kstg"][:, 512 * t:512 * (t + 1)],
                                     ctx.psc[t][:])
            S.op("dve", efn, reads=[pid], writes=["kstg"], tag=f"qkve{m}_{t}")
        dst = ctx.qdst if m < 8 else ctx.kdst
        mk = m % 8
        S.dma("act", lambda e, mk=mk, dst=dst: e.dma_start(
            out=rearr_fm(ctx.dr[dst].ap())[:, mk, :], in_=T["kstg"][:]),
            reads=["kstg"], writes=[("kout", dst, mk)], stream="kout",
            tag=f"kst{m}")
    for m in range(8):
        par = m % 2
        wst = ("wst", par)
        S.dma("pool", lambda e, m=m, par=par: e.dma_start(
            out=T["wstage"][:, par, :],
            in_=wq[2048 + 128 * m:2048 + 128 * (m + 1), :]),
            writes=[wst], stream=("wsts", par), tag=f"vw{m}")
        for k in range(8):
            transpose_tile(ctx, T["wstage"][:, par, 128 * k:128 * (k + 1)],
                           T["wvT"][:, k, 128 * m:128 * (m + 1)],
                           [wst], ("wvT", k), f"vt{m}_{k}")
    p_v_common(ctx, lambda t, n, k: (T["xn_fm"][:, k, 128 * t:128 * (t + 1)],
                                     [("xn_fm", k), ("wvT", k)],
                                     T["wvT"][:, k, 512 * n:512 * (n + 1)]),
               nbias=False)


def p_v_common(ctx, operands, nbias):
    S = ctx.S
    T = dict(ctx.tiles)
    for t in range(NC16):
        for n in range(2):
            pid = ("psc", n)
            for k in range(8):
                lhs, rids, rhs = operands(t, n, k)
                def mfn(e, lhs=lhs, rhs=rhs, k=k, n=n):
                    return e.matmul(ctx.psc[n][:], lhs, rhs,
                                    start=(k == 0), stop=(k == 7 and not nbias))
                S.op("pe", mfn, reads=rids, writes=[pid],
                     milestone=(k == 7 and not nbias), tag=f"vm{t}_{n}_{k}")
            if nbias:
                def bfn(e, t=t, n=n):
                    return e.matmul(ctx.psc[n][:], T["onesr"][0:1, 0:128],
                                    T["brow"][0:1, 512 * n:512 * (n + 1)],
                                    start=False, stop=True)
                S.op("pe", bfn, reads=["onesr", "brow"], writes=[pid],
                     milestone=True, tag=f"vb{t}_{n}")
            for par2 in range(2):
                def efn(e, t=t, n=n, par2=par2):
                    dst = T["vstg"][:].rearrange("p (h w) -> p h w", w=128)
                    src = ctx.psc[n][:].rearrange("p (h w) -> p h w", w=64)
                    return e.tensor_copy(
                        dst[:, 8 * n + par2:8 * (n + 1):2,
                            64 * par2:64 * par2 + 64],
                        src[:, par2::2, :])
                S.op("dve", efn, reads=[pid, "vstg"], writes=[("vstg2", n)],
                     tag=f"ve{t}_{n}_{par2}")
        vdst_ap = rearr_tok(ctx.dr[ctx.vdst].ap())
        S.dma("act", lambda e, t=t, vdst_ap=vdst_ap: e.dma_start(
            out=vdst_ap[:, t, :], in_=T["vstg"][:]),
            reads=[("vstg2", 0), ("vstg2", 1), "vstg"], writes=[("vout", t)],
            stream="vout", tag=f"vst{t}")
        ctx.S.alias(("vstg2", 0), [("vout", t)])
        ctx.S.alias(("vstg2", 1), [("vout", t)])


def p_prep_l0ffn_weights(ctx):
    S = ctx.S
    T = dict(ctx.tiles)
    for wname, dst in (("L0_ffn_w_gate", "wgqT"), ("L0_ffn_w_up", "wuqT")):
        w = ctx.dr[wname].ap()
        for r in range(22):
            par = r % 2
            S.dma("sp", lambda e, w=w, r=r, par=par: e.dma_start(
                out=T["wqst"][:, par, 0:D], in_=w[128 * r:128 * (r + 1), :]),
                writes=[("wqst", par)], stream=("wqs", par), tag=f"pw{dst}{r}")
            emit_fp4_quant(ctx, "dve", T["wqbf"][:, par, 0:D],
                           T["wqst"][:, par, 0:D], D,
                           ("wqst", par), ("wqbf", par), f"pw{dst}{r}_")
            for c in range(8):
                transpose_tile(ctx, T["wqbf"][:, par, 128 * c:128 * (c + 1)],
                               T["wqT"][:, par, c, 0:128],
                               [("wqbf", par)], ("wqT", par, c), f"pwt{dst}{r}_{c}")
            S.dma("act", lambda e, dst=dst, r=r, par=par: e.dma_start(
                out=rearr_fm(ctx.dr[dst].ap())[:, :, 128 * r:128 * (r + 1)],
                in_=T["wqT"][:, par, 0:8, 0:128]),
                reads=[("wqT", par, c) for c in range(8)],
                writes=[("wgu", dst, r)], stream=("wqTo", par), tag=f"pws{dst}{r}")
    w = ctx.dr["L0_ffn_w_down"].ap()
    wd_splits = [(0, 768), (768, 640), (1408, 768), (2176, 640)]
    for r in range(8):
        for hf, (f0, fwd) in enumerate(wd_splits):
            par = hf % 2
            nch = fwd // 128
            S.dma("sp", lambda e, w=w, r=r, f0=f0, fwd=fwd, par=par: e.dma_start(
                out=T["wqst"][:, par, 0:fwd],
                in_=w[128 * r:128 * (r + 1), f0:f0 + fwd]),
                writes=[("wqst", par)], stream=("wqs", par), tag=f"pwd{r}_{hf}")
            emit_fp4_quant(ctx, "dve", T["wqbf"][:, par, 0:fwd],
                           T["wqst"][:, par, 0:fwd], fwd,
                           ("wqst", par), ("wqbf", par), f"pwd{r}_{hf}_")
            for c in range(nch):
                transpose_tile(ctx, T["wqbf"][:, par, 128 * c:128 * (c + 1)],
                               T["wqT"][:, par, c, 0:128],
                               [("wqbf", par)], ("wqT", par, c), f"pwdt{r}_{hf}_{c}")
            S.dma("act", lambda e, r=r, f0=f0, nch=nch, par=par: e.dma_start(
                out=ctx.dr["wdqT"].ap().rearrange("(c p) n -> p c n", p=128)
                [:, f0 // 128:f0 // 128 + nch, 128 * r:128 * (r + 1)],
                in_=T["wqT"][:, par, 0:nch, 0:128]),
                reads=[("wqT", par, c) for c in range(nch)],
                writes=[("wd_", r, hf)], stream=("wqTo", par), tag=f"pwds{r}_{hf}")


def p_prep_wout(ctx):
    S = ctx.S
    T = dict(ctx.tiles)
    w = ctx.dr["L0_attn_w_out"].ap()
    for r in range(8):
        par = r % 2
        S.dma("sp", lambda e, w=w, r=r, par=par: e.dma_start(
            out=T["wqst"][:, par, 0:D], in_=w[128 * r:128 * (r + 1), :]),
            writes=[("wqst", par)], stream=("wqs", par), tag=f"pwo{r}")
        S.op("dve", lambda e, par=par: e.tensor_copy(
            T["wqbf"][:, par, 0:D], T["wqst"][:, par, 0:D]),
            reads=[("wqst", par)], writes=[("wqbf", par)], tag=f"pwoc{r}")
        for c in range(8):
            transpose_tile(ctx, T["wqbf"][:, par, 128 * c:128 * (c + 1)],
                           T["wqT"][:, par, c, 0:128],
                           [("wqbf", par)], ("wqT", par, c), f"pwot{r}_{c}")
        S.dma("act", lambda e, r=r, par=par: e.dma_start(
            out=rearr_fm(ctx.dr[ctx.woT_dram].ap())[:, :, 128 * r:128 * (r + 1)],
            in_=T["wqT"][:, par, 0:8, 0:128]),
            reads=[("wqT", par, c) for c in range(8)],
            writes=[("wouT", r)], stream=("wqTo", par), tag=f"pwos{r}")


def p_l0_ffn(ctx):
    S = ctx.S
    T = dict(ctx.tiles)
    fgs = [(0, 512), (512, 512), (1024, 512), (1536, 512), (2048, 512),
           (2560, 256)]
    for gi, (f0, fw) in enumerate(fgs):
        par = gi % 2
        nks = fw // 128
        for wname, buf in (("wgqT", "wg"), ("wuqT", "wu")):
            S.dma("sp", lambda e, wname=wname, buf=buf, f0=f0, fw=fw, par=par:
                  e.dma_start(
                      out=T[buf][:, par, :, 0:fw],
                      in_=rearr_fm(ctx.dr[wname].ap())[:, :, f0:f0 + fw]),
                  reads=[("wgu", wname, rr)
                         for rr in range(f0 // 128, (f0 + fw) // 128)],
                  writes=[(buf, par)], stream=(buf, par), tag=f"ffw{gi}{buf}")
        S.dma("sp", lambda e, f0=f0, nks=nks, par=par: e.dma_start(
            out=T["wdn"][:, par, 0:nks, :],
            in_=ctx.dr["wdqT"].ap().rearrange("(c p) n -> p c n", p=128)
            [:, f0 // 128:f0 // 128 + nks, :]),
            reads=[("wd_", r, hf) for r in range(8) for hf in range(4)],
            writes=[("wdn", par)], stream=("wdn", par), tag=f"ffwd{gi}")
        for t in range(NC16):
            for which, buf, pi in (("g", "wg", 2), ("u", "wu", 3)):
                for k in range(8):
                    def mfn(e, k=k, t=t, buf=buf, pi=pi, fw=fw, par=par):
                        return e.matmul(
                            ctx.psc[pi][:, 0:fw],
                            T["xnq_fm"][:, k, 128 * t:128 * (t + 1)],
                            T[buf][:, par, k, 0:fw],
                            start=(k == 0), stop=(k == 7))
                    S.op("pe", mfn, reads=[("xnq_fm", k), (buf, par)],
                         writes=[("psc", pi)], milestone=(k == 7),
                         tag=f"ff{which}{gi}_{t}_{k}")
            S.op("act", lambda e, fw=fw: e.activation(T["gs"][:, 0:fw],
                                                      ctx.psc[2][:, 0:fw],
                                                      AF.Silu),
                 reads=[("psc", 2)], writes=["gs"], tag=f"ffs{gi}_{t}")
            S.op("dve", lambda e, fw=fw: e.tensor_tensor(
                T["h32"][:, 0:fw], T["gs"][:, 0:fw], ctx.psc[3][:, 0:fw],
                ALU.mult),
                reads=["gs", ("psc", 3)], writes=["h32"], tag=f"ffh{gi}_{t}")
            emit_fp4_quant(ctx, "dve", T["hq"][:, 0:fw], T["h32"][:, 0:fw], fw,
                           "h32", "hq", f"fq{gi}_{t}_")
            for ks in range(nks):
                transpose_tile(ctx, T["hq"][:, 128 * ks:128 * (ks + 1)],
                               T["hq_fm"][:, ks, :], ["hq"], ("hq_fm", ks),
                               f"fqt{gi}_{t}_{ks}")
            for n in range(2):
                pi = n
                for ks in range(nks):
                    def mfn(e, ks=ks, n=n, t=t, par=par, pi=pi, nks=nks):
                        return e.matmul(
                            ctx.psc[pi][:],
                            T["hq_fm"][:, ks, :],
                            T["wdn"][:, par, ks, 512 * n:512 * (n + 1)],
                            start=(ks == 0), stop=(ks == nks - 1))
                    S.op("pe", mfn, reads=[("hq_fm", ks), ("wdn", par)],
                         writes=[("psc", pi)], milestone=(ks == nks - 1),
                         tag=f"ffd{gi}_{t}_{n}_{ks}")
                def afn(e, t=t, n=n, pi=pi):
                    xs = T["x"][:, t, 512 * n:512 * (n + 1)]
                    return e.tensor_tensor(xs, xs, ctx.psc[pi][:], ALU.add)
                S.op("dve", afn, reads=[("psc", pi), ("x", t)],
                     writes=[("x", t)], tag=f"ffx{gi}_{t}_{n}")


def p_attention(ctx, layer):
    S = ctx.S
    T = dict(ctx.tiles)
    kfull = ctx.dr[ctx.ksrc].ap()
    vfull = ctx.dr[ctx.vsrc].ap()
    qpark = ctx.dr[ctx.qsrc].ap()
    soff = ctx.sched_off
    accf = None
    sacf = None
    S.op("dve", lambda e: e.memset(T["ctxacc"][:], 0.0),
         writes=[("acc", hp, qt, st) for hp in range(8) for qt in range(2)
                 for st in range(2)], tag=f"am{layer}")
    S.op("dve", lambda e: e.memset(T["sacc"][:], 0.0),
         writes=[("sac", i, hh) for i in range(16) for hh in range(2)],
         tag=f"am2{layer}")

    for r in range(NROUND):
        par = r % 2
        c0, c1, c2, c3, c4 = (soff + 5 * r + i for i in range(5))

        def kfn(e, c0=c0, par=par):
            off = reg_off(e, ctx, c0, 4096 * 2048 - 1024, f"ko{c0}")
            src = bass.AP(kfull.tensor, off,
                          [[2048, 128], [128 * 2048, 8], [1, 1024]])
            return e.dma_start(out=T["kslab"][:, par, :, :], in_=src)
        S.dma("sp", kfn,
              reads=["sched"] + list(getattr(ctx, "k_reads", ())),
              writes=[("kslab", par)],
              stream=("kslab", par), tag=f"a{layer}k{r}")

        def vfn(e, c1=c1):
            off = reg_off(e, ctx, c1, 8192 * 2048 - 2048, f"vo{c1}")
            src = bass.AP(vfull.tensor, off,
                          [[2048, 128], [128 * 2048, 8], [1, 2048]])
            return e.dma_start(out=T["vslab"][:], in_=src)
        S.dma("sp", vfn,
              reads=["sched"] + list(getattr(ctx, "v_reads", ())),
              writes=["vslab"],
              stream="vslab", tag=f"a{layer}v{r}")

        for qt in range(2):
            cq = c2 if qt == 0 else c3
            def qfn(e, cq=cq, qt=qt):
                off = reg_off(e, ctx, cq, 1024 * 2048 - 512, "qo")
                src = bass.AP(qpark.tensor, off,
                              [[2048, 128], [128 * 2048, 8], [1, 512]])
                return e.dma_start(out=T["qsb"][:, qt, :, :], in_=src)
            S.dma("act", qfn,
                  reads=["sched"] + list(getattr(ctx, "q_reads", ())),
                  writes=[("qsb", qt)],
                  stream=("qsb", qt), tag=f"a{layer}q{r}_{qt}")

        diag = r < 2
        accf = T["ctxacc"][:].rearrange("p a b c -> p (a b c)")
        sacf = T["sacc"][:].rearrange("p a b -> p (a b)")
        steps = []
        for hp in range(8):
            for qt in range(2):
                kl = list(range(4 * (qt + 1) if diag else 8))
                pcb = 2 * ((2 * hp + qt) % 2)
                for kb in kl:
                    steps.append((hp, qt, kb, kb == kl[0], kb == kl[-1], pcb))

        def score_pair(i):
            hp, qt, kb, first, last, pcb = steps[i]
            sp2 = i % 2
            pid = ("ps2", sp2)
            for hh in range(2):
                h = 2 * hp + hh
                def sfn(e, h=h, kb=kb, qt=qt, par=par, sp2=sp2, hh=hh):
                    lhs = T["kslab"][64 * (h % 2):64 * (h % 2) + 64,
                                     par, h // 2,
                                     128 * kb:128 * (kb + 1)]
                    rhs = T["qsb"][64 * (h % 2):64 * (h % 2) + 64,
                                   qt, h // 2, :]
                    return e.matmul(ctx.ps2[sp2][:, hh, :], lhs, rhs,
                                    start=True, stop=True)
                S.op("pe", sfn, reads=[("kslab", par), ("qsb", qt)],
                     writes=[pid], milestone=True,
                     tag=f"a{layer}s{r}_{hp}_{qt}_{kb}_{hh}")

        score_pair(0)
        for i, (hp, qt, kb, first, last, pcb) in enumerate(steps):
            sp2 = i % 2
            pid = ("ps2", sp2)
            def efn(e, sp2=sp2):
                return e.activation(
                    T["ptile"][:, sp2, :],
                    ctx.ps2[sp2][:].rearrange("p a b -> p (a b)"),
                    AF.Exp, scale=0.125)
            S.op("act", efn, reads=[pid],
                 writes=[("pt", sp2)], tag=f"a{layer}e{r}_{hp}_{qt}_{kb}")
            if i + 1 < len(steps):
                score_pair(i + 1)
            if diag and kb >= 4 * qt:
                dm = kb - 4 * qt
                def mfn(e, sp2=sp2, dm=dm):
                    pt = T["ptile"][:, sp2, :].rearrange(
                        "p (a b) -> p a b", a=2)
                    mk = T["masks"][:, dm, :].rearrange(
                        "p b -> p () b").broadcast_to([128, 2, 512])
                    return e.tensor_tensor(pt, pt, mk, ALU.mult)
                S.op("pool", mfn, reads=[("pt", sp2), "masks"],
                     writes=[("pt", sp2)],
                     tag=f"a{layer}m{r}_{hp}_{qt}_{kb}")
            for hh in range(2):
                h = 2 * hp + hh
                def pfn(e, h=h, kb=kb, hh=hh, sp2=sp2, pcb=pcb,
                        first=first, last=last):
                    lhs = T["vslab"][:, kb, 128 * h:128 * (h + 1)]
                    rhs = T["ptile"][:, sp2, 512 * hh:512 * (hh + 1)]
                    return e.matmul(ctx.psc[pcb + hh][:], lhs, rhs,
                                    start=first, stop=last)
                S.op("pe", pfn, reads=["vslab", ("pt", sp2)],
                     writes=[("psc", pcb + hh)], milestone=last,
                     tag=f"a{layer}p{r}_{hp}_{qt}_{kb}_{hh}")
            if not last:
                continue
            for hh in range(2):
                h = 2 * hp + hh
                for st_t in range(2):
                    cc0 = st_t * 8192 + hp * 1024 + qt * 512
                    pb = 64 * hh
                    def cfn(e, hh=hh, st_t=st_t, r=r, cc0=cc0, pb=pb,
                            pcb=pcb, accf=accf):
                        dst = accf[pb:pb + 64, cc0:cc0 + 512]
                        return e.scalar_tensor_tensor(
                            dst, ctx.psc[pcb + hh][pb:pb + 64, :],
                            T["wtab"][pb:pb + 64,
                                      2 * r + st_t:2 * r + st_t + 1],
                            dst, ALU.mult, ALU.add)
                    S.op("dve", cfn,
                         reads=[("psc", pcb + hh), "wtab",
                                ("acc", hp, qt, st_t)],
                         writes=[("acc", hp, qt, st_t)],
                         tag=f"a{layer}c{r}_{hp}_{qt}_{hh}_{st_t}")
                    sidx = st_t * 16 + qt * 8 + hp
                    rng = sidx % 16
                    b2 = sidx // 16
                    sb0 = (64 + 32 * b2) if hh == 0 else (32 * b2)
                    cs0 = rng * 512
                    def sfn2(e, hh=hh, st_t=st_t, r=r, cs0=cs0, sb0=sb0,
                             pcb=pcb, sacf=sacf):
                        dst = sacf[sb0:sb0 + 32, cs0:cs0 + 512]
                        return e.scalar_tensor_tensor(
                            dst, ctx.psc[pcb + hh][sb0:sb0 + 32, :],
                            T["wtab"][sb0:sb0 + 32,
                                      2 * r + st_t:2 * r + st_t + 1],
                            dst, ALU.mult, ALU.add)
                    S.op("dve", sfn2,
                         reads=[("psc", pcb + hh), "wtab",
                                ("sac", rng, hh)],
                         writes=[("sac", rng, hh)],
                         tag=f"a{layer}z{r}_{hp}_{qt}_{hh}_{st_t}")

    if getattr(ctx, "alias_ctx", False):
        for k in range(8):
            S.alias(("ctx_fm", k), [("kslab", 0), ("kslab", 1)])
            S.alias(("woT", k), ["vslab"])
    accf = T["ctxacc"][:].rearrange("p a b c -> p (a b c)")
    sacf = T["sacc"][:].rearrange("p a b -> p (a b)")
    for stripe in range(2):
        for hp in range(8):
            for qt in range(2):
                for hh in range(2):
                    sidx = stripe * 16 + qt * 8 + hp
                    rng = sidx % 16
                    b2 = sidx // 16
                    sb0 = (64 + 32 * b2) if hh == 0 else (32 * b2)
                    c0 = rng * 512
                    def rfn(e, sb0=sb0, c0=c0, sacf=sacf):
                        sl = sacf[sb0:sb0 + 32, c0:c0 + 512]
                        return e.reciprocal(sl, sl)
                    S.op("dve", rfn, reads=[("sac", rng, hh)],
                         writes=[("sac", rng, hh)],
                         tag=f"nr{layer}_{stripe}_{hp}_{qt}_{hh}")
                    for dd in range(2):
                        def dfn2(e, sb0=sb0, c0=c0, hh=hh, dd=dd, sacf=sacf):
                            sl = sacf[sb0:sb0 + 32, c0:c0 + 512]
                            dst = T["nstg2"][64 * hh + 32 * dd:
                                             64 * hh + 32 * (dd + 1), :]
                            return e.tensor_copy(dst, sl)
                        S.op("dve", dfn2, reads=[("sac", rng, hh)],
                             writes=["nstg2"],
                             tag=f"nd{layer}_{stripe}_{hp}_{qt}_{hh}_{dd}")
                def nfn(e, stripe=stripe, hp=hp, qt=qt, accf=accf):
                    c0 = stripe * 8192 + hp * 1024 + qt * 512
                    src = accf[:, c0:c0 + 512]
                    dst = T["ctx_fm"][:, hp,
                                      1024 * stripe + 512 * qt:
                                      1024 * stripe + 512 * (qt + 1)]
                    return e.tensor_tensor(dst, src, T["nstg2"][:], ALU.mult)
                S.op("dve", nfn, reads=[("acc", hp, qt, stripe), "nstg2"],
                     writes=[("ctx_fm", hp)],
                     tag=f"nm{layer}_{stripe}_{hp}_{qt}")


def p_outproj(ctx, layer):
    S = ctx.S
    T = dict(ctx.tiles)
    if layer == 0 and getattr(ctx, "woT_dram", None):
        woTd = ctx.dr[ctx.woT_dram].ap()
        S.dma("pool", lambda e, woTd=woTd: e.dma_start(
            out=T["woT"][:], in_=rearr_fm(woTd)),
            reads=list(getattr(ctx, "woT_reads", ())),
            writes=[("woT", k) for k in range(8)], stream="wo0", tag="owL0d")
    elif layer == 0:
        w = ctx.dr["L0_attn_w_out"].ap()
        for m in range(8):
            par = m % 2
            S.dma("pool", lambda e, w=w, m=m, par=par: e.dma_start(
                out=T["wstage"][:, par, :], in_=w[128 * m:128 * (m + 1), :]),
                writes=[("wst", par)], stream=("wsts", par), tag=f"ow{m}")
            for k in range(8):
                transpose_tile(ctx, T["wstage"][:, par, 128 * k:128 * (k + 1)],
                               T["woT"][:, k, 128 * m:128 * (m + 1)],
                               [("wst", par)], ("woT", k), f"ot{m}_{k}")
    else:
        S.dma("pool", lambda e: e.dma_start(
            out=T["woT"][:], in_=rearr_fm(ctx.dr["L1_attn_Wo"].ap())),
            writes=[("woT", k) for k in range(8)], stream="wo1", tag="owL1")
        S.dma("pool", lambda e: e.dma_start(
            out=T["brow"][0:1, :], in_=ctx.dr["L1_attn_bo"].ap()[None, :]),
            writes=["brow"], stream="brow", tag="obo")
    stream_x = getattr(ctx, "x_stream", False)
    if stream_x:
        xsrc_ap = rearr_tok(ctx.dr[ctx.xsrc].ap())
        xdst_ap = rearr_tok(ctx.dr[ctx.xdst].ap())
    for t in range(NC16):
        if stream_x:
            par = t % 2
            S.dma("sp", lambda e, t=t, par=par, xsrc_ap=xsrc_ap: e.dma_start(
                out=T["xc"][:, par, :], in_=xsrc_ap[:, t, :]),
                reads=list(getattr(ctx, "xsrc_reads", ())),
                writes=[("xc", par)], stream=("xc", par), tag=f"oxl{t}")
        for n in range(2):
            pid = ("psc", n)
            for k in range(8):
                def mfn(e, t=t, n=n, k=k, layer=layer):
                    return e.matmul(
                        ctx.psc[n][:],
                        T["ctx_fm"][:, k, 128 * t:128 * (t + 1)],
                        T["woT"][:, k, 512 * n:512 * (n + 1)],
                        start=(k == 0), stop=(k == 7 and layer == 0))
                S.op("pe", mfn, reads=[("ctx_fm", k), ("woT", k)],
                     writes=[pid], milestone=(k == 7 and layer == 0),
                     tag=f"o{layer}_{t}_{n}_{k}")
            if layer == 1:
                def bfn(e, n=n):
                    return e.matmul(ctx.psc[n][:], T["onesr"][0:1, 0:128],
                                    T["brow"][0:1, 512 * n:512 * (n + 1)],
                                    start=False, stop=True)
                S.op("pe", bfn, reads=["onesr", "brow"], writes=[pid],
                     milestone=True, tag=f"ob{t}_{n}")
            if stream_x:
                def afn(e, t=t, n=n, par=par):
                    xs = T["xc"][:, par, 512 * n:512 * (n + 1)]
                    return e.tensor_tensor(xs, xs, ctx.psc[n][:], ALU.add)
                S.op("dve", afn, reads=[pid, ("xc", par)], writes=[("xc", par)],
                     tag=f"ox{layer}_{t}_{n}")
            else:
                def afn(e, t=t, n=n):
                    xs = T["x"][:, t, 512 * n:512 * (n + 1)]
                    return e.tensor_tensor(xs, xs, ctx.psc[n][:], ALU.add)
                S.op("dve", afn, reads=[pid, ("x", t)], writes=[("x", t)],
                     tag=f"ox{layer}_{t}_{n}")
        if stream_x:
            S.dma("act", lambda e, t=t, par=par, xdst_ap=xdst_ap: e.dma_start(
                out=xdst_ap[:, t, :], in_=T["xc"][:, par, :]),
                reads=[("xc", par)], writes=[("xco", t)], stream="xco",
                tag=f"oxs{t}")


def p_l1_qkv(ctx):
    S = ctx.S
    T = dict(ctx.tiles)
    for proj, wname, bname in (("q", "L1_attn_Wq", "L1_attn_bq"),
                               ("k", "L1_attn_Wk", "L1_attn_bk")):
        S.dma("pool", lambda e, wname=wname: e.dma_start(
            out=T["w1a"][:], in_=rearr_fm(ctx.dr[wname].ap())),
            writes=[("w1a", k) for k in range(8)], stream="w1a", tag=f"l1w{proj}")
        S.dma("pool", lambda e, bname=bname: e.dma_start(
            out=T["brow"][0:1, :], in_=ctx.dr[bname].ap()[None, :]),
            writes=["brow"], stream="brow", tag=f"l1b{proj}")
        for t in range(NC16):
            for n in range(2):
                pid = ("psc", n)
                for k in range(8):
                    def mfn(e, t=t, n=n, k=k):
                        return e.matmul(
                            ctx.psc[n][:],
                            T["xn_fm"][:, k, 128 * t:128 * (t + 1)],
                            T["w1a"][:, k, 512 * n:512 * (n + 1)],
                            start=(k == 0), stop=False)
                    S.op("pe", mfn, reads=[("xn_fm", k), ("w1a", k)],
                         writes=[pid], milestone=False,
                         tag=f"l1{proj}m{t}_{n}_{k}")
                def bfn(e, n=n):
                    return e.matmul(ctx.psc[n][:], T["onesr"][0:1, 0:128],
                                    T["brow"][0:1, 512 * n:512 * (n + 1)],
                                    start=False, stop=True)
                S.op("pe", bfn, reads=["onesr", "brow"], writes=[pid],
                     milestone=True, tag=f"l1{proj}b{t}_{n}")
                ps3 = ctx.psc[n][:].rearrange("p (h d) -> p h d", d=64)
                cosb = T["ropec"][:, t, :].rearrange("p d -> p () d")\
                    .broadcast_to([128, 8, 32])
                sinb = T["ropes"][:, t, :].rearrange("p d -> p () d")\
                    .broadcast_to([128, 8, 32])
                rp3 = T["rp"][:, 0:512].rearrange("p (h d) -> p h d", d=64)
                ra = T["ra"][:].rearrange("p (h d) -> p h d", d=32)
                rb = T["rb"][:].rearrange("p (h d) -> p h d", d=32)
                S.op("dve", lambda e, ps3=ps3, cosb=cosb: e.tensor_tensor(
                    ra, ps3[:, :, 0:32], cosb, ALU.mult),
                    reads=[pid, "ropec"], writes=["ra"], tag=f"l1{proj}r1{t}_{n}")
                S.op("dve", lambda e, ps3=ps3, sinb=sinb: e.tensor_tensor(
                    rb, ps3[:, :, 32:64], sinb, ALU.mult),
                    reads=[pid, "ropes"], writes=["rb"], tag=f"l1{proj}r2{t}_{n}")
                S.op("dve", lambda e, rp3=rp3: e.tensor_tensor(
                    rp3[:, :, 0:32], ra, rb, ALU.subtract),
                    reads=["ra", "rb"], writes=["rp"], tag=f"l1{proj}r3{t}_{n}")
                S.op("dve", lambda e, ps3=ps3, sinb=sinb: e.tensor_tensor(
                    ra, ps3[:, :, 0:32], sinb, ALU.mult),
                    reads=[pid, "ropes"], writes=["ra"], tag=f"l1{proj}r4{t}_{n}")
                S.op("dve", lambda e, ps3=ps3, cosb=cosb: e.tensor_tensor(
                    rb, ps3[:, :, 32:64], cosb, ALU.mult),
                    reads=[pid, "ropec"], writes=["rb"], tag=f"l1{proj}r5{t}_{n}")
                S.op("dve", lambda e, rp3=rp3: e.tensor_tensor(
                    rp3[:, :, 32:64], ra, rb, ALU.add),
                    reads=["ra", "rb"], writes=["rp"], tag=f"l1{proj}r6{t}_{n}")
                for mm4 in range(4):
                    mglob = 4 * n + mm4
                    transpose_tile(ctx, T["rp"][:, 128 * mm4:128 * (mm4 + 1)],
                                   T["qkstg"][:, mglob, :], ["rp"],
                                   ("qkstg", mglob), f"l1{proj}t{t}_{n}_{mm4}")
            dstname = ctx.qdst if proj == "q" else ctx.kdst
            S.dma("act", lambda e, t=t, dstname=dstname: e.dma_start(
                out=rearr_fm(ctx.dr[dstname].ap())[:, :, 128 * t:128 * (t + 1)],
                in_=T["qkstg"][:]),
                reads=[("qkstg", m) for m in range(8)],
                writes=[("qkout", dstname, t)], stream=("qko", dstname),
                tag=f"l1{proj}s{t}")
    S.dma("pool", lambda e: e.dma_start(
        out=T["w1a"][:], in_=rearr_fm(ctx.dr["L1_attn_Wv"].ap())),
        writes=[("w1a", k) for k in range(8)], stream="w1a", tag="l1wv")
    S.dma("pool", lambda e: e.dma_start(
        out=T["brow"][0:1, :], in_=ctx.dr["L1_attn_bv"].ap()[None, :]),
        writes=["brow"], stream="brow", tag="l1bv")
    S.op("dve", lambda e: e.memset(T["vstg"][:], 1.0), writes=["vstg"],
         tag="vones1")
    p_v_common(ctx, lambda t, n, k: (T["xn_fm"][:, k, 128 * t:128 * (t + 1)],
                                     [("xn_fm", k), ("w1a", k)],
                                     T["w1a"][:, k, 512 * n:512 * (n + 1)]),
               nbias=True)


def p_l1_ffn(ctx):
    S = ctx.S
    T = dict(ctx.tiles)

    for wi, wname in enumerate(("L1_ffn_W1", "L1_ffn_W2")):
        w = ctx.dr[wname].ap()
        loads = ([(r, 0, D) for r in range(32)] if wi == 0
                 else [(r, hf, 1024) for r in range(8) for hf in range(4)])
        S.op("dve", lambda e: e.memset(T["gacc"][:, 0:1], 0.0), writes=["gacc"],
             tag=f"g{wi}z")
        for li, (r, hf, ncol) in enumerate(loads):
            par = li % 2
            S.dma("sp", lambda e, w=w, r=r, hf=hf, ncol=ncol, par=par:
                  e.dma_start(out=T["wqst"][:, par, 0:ncol],
                              in_=w[128 * r:128 * (r + 1),
                                    ncol * hf:ncol * (hf + 1)]),
                  writes=[("wqst", par)], stream=("wqs", par), tag=f"g{wi}l{li}")
            S.op("dve", lambda e, par=par, ncol=ncol: e.tensor_reduce(
                T["gacc"][:, 1:2], T["wqst"][:, par, 0:ncol], AX.X, ALU.add,
                apply_absolute_value=True),
                reads=[("wqst", par)], writes=["gred"], tag=f"g{wi}r{li}")
            S.op("dve", lambda e: e.tensor_tensor(
                T["gacc"][:, 0:1], T["gacc"][:, 0:1], T["gacc"][:, 1:2], ALU.add),
                reads=["gred", "gacc"], writes=["gacc"], tag=f"g{wi}a{li}")
        S.op("pe", lambda e: e.matmul(ctx.psc[0][0:1, 0:1], T["onesc"][:, 0:1],
                                      T["gacc"][:, 0:1], start=True, stop=True),
             reads=["gacc", "onesc"], writes=[("psc", 0)], tag=f"g{wi}mm")
        sz = float(DFF1 * D)
        S.op("dve", lambda e, sz=sz: e.tensor_scalar(
            T["gval"][0:1, 0:1], ctx.psc[0][0:1, 0:1], 1.0 / sz, 1e-5,
            ALU.mult, ALU.add),
            reads=[("psc", 0)], writes=["gval"], tag=f"g{wi}g")
        S.op("dve", lambda e: e.reciprocal(T["gval"][0:1, 0:1],
                                           T["gval"][0:1, 0:1]),
             reads=["gval"], writes=["gval"], tag=f"g{wi}i")
        S.op("pe", lambda e: e.matmul(ctx.psc[0][:, 0:1], T["onesrf"][0:1, 0:128],
                                      T["gval"][0:1, 0:1], start=True, stop=True),
             reads=["gval", "onesrf"], writes=[("psc", 0)], tag=f"g{wi}bb")
        S.op("dve", lambda e, wi=wi: e.tensor_copy(T["ginv"][:, wi:wi + 1],
                                                   ctx.psc[0][:, 0:1]),
             reads=[("psc", 0)], writes=[("ginv", wi)], tag=f"g{wi}e")

    def tern_quant(src_ap, dst_ap, wi, rids, wid, tag, ncol):
        y = T["qy"][:, 0:ncol]
        m = T["qg"][:, 0:ncol]
        S.op("dve", lambda e: e.tensor_single_scalar(
            y, src_ap, T["ginv"][:, wi:wi + 1], ALU.mult),
            reads=list(rids) + [("ginv", wi)], writes=["qy"], tag=tag + "y")
        abs_b2 = T["iconst"][:, 3:4].rearrange("p x -> p x")\
            .broadcast_to([128, ncol]).bitcast(I32)
        S.op("dve", lambda e: e.tensor_tensor(m.bitcast(I32), y.bitcast(I32),
                                              abs_b2, ALU.bitwise_and),
             reads=["qy", "iconst"], writes=["qg3"], tag=tag + "ma")
        S.op("dve", lambda e: e.tensor_single_scalar(m, m, 0.5, ALU.is_ge),
             reads=["qg3"], writes=["qg3"], tag=tag + "m")
        sgn_b = T["iconst"][:, 1:2].rearrange("p x -> p x")\
            .broadcast_to([128, ncol]).bitcast(I32)
        S.op("dve", lambda e: e.tensor_tensor(
            y.bitcast(I32), y.bitcast(I32), sgn_b, ALU.bitwise_and),
            reads=["qy", "iconst"], writes=["qy"], tag=tag + "q1")
        S.op("dve", lambda e: e.tensor_tensor(
            y.bitcast(I32), y.bitcast(I32), m.bitcast(I32), ALU.bitwise_or),
            reads=["qy", "qg3"], writes=["qy"], tag=tag + "q2")
        S.op("dve", lambda e: e.tensor_copy(dst_ap, y),
             reads=["qy"], writes=[wid], tag=tag + "o")

    w2 = ctx.dr["L1_ffn_W2"].ap()
    for r in range(8):
        for hf in range(4):
            par = hf % 2
            S.dma("sp", lambda e, r=r, hf=hf, par=par: e.dma_start(
                out=T["wqst"][:, par, 0:1024],
                in_=w2[128 * r:128 * (r + 1), 1024 * hf:1024 * (hf + 1)]),
                writes=[("wqst", par)], stream=("wqs", par), tag=f"w2l{r}_{hf}")
            tern_quant(T["wqst"][:, par, 0:1024], T["wqbf"][:, 0, 0:1024], 1,
                       [("wqst", par)], ("wqbf", 0), f"w2q{r}_{hf}_", 1024)
            for cc in range(8):
                c = 8 * hf + cc
                transpose_tile(ctx, T["wqbf"][:, 0, 128 * cc:128 * (cc + 1)],
                               T["w2stg"][:, par, :], [("wqbf", 0)],
                               ("w2stg", par), f"w2t{r}_{hf}_{cc}")
                S.dma("act", lambda e, r=r, c=c, par=par: e.dma_start(
                    out=ctx.dr["W2qT"].ap().rearrange("(c p) n -> p c n", p=128)
                    [:, c, 128 * r:128 * (r + 1)],
                    in_=T["w2stg"][:, par, :]),
                    reads=[("w2stg", par)], writes=[("w2p", r, c)],
                    stream=("wqTo", par), tag=f"w2s{r}_{c}")

    w1 = ctx.dr["L1_ffn_W1"].ap()
    for fg in range(8):
        for mc in range(4):
            m = 4 * fg + mc
            par = m % 2
            S.dma("sp", lambda e, m=m, par=par: e.dma_start(
                out=T["wqst"][:, par, 0:D], in_=w1[128 * m:128 * (m + 1), :]),
                writes=[("wqst", par)], stream=("wqs", par), tag=f"w1l{m}")
            tern_quant(T["wqst"][:, par, 0:D], T["wqbf"][:, 0, 0:D], 0,
                       [("wqst", par)], ("wqbf", 0), f"w1q{m}_", D)
            for c in range(8):
                transpose_tile(ctx, T["wqbf"][:, 0, 128 * c:128 * (c + 1)],
                               T["wT"][:, par, c, :], [("wqbf", 0)],
                               ("wT", par, c), f"w1t{m}_{c}")
            for t in range(4):
                pid = ("psc", t)
                for k in range(8):
                    def mfn(e, k=k, t=t, par=par):
                        return e.matmul(ctx.psc[t][:],
                                        T["wT"][:, par, k, :],
                                        T["xn_fm"][:, k, 512 * t:512 * (t + 1)],
                                        start=(k == 0), stop=(k == 7))
                    S.op("pe", mfn, reads=[("wT", par, k), ("xn_fm", k)],
                         writes=[pid], milestone=(k == 7), tag=f"w1m{m}_{t}_{k}")
                def efn(e, mc=mc, m=m, t=t):
                    return e.tensor_scalar(
                        T["hfm"][:, mc, 512 * t:512 * (t + 1)], ctx.psc[t][:],
                        T["b1sb"][:, m:m + 1], 0.0, ALU.add, ALU.max)
                S.op("dve", efn, reads=[pid, "b1sb"], writes=[("hfm", mc)],
                     tag=f"w1e{m}_{t}")
        par = fg % 2
        S.dma("sp", lambda e, fg=fg, par=par: e.dma_start(
            out=T["w2g"][:, 0, :, :],
            in_=ctx.dr["W2qT"].ap().rearrange("(c p) n -> p c n", p=128)
            [:, 4 * fg:4 * (fg + 1), :]),
            reads=[("w2p", r, c) for r in range(8)
                   for c in range(4 * fg, 4 * (fg + 1))],
            writes=[("w2g", par)], stream=("w2g", par), tag=f"w2gl{fg}")
        if fg == 7:
            S.dma("pool", lambda e: e.dma_start(
                out=T["brow"][0:1, :], in_=ctx.dr["L1_ffn_b2"].ap()[None, :]),
                writes=["brow"], stream="brow", tag="b2l")
        for t in range(NC16):
            for n in range(2):
                pid = ("psc", n)
                last = (fg == 7)
                for ks in range(4):
                    def mfn(e, ks=ks, t=t, n=n, par=par, last=last):
                        return e.matmul(
                            ctx.psc[n][:],
                            T["hfm"][:, ks, 128 * t:128 * (t + 1)],
                            T["w2g"][:, 0, ks, 512 * n:512 * (n + 1)],
                            start=(ks == 0), stop=(ks == 3 and not last))
                    S.op("pe", mfn, reads=[("hfm", ks), ("w2g", par)],
                         writes=[pid], milestone=(ks == 3 and not last),
                         tag=f"w2m{fg}_{t}_{n}_{ks}")
                if last:
                    def bfn(e, n=n):
                        return e.matmul(ctx.psc[n][:], T["onesr"][0:1, 0:128],
                                        T["brow"][0:1, 512 * n:512 * (n + 1)],
                                        start=False, stop=True)
                    S.op("pe", bfn, reads=["onesr", "brow"], writes=[pid],
                         milestone=True, tag=f"w2b{t}_{n}")
                def afn(e, t=t, n=n):
                    xs = T["x"][:, t, 512 * n:512 * (n + 1)]
                    return e.tensor_tensor(xs, xs, ctx.psc[n][:], ALU.add)
                S.op("dve", afn, reads=[pid, ("x", t)], writes=[("x", t)],
                     tag=f"w2x{fg}_{t}_{n}")


def new_ctx(nc, stack):
    ctx = Ctx()
    ctx.nc = nc
    ctx.stack = stack
    ctx.tiles = {}
    ctx.dr = {}
    ctx.S = Sched(nc)
    return ctx


def din(ctx, name, shape, dtype):
    ctx.dr[name] = ctx.nc.dram_tensor(name, shape, dtype, kind="ExternalInput")


def dout(ctx, name, shape, dtype):
    ctx.dr[name] = ctx.nc.dram_tensor(name, shape, dtype, kind="ExternalOutput")


def dint(ctx, name, shape, dtype):
    ctx.dr[name] = ctx.nc.dram_tensor(name, shape, dtype, kind="Internal")


def view_flat(t, n0, n1):
    ap = t[:]
    nd = len(ap.shape)
    names = " ".join(f"d{i}" for i in range(nd - 1))
    ap = ap.rearrange(f"p {names} -> p ({names})")
    return ap[:, n0:n1]


def alloc_tiny(ctx):
    sb(ctx, "ident", [128, 128], BF16)
    sb(ctx, "iconst", [128, 4], I32)
    sb(ctx, "onesr", [1, 128], BF16)


def alloc_psum(ctx):
    nc = ctx.nc
    ctx.psc = [ctx.stack.enter_context(nc.psum_tensor(f"psc{i}", [128, 512], F32))
               for i in range(4)]
    ctx.ps2 = [ctx.stack.enter_context(
        nc.psum_tensor(f"ps2{i}", [128, 2, 512], F32)) for i in range(2)]
    ctx.ps_t = [ctx.ps2[i][:].bitcast(BF16).rearrange("p a b -> p (a b)")[:, 0:128]
                for i in range(2)]
    ctx.tp_rot = [0]


def alloc_quant(ctx, qs, tern_only=False):
    names = ("qy", "qg") if tern_only else ("qy", "qt2", "qg", "ql")
    for nm in names:
        sb(ctx, nm, [128, qs], F32)
    if not tern_only:
        sb(ctx, "qbm", [128, qs // 16], F32)
        sb(ctx, "qs", [128, qs // 16], F32)
        sb(ctx, "qsi", [128, qs // 16], F32)


def alloc_ln(ctx):
    sb(ctx, "lnst", [128, 8], F32)
    sb(ctx, "lneps", [128, 1], F32)
    sb(ctx, "lnsq", [128, D], F32)
    sb(ctx, "xn32", [128, D], F32)
    sb(ctx, "xnbf", [128, D], BF16)


def build_launch1a():
    nc = bass.Bass("TRN2", target_bir_lowering=False)
    stack = contextlib.ExitStack()
    ctx = new_ctx(nc, stack)
    din(ctx, "L0_ffn_w_gate", [DFF0, D], F32)
    din(ctx, "L0_ffn_w_up", [DFF0, D], F32)
    din(ctx, "L0_ffn_w_down", [D, DFF0], F32)
    din(ctx, "ident", [128, 128], BF16)
    dout(ctx, "wgqT", [D, DFF0], BF16)
    dout(ctx, "wuqT", [D, DFF0], BF16)
    dout(ctx, "wdqT", [DFF0, D], BF16)
    with stack:
        alloc_tiny(ctx)
        alloc_psum(ctx)
        alloc_quant(ctx, 1408)
        sb(ctx, "wqst", [128, 2, 1408], F32)
        sb(ctx, "wqbf", [128, 2, 1408], BF16)
        sb(ctx, "wqT", [128, 2, 11, 128], BF16)
        p_consts(ctx)
        p_prep_l0ffn_weights(ctx)
        ctx.S.barrier_step()
        ctx.S.emit(stack)
    return nc


def build_launch1b():
    nc = bass.Bass("TRN2", target_bir_lowering=False)
    stack = contextlib.ExitStack()
    ctx = new_ctx(nc, stack)
    din(ctx, "x0", [TOWN, D], F32)
    din(ctx, "L0_attn_w_qkv", [3 * D, D], F32)
    din(ctx, "wgqT", [D, DFF0], BF16)
    din(ctx, "wuqT", [D, DFF0], BF16)
    din(ctx, "wdqT", [DFF0, D], BF16)
    din(ctx, "ident", [128, 128], BF16)
    dout(ctx, "kout", [D, TOWN], BF16)
    dout(ctx, "vout", [TOWN, 2048], BF16)
    dout(ctx, "qout", [D, TOWN], BF16)
    dout(ctx, "xf", [TOWN, D], F32)
    ctx.kdst, ctx.vdst, ctx.qdst = "kout", "vout", "qout"
    with stack:
        alloc_tiny(ctx)
        alloc_psum(ctx)
        alloc_quant(ctx, 1024)
        alloc_ln(ctx)
        sb(ctx, "x", [128, NC16, D], F32)
        sb(ctx, "xn_fm", [128, 8, TOWN], BF16)
        sb(ctx, "xnq_fm", [128, 8, TOWN], BF16)
        sb(ctx, "xnqbf", [128, D], BF16)
        sb(ctx, "wstage", [128, 2, D], BF16)
        sb(ctx, "wT", [128, 2, 8, 128], BF16)
        sb(ctx, "wvT", [128, 8, D], BF16)
        sb(ctx, "kstg", [128, TOWN], BF16)
        sb(ctx, "vstg", [128, 2048], BF16)
        sb(ctx, "gs", [128, 512], F32)
        sb(ctx, "h32", [128, 512], F32)
        sb(ctx, "hq", [128, 512], BF16)
        sb(ctx, "hq_fm", [128, 4, 128], BF16)
        xfm = ctx.tiles["xn_fm"]
        ctx.tiles["wg"] = view_flat(xfm, 0, 8192).rearrange(
            "p (a k c) -> p a k c", a=2, k=8)
        ctx.tiles["wu"] = view_flat(xfm, 8192, 16384).rearrange(
            "p (a k c) -> p a k c", a=2, k=8)
        ctx.tiles["wdn"] = ctx.tiles["wvT"][:].rearrange(
            "p k n -> p (k n)").rearrange("p (a k c) -> p a k c", a=2, k=4)
        p_consts(ctx)
        p_load_x(ctx, "x0")
        p_ln(ctx, 0)
        p_l0_qkv(ctx)
        for k in range(8):
            ctx.S.alias(("wg", 0), [("xn_fm", k)])
            ctx.S.alias(("wg", 1), [("xn_fm", k)])
            ctx.S.alias(("wu", 0), [("xn_fm", k)])
            ctx.S.alias(("wu", 1), [("xn_fm", k)])
            ctx.S.alias(("wdn", 0), [("wvT", k)])
            ctx.S.alias(("wdn", 1), [("wvT", k)])
        p_l0_ffn(ctx)
        p_store_x(ctx, "xf")
        ctx.S.barrier_step()
        ctx.S.emit(stack)
    return nc


def build_attn_launch(layer):
    nc = bass.Bass("TRN2", target_bir_lowering=False)
    stack = contextlib.ExitStack()
    ctx = new_ctx(nc, stack)
    din(ctx, "xf", [TOWN, D], F32)
    din(ctx, "qin", [D, TOWN], BF16)
    din(ctx, "kfull", [4 * D, TOWN], BF16)
    din(ctx, "vfull", [8192, 2048], BF16)
    if layer == 0:
        din(ctx, "L0_attn_w_out", [D, D], F32)
    else:
        din(ctx, "L1_attn_Wo", [D, D], F16)
        din(ctx, "L1_attn_bo", [D], F16)
    din(ctx, "ident", [128, 128], BF16)
    din(ctx, "masks", [128, 2048], BF16)
    din(ctx, "sched", [1, SCHED_COLS], I32)
    din(ctx, "wtab", [128, 2 * NROUND], F32)
    dout(ctx, "xout", [TOWN, D], F32)
    with stack:
        alloc_tiny(ctx)
        alloc_psum(ctx)
        sb(ctx, "sched", [1, SCHED_COLS], I32)
        sb(ctx, "masks", [128, 4, 512], BF16)
        sb(ctx, "wtab", [128, 2 * NROUND], F32)
        sb(ctx, "ctxacc", [128, 2, 8, 1024], F32)
        sb(ctx, "sacc", [128, 16, 512], F32)
        sb(ctx, "kslab", [128, 2, 8, 1024], BF16)
        sb(ctx, "vslab", [128, 8, 2048], BF16)
        sb(ctx, "qsb", [128, 2, 8, 512], BF16)
        sb(ctx, "ptile", [128, 2, 1024], BF16)
        sb(ctx, "nstg2", [128, 512], F32)
        sb(ctx, "xc", [128, 2, D], F32)
        sb(ctx, "wstage", [128, 2, D], BF16)
        sb(ctx, "brow", [1, D], BF16)
        ctx.tiles["ctx_fm"] = ctx.tiles["kslab"][:].rearrange(
            "p a k c -> p (a k c)").rearrange("p (k t) -> p k t", k=8)
        ctx.tiles["woT"] = view_flat(ctx.tiles["vslab"], 0, 8192).rearrange(
            "p (k n) -> p k n", k=8)
        p_consts(ctx)
        ctx.ksrc, ctx.vsrc, ctx.qsrc = "kfull", "vfull", "qin"
        ctx.xsrc, ctx.xdst = "xf", "xout"
        ctx.x_stream = True
        ctx.sched_off = 0
        ctx.alias_ctx = True
        p_attention(ctx, layer)
        p_outproj(ctx, layer)
        ctx.S.barrier_step()
        ctx.S.emit(stack)
    return nc


def build_launch3():
    nc = bass.Bass("TRN2", target_bir_lowering=False)
    stack = contextlib.ExitStack()
    ctx = new_ctx(nc, stack)
    din(ctx, "xf", [TOWN, D], F32)
    din(ctx, "L1_attn_Wq", [D, D], F16)
    din(ctx, "L1_attn_bq", [D], F16)
    din(ctx, "L1_attn_Wk", [D, D], F16)
    din(ctx, "L1_attn_bk", [D], F16)
    din(ctx, "L1_attn_Wv", [D, D], F16)
    din(ctx, "L1_attn_bv", [D], F16)
    din(ctx, "L1_ffn_W1", [DFF1, D], F32)
    din(ctx, "b1sb_in", [128, 32], F32)
    din(ctx, "L1_ffn_W2", [D, DFF1], F32)
    din(ctx, "L1_ffn_b2", [D], F32)
    din(ctx, "ident", [128, 128], BF16)
    din(ctx, "ropec", [TOWN, 32], F32)
    din(ctx, "ropes", [TOWN, 32], F32)
    dout(ctx, "k1out", [D, TOWN], BF16)
    dout(ctx, "v1out", [TOWN, 2048], BF16)
    dout(ctx, "q1out", [D, TOWN], BF16)
    dout(ctx, "xf1", [TOWN, D], F32)
    dint(ctx, "W2qT", [DFF1, D], BF16)
    ctx.kdst, ctx.vdst, ctx.qdst = "k1out", "v1out", "q1out"
    with stack:
        alloc_tiny(ctx)
        alloc_psum(ctx)
        alloc_quant(ctx, 2048, tern_only=True)
        alloc_ln(ctx)
        sb(ctx, "x", [128, NC16, D], F32)
        sb(ctx, "xn_fm", [128, 8, TOWN], BF16)
        sb(ctx, "ropec", [128, NC16, 32], F32)
        sb(ctx, "ropes", [128, NC16, 32], F32)
        sb(ctx, "w1a", [128, 8, D], BF16)
        sb(ctx, "rp", [128, 512], BF16)
        sb(ctx, "ra", [128, 256], F32)
        sb(ctx, "rb", [128, 256], F32)
        sb(ctx, "qkstg", [128, 8, 128], BF16)
        sb(ctx, "vstg", [128, 2048], BF16)
        sb(ctx, "gacc", [128, 2], F32)
        sb(ctx, "gval", [1, 1], F32)
        sb(ctx, "gbf", [128, 1], BF16)
        sb(ctx, "ginv", [128, 2], F32)
        sb(ctx, "onesc", [128, 1], F32)
        sb(ctx, "onesrf", [1, 128], F32)
        sb(ctx, "b1sb", [128, 32], F32)
        sb(ctx, "hfm", [128, 4, TOWN], BF16)
        sb(ctx, "w2g", [128, 1, 4, D], BF16)
        sb(ctx, "wqst", [128, 2, 2048], F32)
        sb(ctx, "wqbf", [128, 1, 2048], BF16)
        sb(ctx, "wT", [128, 2, 8, 128], BF16)
        sb(ctx, "w2stg", [128, 2, 128], BF16)
        sb(ctx, "brow", [1, D], BF16)
        S = ctx.S
        p_consts(ctx)
        S.dma("sp", lambda e: e.dma_start(out=ctx.tiles["b1sb"][:],
                                          in_=ctx.dr["b1sb_in"].ap()),
              writes=["b1sb"], tag="b1l")
        S.op("dve", lambda e: e.memset(ctx.tiles["onesc"][:], 1.0),
             writes=["onesc"], tag="onesc")
        S.op("dve", lambda e: e.memset(ctx.tiles["onesrf"][:], 1.0),
             writes=["onesrf"], tag="onesrf")
        p_load_x(ctx, "xf")
        p_ln(ctx, 1)
        p_l1_qkv(ctx)
        p_l1_ffn(ctx)
        p_store_x(ctx, "xf1")
        ctx.S.barrier_step()
        ctx.S.emit(stack)
    return nc


def host_sched_table(j):
    rows = []

    def slab_offs(sp):
        rk = sp if sp < 4 else 7 - sp
        hf = 0 if sp < 4 else 1
        return rk * 1024 * 2048 + hf * 1024, (2048 * rk + 1024 * hf) * 2048

    def add(sp, stripe):
        ko, vo = slab_offs(sp)
        rows.append([ko, vo, stripe * 1024 + 0, stripe * 1024 + 512, stripe])

    add(j, 0)
    add(7 - j, 1)
    for sp in range(j):
        add(sp, 0)
    for sp in range(7 - j):
        add(sp, 1)
    assert len(rows) == NROUND
    tab = np.zeros((1, SCHED_COLS), dtype=np.int32)
    wtab = np.zeros((128, 2 * NROUND), dtype=np.float32)
    for r, row in enumerate(rows):
        wtab[:, 2 * r + row[4]] = 1.0
    for r, row in enumerate(rows):
        tab[0, 5 * r:5 * r + 4] = row[:4]
    return tab, wtab


def host_masks():
    m = np.zeros((128, 4, 512), dtype=np.float32)
    qi = np.arange(512)
    for dd in range(4):
        for p in range(128):
            m[p, dd, :] = (p <= qi - 128 * dd).astype(np.float32)
    return np.ascontiguousarray(m.reshape(128, 2048).astype(ml_dtypes.bfloat16))


def host_rope(positions):
    inv = np.exp(np.arange(0, HD, 2, dtype=np.float32) *
                 (-np.log(10000.0) / HD))
    ang = positions[:, None].astype(np.float32) * inv[None, :]
    sin = np.sin(ang).astype(np.float16).astype(np.float32)
    cos = np.cos(ang).astype(np.float16).astype(np.float32)
    return np.ascontiguousarray(cos), np.ascontiguousarray(sin)


def own_rows(j):
    a = np.arange(1024 * j, 1024 * (j + 1))
    b = np.arange(1024 * (7 - j), 1024 * (8 - j))
    return np.concatenate([a, b])


def kernel(**inputs):
    from concourse.bass_utils import run_bass_kernel_spmd
    ident = np.ascontiguousarray(np.eye(128, dtype=np.float32)
                                 .astype(ml_dtypes.bfloat16))
    masks = host_masks()
    x0 = np.asarray(inputs["L0_attn_x"])
    B, T, _ = x0.shape
    cores = list(range(8))
    rows = [own_rows(c % 4) for c in cores]
    x0s = [np.ascontiguousarray(x0[c // 4][rows[c]]) for c in cores]
    sw = [host_sched_table(c % 4) for c in cores]
    scheds = [x[0] for x in sw]
    wtabs = [np.ascontiguousarray(x[1]) for x in sw]
    ropecs, ropess = [], []
    for c in cores:
        cos, sin = host_rope(rows[c])
        ropecs.append(cos)
        ropess.append(sin)
    b1sb = np.ascontiguousarray(
        np.asarray(inputs["L1_ffn_b1"], np.float32).reshape(32, 128).T)
    f32 = np.float32

    nc1a = build_launch1a()
    in1a = [dict(
        L0_ffn_w_gate=np.asarray(inputs["L0_ffn_w_gate"], f32),
        L0_ffn_w_up=np.asarray(inputs["L0_ffn_w_up"], f32),
        L0_ffn_w_down=np.asarray(inputs["L0_ffn_w_down"], f32),
        ident=ident) for c in cores]
    r1a = run_bass_kernel_spmd(nc1a, in1a, core_ids=cores).results
    print("L1a done", flush=True)

    nc1b = build_launch1b()
    in1b = [dict(
        x0=x0s[c],
        L0_attn_w_qkv=np.asarray(inputs["L0_attn_w_qkv"], f32),
        wgqT=r1a[c]["wgqT"], wuqT=r1a[c]["wuqT"], wdqT=r1a[c]["wdqT"],
        ident=ident) for c in cores]
    r1 = run_bass_kernel_spmd(nc1b, in1b, core_ids=cores).results
    print("L1b done", flush=True)

    kf = [np.concatenate([r1[4 * g + r]["kout"] for r in range(4)], axis=0)
          for g in range(2)]
    vf = [np.concatenate([r1[4 * g + r]["vout"] for r in range(4)], axis=0)
          for g in range(2)]

    nc2 = build_attn_launch(0)
    in2 = [dict(
        xf=r1[c]["xf"], qin=r1[c]["qout"], kfull=kf[c // 4], vfull=vf[c // 4],
        L0_attn_w_out=np.asarray(inputs["L0_attn_w_out"], f32),
        ident=ident, masks=masks, sched=scheds[c], wtab=wtabs[c]) for c in cores]
    r2 = run_bass_kernel_spmd(nc2, in2, core_ids=cores).results
    print("L2 done", flush=True)

    nc3 = build_launch3()
    in3 = [dict(
        xf=r2[c]["xout"],
        L1_attn_Wq=np.asarray(inputs["L1_attn_Wq"]),
        L1_attn_bq=np.asarray(inputs["L1_attn_bq"]),
        L1_attn_Wk=np.asarray(inputs["L1_attn_Wk"]),
        L1_attn_bk=np.asarray(inputs["L1_attn_bk"]),
        L1_attn_Wv=np.asarray(inputs["L1_attn_Wv"]),
        L1_attn_bv=np.asarray(inputs["L1_attn_bv"]),
        L1_ffn_W1=np.asarray(inputs["L1_ffn_W1"], f32),
        b1sb_in=b1sb,
        L1_ffn_W2=np.asarray(inputs["L1_ffn_W2"], f32),
        L1_ffn_b2=np.asarray(inputs["L1_ffn_b2"], f32),
        ident=ident, ropec=ropecs[c], ropes=ropess[c]) for c in cores]
    r3 = run_bass_kernel_spmd(nc3, in3, core_ids=cores).results
    print("L3 done", flush=True)

    kf1 = [np.concatenate([r3[4 * g + r]["k1out"] for r in range(4)], axis=0)
           for g in range(2)]
    vf1 = [np.concatenate([r3[4 * g + r]["v1out"] for r in range(4)], axis=0)
           for g in range(2)]

    nc4 = build_attn_launch(1)
    in4 = [dict(
        xf=r3[c]["xf1"], qin=r3[c]["q1out"], kfull=kf1[c // 4],
        vfull=vf1[c // 4],
        L1_attn_Wo=np.asarray(inputs["L1_attn_Wo"]),
        L1_attn_bo=np.asarray(inputs["L1_attn_bo"]),
        ident=ident, masks=masks, sched=scheds[c], wtab=wtabs[c]) for c in cores]
    r4 = run_bass_kernel_spmd(nc4, in4, core_ids=cores).results
    print("L4 done", flush=True)

    out = np.zeros((B, T, D), dtype=np.float32)
    for c in cores:
        out[c // 4][rows[c]] = r4[c]["xout"]
    return out


def vw(t, n0_bytes, shape, dtype):
    ap = t[:]
    nd = len(ap.shape)
    names = " ".join(f"d{i}" for i in range(nd - 1))
    flat = ap.rearrange(f"p {names} -> p ({names})")
    esz = {F32: 4, BF16: 2, I32: 4}[dtype]
    if ap.dtype != dtype:
        flat = flat.bitcast(dtype)
    n0 = n0_bytes // esz
    n1 = n0 + int(np.prod(shape))
    v = flat[:, n0:n1]
    if len(shape) > 1:
        dims = " ".join(f"e{i}" for i in range(len(shape)))
        kw = {f"e{i}": shape[i] for i in range(len(shape))}
        v = v.rearrange(f"p ({dims}) -> p {dims}", **kw)
    return v


def build_fused():
    nc = bass.Bass("TRN2", target_bir_lowering=False)
    stack = contextlib.ExitStack()
    ctx = new_ctx(nc, stack)
    din(ctx, "x0", [TOWN, D], F32)
    din(ctx, "L0_attn_w_qkv", [3 * D, D], F32)
    din(ctx, "L0_attn_w_out", [D, D], F32)
    din(ctx, "L0_ffn_w_gate", [DFF0, D], F32)
    din(ctx, "L0_ffn_w_up", [DFF0, D], F32)
    din(ctx, "L0_ffn_w_down", [D, DFF0], F32)
    din(ctx, "L1_attn_Wq", [D, D], F16)
    din(ctx, "L1_attn_bq", [D], F16)
    din(ctx, "L1_attn_Wk", [D, D], F16)
    din(ctx, "L1_attn_bk", [D], F16)
    din(ctx, "L1_attn_Wv", [D, D], F16)
    din(ctx, "L1_attn_bv", [D], F16)
    din(ctx, "L1_attn_Wo", [D, D], F16)
    din(ctx, "L1_attn_bo", [D], F16)
    din(ctx, "L1_ffn_W1", [DFF1, D], F32)
    din(ctx, "b1sb_in", [128, 32], F32)
    din(ctx, "L1_ffn_W2", [D, DFF1], F32)
    din(ctx, "L1_ffn_b2", [D], F32)
    din(ctx, "ident", [128, 128], BF16)
    din(ctx, "masks", [128, 2048], BF16)
    din(ctx, "sched", [1, SCHED_COLS], I32)
    din(ctx, "wtab", [128, 2 * NROUND], F32)
    din(ctx, "ropec", [TOWN, 32], F32)
    din(ctx, "ropes", [TOWN, 32], F32)
    dout(ctx, "xout", [TOWN, D], F32)
    for nm, shp in (("wgqT", [D, DFF0]), ("wuqT", [D, DFF0]),
                    ("wdqT", [DFF0, D]), ("W2qT", [DFF1, D]),
                    ("woTd", [D, D]),
                    ("qpark0", [D, TOWN]), ("qpark1", [D, TOWN]),
                    ("kag_in0", [D, TOWN]), ("kag_in1", [D, TOWN]),
                    ("vag_in0", [TOWN, 2048]), ("vag_in1", [TOWN, 2048])):
        dint(ctx, nm, shp, BF16)
    for nm in ("kag_out0", "kag_out1"):
        dint(ctx, nm, [4 * D, TOWN], BF16)
    for nm in ("vag_out0", "vag_out1"):
        dint(ctx, nm, [8192, 2048], BF16)
    dbg = getattr(build_fused, "_dbg", False)
    for nm in ("xfd", "x1d", "xf1d"):
        (dout if dbg else dint)(ctx, nm, [TOWN, D], F32)

    ctx.defer_rope = True
    with stack:
        alloc_tiny(ctx)
        alloc_psum(ctx)
        alloc_ln(ctx)
        T = ctx.tiles
        sb(ctx, "x", [128, NC16, D], F32)
        sb(ctx, "xn_fm", [128, 8, TOWN], BF16)
        sb(ctx, "kslab", [128, 2, 8, 1024], BF16)
        sb(ctx, "vslab", [128, 8, 2048], BF16)
        sb(ctx, "qsb", [128, 2, 8, 512], BF16)
        sb(ctx, "ptile", [128, 2, 1024], BF16)
        sb(ctx, "nstg2", [128, 512], F32)
        sb(ctx, "xc", [128, 2, D], F32)
        sb(ctx, "masks", [128, 4, 512], BF16)
        sb(ctx, "sched", [1, SCHED_COLS], I32)
        sb(ctx, "wtab", [128, 2 * NROUND], F32)
        sb(ctx, "brow", [1, D], BF16)
        sb(ctx, "gacc", [128, 2], F32)
        sb(ctx, "gval", [1, 1], F32)
        sb(ctx, "gbf", [128, 1], BF16)
        sb(ctx, "ginv", [128, 2], F32)
        sb(ctx, "onesc", [128, 1], F32)
        sb(ctx, "onesrf", [1, 128], F32)
        sb(ctx, "b1sb", [128, 32], F32)
        vs = T["vslab"]
        T["wqst"] = vw(vs, 0, [2, 1024], F32)
        T["wqbf"] = vw(vs, 8192, [2, 1024], BF16)
        T["wqT"] = vw(vs, 12288, [2, 11, 128], BF16)
        T["wstage"] = vw(vs, 0, [2, 1024], BF16)
        T["wT"] = vw(vs, 4096, [2, 8, 128], BF16)
        T["wvT"] = vw(vs, 8192, [8, 1024], BF16)
        T["wdn"] = vw(vs, 8192, [2, 4, 1024], BF16)
        T["kstg"] = vw(vs, 24576, [TOWN], BF16)
        T["vstg"] = vw(vs, 28672, [2048], BF16)
        T["gs"] = vw(vs, 0, [512], F32)
        T["h32"] = vw(vs, 2048, [512], F32)
        T["hq"] = vw(vs, 4096, [512], BF16)
        T["hq_fm"] = vw(vs, 5120, [4, 128], BF16)
        T["woT"] = vw(vs, 0, [8, 1024], BF16)
        T["w1a"] = vw(T["qsb"], 0, [8, 1024], BF16)
        T["rp"] = vw(T["ptile"], 0, [512], BF16)
        T["ra"] = vw(T["ptile"], 1024, [256], F32)
        T["rb"] = vw(T["ptile"], 2048, [256], F32)
        T["qkstg"] = vw(T["nstg2"], 0, [8, 128], BF16)
        T["wqstL1"] = vw(vs, 0, [2, 1024], F32)
        T["wqbfL1"] = vw(vs, 8192, [1, 1024], BF16)
        T["wTl1"] = vw(vs, 10240, [2, 8, 128], BF16)
        T["vstgL1"] = vw(vs, 16384, [2048], BF16)
        T["w2stg"] = vw(vs, 14336, [2, 128], BF16)
        T["hfm"] = vw(T["kslab"], 0, [4, TOWN], BF16)
        T["ropec"] = vw(T["kslab"], 24576, [NC16, 32], F32)
        T["ropes"] = vw(T["kslab"], 26624, [NC16, 32], F32)
        T["w2g"] = vw(T["kslab"], 16384, [1, 4, D], BF16)
        T["qy"] = vw(T["qsb"], 0, [1024], F32)
        T["qt2"] = vw(T["qsb"], 4096, [1024], F32)
        T["qg"] = vw(T["qsb"], 8192, [1024], F32)
        T["qbm"] = vw(T["qsb"], 12288, [64], F32)
        T["qs"] = vw(T["qsb"], 12544, [64], F32)
        T["qsi"] = vw(T["qsb"], 12800, [64], F32)
        T["ql"] = vw(T["ptile"], 0, [1024], F32)
        T["wg"] = vw(T["xn_fm"], 0, [2, 8, 512], BF16)
        T["wu"] = vw(T["xn_fm"], 16384, [2, 8, 512], BF16)
        T["ctxacc"] = vw(T["x"], 0, [2, 8, 1024], F32)
        T["xnqbf"] = vw(T["xc"], 4096, [1024], BF16)
        T["sacc"] = vw(T["xn_fm"], 0, [16, 512], F32)
        T["xnq_fm"] = vw(T["kslab"], 0, [8, TOWN], BF16)
        T["ctx_fm"] = vw(T["kslab"], 0, [8, TOWN], BF16)

        S = ctx.S

        def phase_alias(new_ids, old_ids):
            for nid in new_ids:
                S.alias(nid, old_ids)

        p_consts(ctx)
        S.dma("sp", lambda e: e.dma_start(out=T["b1sb"][:],
                                          in_=ctx.dr["b1sb_in"].ap()),
              writes=["b1sb"], tag="b1l")
        S.op("dve", lambda e: e.memset(T["onesc"][:], 1.0), writes=["onesc"])
        S.op("dve", lambda e: e.memset(T["onesrf"][:], 1.0), writes=["onesrf"])
        p_load_x(ctx, "x0")
        p_prep_l0ffn_weights(ctx)
        ctx.woT_dram = "woTd"
        ctx.woT_reads = [("wouT", r) for r in range(8)]
        p_prep_wout(ctx)
        phase_alias([("wst", 0), ("wst", 1)] + [("wT", p, k) for p in range(2)
                                                for k in range(8)]
                    + [("wvT", k) for k in range(8)] + ["kstg", "vstg"],
                    [("wqst", 0), ("wqst", 1), ("wqbf", 0), ("wqbf", 1)]
                    + [("wqT", p, c) for p in range(2) for c in range(11)])
        p_ln(ctx, 0)
        ctx.kdst, ctx.vdst, ctx.qdst = "kag_in0", "vag_in0", "qpark0"
        p_l0_qkv(ctx)
        rg = [[0, 1, 2, 3], [4, 5, 6, 7]]
        S.collective(lambda e: e.collective_compute(
            "AllGather", ALU.bypass, replica_groups=rg,
            ins=[ctx.dr["kag_in0"].ap().opt()],
            outs=[ctx.dr["kag_out0"].ap().opt()]),
            reads=[("kout", "kag_in0", mk_) for mk_ in range(8)],
            writes=["agk0"], tag="agk0")
        S.collective(lambda e: e.collective_compute(
            "AllGather", ALU.bypass, replica_groups=rg,
            ins=[ctx.dr["vag_in0"].ap().opt()],
            outs=[ctx.dr["vag_out0"].ap().opt()]),
            reads=[("vout", t) for t in range(NC16)],
            writes=["agv0"], tag="agv0")
        phase_alias([("wdn", 0), ("wdn", 1), "gs", "h32", "hq"]
                    + [("hq_fm", k) for k in range(4)],
                    [("wst", 0), ("wst", 1), "kstg", "vstg"]
                    + [("wT", p, k) for p in range(2) for k in range(8)]
                    + [("wvT", k) for k in range(8)])
        phase_alias([("wg", 0), ("wg", 1), ("wu", 0), ("wu", 1)],
                    [("xn_fm", m) for m in range(8)])
        p_l0_ffn(ctx)
        S.dma("act", lambda e: e.dma_start(
            out=rearr_tok(ctx.dr["xfd"].ap()), in_=T["x"][:]),
            reads=[("x", c) for c in range(NC16)], writes=["xfd"], tag="xspill0")
        phase_alias([("acc", hp, qt, st) for hp in range(8) for qt in range(2)
                     for st in range(2)],
                    [("x", c) for c in range(NC16)] + ["xfd"])
        phase_alias([("sac", i, hh) for i in range(16) for hh in range(2)],
                    [("xn_fm", m) for m in range(8)]
                    + [("wg", 0), ("wg", 1), ("wu", 0), ("wu", 1)])
        phase_alias([("kslab", 0), ("kslab", 1)],
                    [("xnq_fm", k) for k in range(8)])
        phase_alias(["vslab"],
                    [("wdn", 0), ("wdn", 1), "gs", "h32", "hq"]
                    + [("hq_fm", k) for k in range(4)])
        phase_alias([("qsb", 0), ("qsb", 1)],
                    ["qy", "qhi", "qg3", "qbm", "qsc", "qsi"])
        phase_alias([("pt", 0), ("pt", 1)], ["qlo"])
        phase_alias([("xc", 0), ("xc", 1)], ["xnqbf"])
        ctx.ksrc, ctx.vsrc, ctx.qsrc = "kag_out0", "vag_out0", "qpark0"
        ctx.k_reads = ["agk0"]
        ctx.v_reads = ["agv0"]
        ctx.q_reads = [("kout", "qpark0", mk_) for mk_ in range(8)]
        ctx.xsrc_reads = ["xfd"]
        ctx.xsrc, ctx.xdst = "xfd", "x1d"
        ctx.x_stream = True
        ctx.sched_off = 0
        ctx.alias_ctx = True
        p_attention(ctx, 0)
        p_outproj(ctx, 0)
        phase_alias([("x", c) for c in range(NC16)],
                    [("acc", hp, qt, st) for hp in range(8) for qt in range(2)
                     for st in range(2)])
        ctx.x_stream = False
        p_load_x(ctx, "x1d", extra_reads=[("xco", t) for t in range(NC16)])
        phase_alias([("xn_fm", m) for m in range(8)],
                    [("sac", i, hh) for i in range(16) for hh in range(2)])
        p_ln(ctx, 1)
        phase_alias([("w1a", k) for k in range(8)],
                    [("qsb", 0), ("qsb", 1)])
        phase_alias(["rp", "ra", "rb"], [("pt", 0), ("pt", 1)])
        phase_alias([("qkstg", m) for m in range(8)], ["nstg2"])
        phase_alias(["vstg"], ["vslab"])
        phase_alias(["ropec", "ropes"], [("ctx_fm", k) for k in range(8)])
        S.dma("sp", lambda e: e.dma_start(out=T["ropec"],
                                          in_=rearr_tok(ctx.dr["ropec"].ap())),
              writes=["ropec"], tag="ropecL")
        S.dma("sp", lambda e: e.dma_start(out=T["ropes"],
                                          in_=rearr_tok(ctx.dr["ropes"].ap())),
              writes=["ropes"], tag="ropesL")
        ctx.kdst, ctx.vdst, ctx.qdst = "kag_in1", "vag_in1", "qpark1"
        p_l1_qkv(ctx)
        S.collective(lambda e: e.collective_compute(
            "AllGather", ALU.bypass, replica_groups=rg,
            ins=[ctx.dr["kag_in1"].ap().opt()],
            outs=[ctx.dr["kag_out1"].ap().opt()]),
            reads=[("qkout", "kag_in1", t) for t in range(NC16)],
            writes=["agk1"], tag="agk1")
        S.collective(lambda e: e.collective_compute(
            "AllGather", ALU.bypass, replica_groups=rg,
            ins=[ctx.dr["vag_in1"].ap().opt()],
            outs=[ctx.dr["vag_out1"].ap().opt()]),
            reads=[("vout", t) for t in range(NC16)],
            writes=["agv1"], tag="agv1")
        phase_alias([("wqst", 0), ("wqst", 1), ("wqbf", 0), "vstgL1"]
                    + [("wT", p, k) for p in range(2) for k in range(8)],
                    ["vstg"] + [("w1a", k) for k in range(8)]
                    + [("woT", k) for k in range(8)])
        phase_alias([("hfm", k) for k in range(4)] + [("w2g", 0)],
                    [("xnq_fm", k) for k in range(8)]
                    + [("ctx_fm", k) for k in range(8)])
        T["wqst"] = T["wqstL1"]
        T["wqbf"] = T["wqbfL1"]
        T["wT"] = T["wTl1"]
        T["vstg"] = T["vstgL1"]
        p_l1_ffn(ctx)
        S.dma("act", lambda e: e.dma_start(
            out=rearr_tok(ctx.dr["xf1d"].ap()), in_=T["x"][:]),
            reads=[("x", c) for c in range(NC16)], writes=["xf1d"], tag="xspill1")
        phase_alias([("acc", hp, qt, st) for hp in range(8) for qt in range(2)
                     for st in range(2)],
                    [("x", c) for c in range(NC16)] + ["xf1d"])
        phase_alias([("sac", i, hh) for i in range(16) for hh in range(2)],
                    [("xn_fm", m) for m in range(8)])
        phase_alias([("kslab", 0), ("kslab", 1)],
                    [("hfm", k) for k in range(4)] + [("w2g", 0), ("w2g", 1),
                                                      "ropec", "ropes"])
        phase_alias(["vslab"],
                    [("wqst", 0), ("wqst", 1), ("wqbf", 0), "vstgL1",
                     ("w2stg", 0), ("w2stg", 1)]
                    + [("wT", p, k) for p in range(2) for k in range(8)])
        phase_alias([("qsb", 0), ("qsb", 1)],
                    [("w1a", k) for k in range(8)] + ["qy", "qg3"])
        phase_alias([("pt", 0), ("pt", 1)], ["rp", "ra", "rb", "qlo"])
        phase_alias(["nstg2"], [("qkstg", m) for m in range(8)])
        ctx.ksrc, ctx.vsrc, ctx.qsrc = "kag_out1", "vag_out1", "qpark1"
        ctx.k_reads = ["agk1"]
        ctx.v_reads = ["agv1"]
        ctx.q_reads = [("qkout", "qpark1", t) for t in range(NC16)]
        ctx.xsrc_reads = ["xf1d"]
        ctx.xsrc, ctx.xdst = "xf1d", "xout"
        ctx.x_stream = True
        p_attention(ctx, 1)
        p_outproj(ctx, 1)
        S.barrier_step()
        S.emit(stack)
    return nc


def kernel_fused(**inputs):
    from concourse.bass_utils import run_bass_kernel_spmd
    ident = np.ascontiguousarray(np.eye(128, dtype=np.float32)
                                 .astype(ml_dtypes.bfloat16))
    masks = host_masks()
    x0 = np.asarray(inputs["L0_attn_x"])
    B, Tt, _ = x0.shape
    cores = list(range(8))
    rows = [own_rows(c % 4) for c in cores]
    sw = [host_sched_table(c % 4) for c in cores]
    scheds = [x[0] for x in sw]
    wtabs = [np.ascontiguousarray(x[1]) for x in sw]
    b1sb = np.ascontiguousarray(
        np.asarray(inputs["L1_ffn_b1"], np.float32).reshape(32, 128).T)
    f32 = np.float32
    nc = build_fused()
    in_maps = []
    for c in cores:
        cos, sin = host_rope(rows[c])
        in_maps.append(dict(
            x0=np.ascontiguousarray(x0[c // 4][rows[c]]),
            L0_attn_w_qkv=np.asarray(inputs["L0_attn_w_qkv"], f32),
            L0_attn_w_out=np.asarray(inputs["L0_attn_w_out"], f32),
            L0_ffn_w_gate=np.asarray(inputs["L0_ffn_w_gate"], f32),
            L0_ffn_w_up=np.asarray(inputs["L0_ffn_w_up"], f32),
            L0_ffn_w_down=np.asarray(inputs["L0_ffn_w_down"], f32),
            L1_attn_Wq=np.asarray(inputs["L1_attn_Wq"]),
            L1_attn_bq=np.asarray(inputs["L1_attn_bq"]),
            L1_attn_Wk=np.asarray(inputs["L1_attn_Wk"]),
            L1_attn_bk=np.asarray(inputs["L1_attn_bk"]),
            L1_attn_Wv=np.asarray(inputs["L1_attn_Wv"]),
            L1_attn_bv=np.asarray(inputs["L1_attn_bv"]),
            L1_attn_Wo=np.asarray(inputs["L1_attn_Wo"]),
            L1_attn_bo=np.asarray(inputs["L1_attn_bo"]),
            L1_ffn_W1=np.asarray(inputs["L1_ffn_W1"], f32),
            b1sb_in=b1sb,
            L1_ffn_W2=np.asarray(inputs["L1_ffn_W2"], f32),
            L1_ffn_b2=np.asarray(inputs["L1_ffn_b2"], f32),
            ident=ident, masks=masks, sched=scheds[c], wtab=wtabs[c],
            ropec=cos, ropes=sin))
    res = run_bass_kernel_spmd(nc, in_maps, core_ids=cores)
    out = np.zeros((B, Tt, D), dtype=np.float32)
    for c in cores:
        out[c // 4][rows[c]] = res.results[c]["xout"]
    return out


kernel = kernel_fused
